# version 05
# baseline (speedup 1.0000x reference)
import numpy as np
import ml_dtypes
from contextlib import ExitStack

import concourse.bass as bass
import concourse.mybir as mybir
from concourse.bass_utils import run_bass_kernel_spmd

F32 = mybir.dt.float32
BF16 = mybir.dt.bfloat16
AF = mybir.ActivationFunctionType
ALU = mybir.AluOpType
BF = ml_dtypes.bfloat16

B, L, Dm = 2, 4096, 2048
NH, DH = 16, 128
HID = 8192
C = 128
NCH = L // C
NP = 4
KD = Dm // 128
KH = HID // 128
NLEV = 7
EPS = 1e-5
RSH = float(1.0 / np.sqrt(HID))
DHS = float(DH ** -0.5)
TAPS = False


class Ctr:
    __slots__ = ("sem", "name", "n")

    def __init__(self, sem, name):
        self.sem, self.name, self.n = sem, name, 0


class KB:
    def __init__(self):
        self.wcache = {x: {} for x in
                       ("sync", "vector", "scalar", "tensor", "gpsimd")}
        self.rel = {}

    def wait(self, eng, engname, ctr, val):
        if val <= 0:
            return
        cache = self.wcache[engname]
        if cache.get(ctr.name, 0) >= val:
            return
        eng.wait_ge(ctr.sem, val)
        cache[ctr.name] = val

    def waitrel(self, eng, engname, key):
        r = self.rel.get(key)
        if r:
            self.wait(eng, engname, r[0], r[1])

    @staticmethod
    def inc(instr, ctr, kk=1):
        instr.then_inc(ctr.sem, kk)
        ctr.n += kk
        return ctr.n


def host_consts():
    i = np.arange(128)
    h128 = np.where((np.bitwise_count(i[:, None] & i[None, :]) % 2) == 0,
                    1.0, -1.0)
    g = np.arange(64)
    h64 = np.where((np.bitwise_count(g[:, None] & g[None, :]) % 2) == 0,
                   1.0, -1.0)
    i2h64 = np.kron(np.eye(2), h64)
    mask_s = -np.triu(np.ones((128, 128), np.float32), 1)
    mask_i = np.triu(np.ones((128, 128), np.float32), 0)
    return {
        "identb": np.eye(128).astype(BF),
        "h128b": h128.astype(BF),
        "i2h64b": i2h64.astype(BF),
        "masku_s4": np.tile(mask_s, (1, 4)).astype(np.float32),
        "masku_i4": np.tile(mask_i, (1, 4)).astype(np.float32),
        "onescol_b": np.ones((128, 1), BF),
        "onesrow_f": np.ones((1, 128), np.float32),
    }


def build_nc():
    nc = bass.Bass("TRN2", target_bir_lowering=False)
    k = KB()
    k.nc = nc

    di = lambda n, s, dt: nc.dram_tensor(n, s, dt, kind="ExternalInput").ap()
    do = lambda n, s, dt: nc.dram_tensor(n, s, dt, kind="ExternalOutput").ap()
    dd = lambda n, s, dt: nc.dram_tensor(n, s, dt).ap()

    k.xt = di("xt", [Dm, L], F32)
    k.xft = di("xft", [Dm, 1024], F32)
    k.wq = di("wq", [Dm, 512], F32)
    k.wk = di("wk", [Dm, 512], F32)
    k.wv = di("wv", [Dm, 512], F32)
    k.wb4 = di("wb4", [Dm, 4], F32)
    k.gam = di("gam", [128, 1], F32)
    k.woP = di("woP", [512, Dm], F32)
    k.w1S = di("w1S", [Dm, 1024], F32)
    k.w3S = di("w3S", [Dm, 1024], F32)
    k.w2S = di("w2S", [HID, 256], F32)
    k.identb = di("identb", [128, 128], BF16)
    k.h128b = di("h128b", [128, 128], BF16)
    k.i2h64b = di("i2h64b", [128, 128], BF16)
    k.masku_s4 = di("masku_s4", [128, 512], F32)
    k.masku_i4 = di("masku_i4", [128, 512], F32)
    k.onescol_b = di("onescol_b", [128, 1], BF16)
    k.onesrow_f = di("onesrow_f", [1, 128], F32)

    k.out = do("out", [Dm, 1024], F32)
    if TAPS:
        k.tap_o = do("tap_o", [512, L], BF16)

    k.w1b_in = dd("w1b_in", [Dm, 1024], BF16)
    k.w1b_out = dd("w1b_out", [8 * Dm, 1024], BF16)
    k.w3b_in = dd("w3b_in", [Dm, 1024], BF16)
    k.w3b_out = dd("w3b_out", [8 * Dm, 1024], BF16)
    k.w2bA = dd("w2bA", [HID, 256], BF16)
    k.w2p_in = dd("w2p_in", [HID, 256], BF16)
    k.w2p_out = dd("w2p_out", [8 * HID, 256], BF16)
    k.rs_in = [dd(f"rs_in{h}", [4 * Dm, 512], F32) for h in range(2)]
    k.rs_out = [dd(f"rs_out{h}", [Dm, 512], F32) for h in range(2)]
    k.dacc = dd("dacc", [Dm, 1024], F32)

    with ExitStack() as ctx:
        E = ctx.enter_context
        sb = lambda n, s, dt: E(nc.sbuf_tensor(n, s, dt))

        k.c_identb = sb("c_identb", [128, 128], BF16)
        k.c_h128b = sb("c_h128b", [128, 128], BF16)
        k.c_i2h64b = sb("c_i2h64b", [128, 128], BF16)
        k.c_masku_s4 = sb("c_masku_s4", [128, 512], F32)
        k.c_masku_i4 = sb("c_masku_i4", [128, 512], F32)
        k.c_onescol_b = sb("c_onescol_b", [128, 1], BF16)
        k.c_onesrow_f = sb("c_onesrow_f", [1, 128], F32)
        k.c_gam = sb("c_gam", [128, 1], F32)
        k.c_eps = sb("c_eps", [128, 1], F32)
        k.c_nhalf = sb("c_nhalf", [128, 1], F32)
        k.c_i128 = sb("c_i128", [128, 1], F32)
        k.c_i2048 = sb("c_i2048", [128, 1], F32)

        k.pquad = E(nc.psum_tensor("pquad", [128, 2048], F32))
        k.pa = E(nc.psum_tensor("pa", [128, 512], F32))
        k.pb = E(nc.psum_tensor("pb", [128, 512], F32))
        k.ptr = E(nc.psum_tensor("ptr", [128, 512], BF16))
        k.pst = E(nc.psum_tensor("pst", [128, 512], F32))

        names = ["dC", "dOut", "daR", "vC", "aC", "pC",
                 "ccW1", "ccW3", "ccW2", "cc0", "cc1"]
        for base in ("sq", "sx", "bg", "wc", "we", "rs", "ws", "w2", "xr"):
            names += [base + "0", base + "1"]
        names += ["dw0", "dw1", "dw2", "dw3", "ws2", "ws3", "w22", "w23"]
        for nm in names:
            setattr(k, nm, Ctr(E(nc.semaphore(nm)), nm))
        k.sq = [k.sq0, k.sq1]
        k.sx = [k.sx0, k.sx1]
        k.bgp = [k.bg0, k.bg1]
        k.wcp = [k.wc0, k.wc1]
        k.wep = [k.we0, k.we1]
        k.rsp = [k.rs0, k.rs1]
        k.wsp = [k.ws0, k.ws1, k.ws2, k.ws3]
        k.w2p_ = [k.w20, k.w21, k.w22, k.w23]
        k.xrp = [k.xr0, k.xr1]
        k.dwp = [k.dw0, k.dw1, k.dw2, k.dw3]

        k.block = E(nc.Block())

        with ExitStack() as bc_ctx:
            Eb = bc_ctx.enter_context
            sbb = lambda n, s, dt: Eb(nc.sbuf_tensor(n, s, dt))
            k.stgw = [sbb(f"stgw{i}", [128, 1024], F32) for i in range(2)]
            k.stgx = [sbb(f"stgx{i}", [128, 1024], F32) for i in range(2)]
            k.w2cs = [sbb(f"w2cs{i}", [128, 256], BF16) for i in range(2)]
            k.w2ev = [sbb(f"w2ev{i}", [128, 256], BF16) for i in range(2)]
            k.bgev = [sbb(f"bgev{i}", [128, 1024], BF16) for i in range(2)]
            k.wqb = sbb("wqb", [128, KD * 512], BF16)
            k.wkb = sbb("wkb", [128, KD * 512], BF16)
            k.wvb = sbb("wvb", [128, KD * 512], BF16)
            k.wbb = sbb("wbb", [128, KD * 4], BF16)
            k.wopb = sbb("wopb", [128, NP * Dm], BF16)
            k.xb = [sbb(f"xb{i}", [128, KD * 512], BF16) for i in range(2)]
            k.ktm = [sbb(f"ktm{i}", [128, 4 * 512], BF16) for i in range(2)]
            k.qtm = [sbb(f"qtm{i}", [128, 4 * 512], BF16) for i in range(2)]
            k.vtm = [sbb(f"vtm{i}", [128, 4 * 512], BF16) for i in range(2)]
            k.btf = [sbb(f"btf{i}", [128, 4 * 4], F32) for i in range(2)]
            k.qf = sbb("qf", [128, 512], F32)
            k.kf = sbb("kf", [128, 512], F32)
            k.qsq = sbb("qsq", [128, 512], F32)
            k.qss = sbb("qss", [128, 8], F32)
            k.s32 = sbb("s32", [128, NP * 128], F32)
            k.rksq = sbb("rksq", [128, NP * 384], BF16)
            k.u4 = sbb("u4", [128, NP * 128], BF16)
            k.gm4 = sbb("gm4", [128, NP * 128], BF16)
            k.dgn4 = sbb("dgn4", [128, NP * 128], BF16)
            k.xp = [sbb(f"xp{j}", [128, NP * 384], BF16) for j in range(2)]
            k.tvring = [sbb(f"tvr{i}", [128, NP * 128], BF16) for i in range(4)]
            k.tkring = [sbb(f"tkr{i}", [128, NP * 128], BF16) for i in range(4)]
            k.gmring = [sbb(f"gmr{i}", [128, NP * 128], BF16) for i in range(4)]
            k.qring = [sbb(f"qr{i}", [128, NP * 128], BF16) for i in range(4)]
            k.pt = [sbb(f"pt{j}", [128, NP * 128], BF16) for j in range(2)]
            k.dl4 = sbb("dl4", [128, NP * 128], BF16)
            k.osq = sbb("osq", [128, NP * 128], F32)
            k.oss = sbb("oss", [128, NP], F32)
            k.orstd = sbb("orstd", [128, NP], F32)
            k.onb = sbb("onb", [128, NP * 128], BF16)
            k.obuf = [sbb(f"obuf{i}", [128, NP * 512], BF16) for i in range(2)]
            k.rsstg = [sbb(f"rsstg{i}", [128, 512], F32) for i in range(2)]

            k.qk_done_c = {}
            k.v_done_c = {}
            k.rec1_done = {}
            k.e_sbf = 0
            emit_phase_a(k)
            for ci in range(NCH + 3):
                if ci < NCH:
                    if ci % 4 == 0 and ci // 4 + 1 < 8:
                        emit_x_slab(k, ci // 4 + 1)
                    emit_proj(k, ci)
                if 1 <= ci < NCH + 1:
                    emit_rec1(k, ci - 1)
                if ci >= 3:
                    emit_rec2(k, ci - 3)
                    if (ci - 3) % 4 == 3:
                        emit_wo_group(k, (ci - 3) // 4)
                if ci < NCH:
                    emit_bg(k, ci)
            emit_collectives(k)
            emit_barrier(k)

        with ExitStack() as d_ctx:
            Ed = d_ctx.enter_context
            sbd = lambda n, s, dt: Ed(nc.sbuf_tensor(n, s, dt))
            k.x1 = [sbd(f"x1_{t}", [128, KD * 512], F32) for t in range(2)]
            k.x1b = [sbd(f"x1b_{t}", [128, KD * 512], BF16) for t in range(2)]
            k.hid = sbd("hid", [128, 16 * 1024], BF16)
            k.wsl = [sbd(f"wsl{i}", [128, 16 * 128], BF16) for i in range(6)]
            k.w2sl = [sbd(f"w2sl{i}", [128, 2048], BF16) for i in range(3)]
            k.xrs = [sbd(f"xrs{i}", [128, 512], F32) for i in range(2)]
            k.dar = [sbd(f"dar{i}", [128, 512], F32) for i in range(4)]
            k.daw = [sbd(f"daw{i}", [128, 512], F32) for i in range(4)]
            k.preb = [sbd(f"preb{i}", [128, 512], BF16) for i in range(2)]
            k.sqb = [sbd(f"sqb{i}", [128, 512], BF16) for i in range(2)]
            k.stat = sbd("stat", [1, 1024], F32)
            k.lnt = sbd("lnt", [1, 512], F32)
            k.abc = sbd("abc", [128, 512], F32)
            k.bbc = sbd("bbc", [128, 512], F32)
            k.gsl = sbd("gsl", [128, 1024], BF16)

            for tile in range(2):
                emit_ln1(k, tile)
            for quarter in range(4):
                emit_ffn_quarter(k, quarter)
            for tile in range(2):
                emit_ln2_out(k, tile)

        def sync_end(sync):
            k.wait(sync, "sync", k.dOut, k.dOut.n)
        k.block.sync(sync_end)

        def vec_end(vector):
            k.wait(vector, "vector", k.dOut, k.dOut.n)
        k.block.vector(vec_end)
    return nc


def emit_phase_a(k):
    blk = k.block

    def sync_consts(sync):
        d = lambda o, i: k.inc(sync.dma_start(out=o, in_=i), k.dC, 16)
        d(k.c_identb[:, :], k.identb)
        d(k.c_h128b[:, :], k.h128b)
        d(k.c_i2h64b[:, :], k.i2h64b)
        d(k.c_masku_s4[:, :], k.masku_s4)
        d(k.c_masku_i4[:, :], k.masku_i4)
        d(k.c_onescol_b[:, :], k.onescol_b)
        d(k.c_onesrow_f[:, :], k.onesrow_f)
        d(k.c_gam[:, :], k.gam)
    blk.sync(sync_consts)

    def gp_consts(gpsimd):
        gpsimd.memset(k.c_eps[:, :], EPS)
        gpsimd.memset(k.c_nhalf[:, :], -0.5)
        gpsimd.memset(k.c_i128[:, :], 1.0 / 128.0)
        m = gpsimd.memset(k.c_i2048[:, :], 1.0 / 2048.0)
        k.inc(m, k.dC, 16)
    blk.gpsimd(gp_consts)
    k.CONSTS = k.dC.n

    def vec_zero(vector):
        k.inc(vector.memset(k.s32[:, :], 0.0), k.vC)
        z = k.rksq[:, :].rearrange("p (f c) -> p f c", f=NP)[:, :, 256:384]
        k.inc(vector.memset(z, 0.0), k.vC)
    blk.vector(vec_zero)
    k.SZERO = k.vC.n

    jobs = []
    for (src, dst, width) in ((k.wq, "wqb", 512), (k.wk, "wkb", 512),
                              (k.wv, "wvb", 512), (k.wb4, "wbb", 4)):
        for kk in range(KD):
            jobs.append((src, dst, width, kk, kk))
    for kk in range(NP):
        jobs.append((k.woP, "wopb", 1024, 2 * kk, kk))
        jobs.append((k.woP, "wopb", 1024, 2 * kk + 1, kk))
    k.qw_marks = {}

    qw_marks = {}
    VBASE = k.vC.n

    def scalar_qw(scalar):
        for t, (src, dst, width, sl, kk) in enumerate(jobs):
            if t >= 2:
                k.wait(scalar, "scalar", k.vC, VBASE + t - 1)
            if dst == "wopb":
                half = sl % 2
                src_ap = src[kk * 128:(kk + 1) * 128,
                             half * 1024:(half + 1) * 1024]
            else:
                src_ap = src[sl * 128:(sl + 1) * 128, :]
            qw_marks[t] = k.inc(
                scalar.dma_start(out=k.stgw[t % 2][:, 0:width], in_=src_ap),
                k.sq[t % 2], 16)
    blk.scalar(scalar_qw)

    def vec_qw(vector):
        for t, (src, dst, width, sl, kk) in enumerate(jobs):
            k.wait(vector, "vector", k.sq[t % 2], qw_marks[t])
            dt_ = getattr(k, dst)
            if dst == "wopb":
                half = sl % 2
                dv = dt_[:, kk * 2048 + half * 1024: kk * 2048 + (half + 1) * 1024]
            else:
                dv = dt_[:, sl * width:(sl + 1) * width]
            m = k.inc(vector.tensor_copy(dv, k.stgw[t % 2][:, 0:width]), k.vC)
            k.rel[("stgw", t % 2)] = (k.vC, m)
    blk.vector(vec_qw)
    k.QW_RDY = k.vC.n

    k.xcast_marks = {}
    k.xslab_done = {}
    k.proj_done = {}
    emit_x_slab(k, 0)

    k.bg = ([("w1", i) for i in range(16)] + [("w3", i) for i in range(16)] +
            [("w2a", i) for i in range(64)] + [("w2b", i) for i in range(64)])
    k.bg_i = 0


def emit_x_slab(k, a):
    blk = k.block
    base_t = a * 8
    xmarks = {}
    VBASE = k.vC.n

    def sync_fn(sync):
        for i in range(8):
            t = base_t + i
            if i >= 2:
                k.wait(sync, "sync", k.vC, VBASE + (i - 2) + 1)
            elif a >= 1:
                k.wait(sync, "sync", k.vC, k.xcast_marks[base_t - 2 + i])
            src = k.xt[:, a * 512:(a + 1) * 512].rearrange(
                "(kk p) t -> p kk t", p=128)[:, 2 * i:2 * i + 2, :]
            dst = k.stgx[t % 2][:, :].rearrange("p (a t) -> p a t", a=2)
            xmarks[t] = k.inc(sync.dma_start(out=dst, in_=src),
                              k.sx[t % 2], 16)

    def vec_fn(vector):
        for i in range(8):
            t = base_t + i
            if i == 0 and a >= 2:
                k.wait(vector, "vector", k.pC, k.proj_done[a - 2])
            k.wait(vector, "vector", k.sx[t % 2], xmarks[t])
            m = k.inc(vector.tensor_copy(
                k.xb[a % 2][:, 2 * i * 512:(2 * i + 2) * 512],
                k.stgx[t % 2][:, :]), k.vC)
            assert m == VBASE + i + 1, "xcast mark prediction broken"
            k.xcast_marks[t] = m
    blk.sync(sync_fn)
    blk.vector(vec_fn)
    k.xslab_done[a] = k.vC.n


def emit_proj(k, ci):
    blk = k.block
    a = ci // 4
    wv = a % 2
    cc = ci % 4
    slab = k.xb[wv]

    def pe(tensor):
        k.wait(tensor, "tensor", k.dC, k.CONSTS)
        k.wait(tensor, "tensor", k.vC, k.xslab_done[a])
        k.wait(tensor, "tensor", k.vC, k.QW_RDY)
        k.wait(tensor, "tensor", k.vC, k.SZERO)
        marks = {}
        for (wname, bank) in (("wqb", 0), ("wkb", 1), ("wvb", 2),
                              ("wbb", 3)):
            k.waitrel(tensor, "tensor", ("bank", bank))
            w = getattr(k, wname)
            width = 4 if wname == "wbb" else 512
            outp = k.pquad[:, bank * 512: bank * 512 + width]
            for kk in range(KD):
                mm = tensor.matmul(
                    outp,
                    slab[:, kk * 512 + cc * 128: kk * 512 + cc * 128 + 128],
                    w[:, kk * width:(kk + 1) * width],
                    start=(kk == 0), stop=(kk == KD - 1))
            marks[wname] = k.inc(mm, k.pC)
        k.pj = marks
    blk.tensor(pe)

    def vec(vector):
        k.wait(vector, "vector", k.dC, k.CONSTS)
        k.wait(vector, "vector", k.pC, k.pj["wqb"])
        e_q = k.inc(vector.tensor_copy(k.qf[:, :], k.pquad[:, 0:512]), k.vC)
        k.wait(vector, "vector", k.pC, k.pj["wkb"])
        e_k = k.inc(vector.tensor_copy(k.kf[:, :], k.pquad[:, 512:1024]),
                    k.vC)
        k.rel[("bank", 0)] = (k.vC, e_q)
        k.rel[("bank", 1)] = (k.vC, e_k)
        k.ev_qk = (e_q, e_k)
    blk.vector(vec)

    def sc(scalar):
        e_q, e_k = k.ev_qk
        k.wait(scalar, "scalar", k.dC, k.CONSTS)
        k.wait(scalar, "scalar", k.aC, k.aC.n)
        k.wait(scalar, "scalar", k.vC, e_q)
        for h in range(4):
            aa = scalar.activation(k.qsq[:, h * 128:(h + 1) * 128],
                                   k.qf[:, h * 128:(h + 1) * 128], AF.Square,
                                   accum_out=k.qss[:, h:h + 1])
        mq = k.inc(aa, k.aC)
        k.wait(scalar, "scalar", k.aC, mq)
        k.wait(scalar, "scalar", k.vC, e_k)
        for h in range(4):
            aa = scalar.activation(k.qsq[:, h * 128:(h + 1) * 128],
                                   k.kf[:, h * 128:(h + 1) * 128], AF.Square,
                                   accum_out=k.qss[:, 4 + h:5 + h])
        k.inc(aa, k.aC)
        k.wait(scalar, "scalar", k.pC, k.pj["wvb"])
        ev = k.inc(scalar.activation(k.vtm[wv][:, cc * 512:(cc + 1) * 512],
                                     k.pquad[:, 1024:1536], AF.Silu), k.aC)
        k.wait(scalar, "scalar", k.pC, k.pj["wbb"])
        eb = k.inc(scalar.activation(k.btf[wv][:, cc * 4:(cc + 1) * 4],
                                     k.pquad[:, 1536:1540], AF.Sigmoid),
                   k.aC)
        k.rel[("bank", 2)] = (k.aC, ev)
        k.rel[("bank", 3)] = (k.aC, eb)
        k.wait(scalar, "scalar", k.aC, k.aC.n)
        aa = scalar.activation(k.qss[:, 0:8], k.qss[:, 0:8], AF.Sqrt)
        k.sq_done = k.inc(aa, k.aC)
        k.v_done = ev
        k.b_done = eb
    blk.scalar(sc)

    def vec2(vector):
        k.wait(vector, "vector", k.aC, k.sq_done)
        v = vector.tensor_single_scalar(k.qss[:, 0:8], k.qss[:, 0:8], 1e-12,
                                        ALU.max)
        m = k.inc(v, k.vC)
        k.wait(vector, "vector", k.vC, m)
        v = vector.reciprocal(k.qss[:, 0:8], k.qss[:, 0:8])
        m = k.inc(v, k.vC)
        k.wait(vector, "vector", k.vC, m)
        for h in range(4):
            v = vector.tensor_scalar(
                k.qtm[wv][:, cc * 512 + h * 128: cc * 512 + (h + 1) * 128],
                k.qf[:, h * 128:(h + 1) * 128],
                k.qss[:, h:h + 1], DHS, op0=ALU.mult, op1=ALU.mult)
            k.inc(v, k.vC)
        for h in range(4):
            v = vector.tensor_scalar(
                k.ktm[wv][:, cc * 512 + h * 128: cc * 512 + (h + 1) * 128],
                k.kf[:, h * 128:(h + 1) * 128],
                k.qss[:, 4 + h:5 + h], None, op0=ALU.mult)
            k.inc(v, k.vC)
    blk.vector(vec2)
    k.qk_done_c[ci] = k.vC.n
    k.v_done_c[ci] = k.v_done
    if ci % 4 == 3:
        k.proj_done[a] = k.pC.n


def emit_rec1(k, ci):
    blk = k.block
    wv = (ci // 4) % 2
    cc = ci % 4
    ring = ci % 4
    ks_ = lambda p: k.ktm[wv][:, cc * 512 + p * 128: cc * 512 + (p + 1) * 128]
    qs_ = lambda p: k.qtm[wv][:, cc * 512 + p * 128: cc * 512 + (p + 1) * 128]
    rk = lambda p, o, w: k.rksq[:, p * 384 + o: p * 384 + o + w]
    pqv = lambda o, w: k.pquad[:, :].rearrange(
        "p (f c) -> p f c", f=4)[:, :, o:o + w]
    r4 = lambda t: t[:, :].rearrange("p (f c) -> p f c", f=NP)
    rks4 = lambda o, w: k.rksq[:, :].rearrange(
        "p (f c) -> p f c", f=NP)[:, :, o:o + w]
    xpv = lambda j, o, w: k.xp[j][:, :].rearrange(
        "p (f c) -> p f c", f=NP)[:, :, o:o + w]
    bview = k.btf[wv][:, cc * 4: cc * 4 + 4].unsqueeze(2).broadcast_to(
        [128, 4, 128])
    identv = k.c_identb[:, :].unsqueeze(1).broadcast_to([128, 4, 128])

    def pe1(tensor):
        k.wait(tensor, "tensor", k.dC, k.CONSTS)
        k.wait(tensor, "tensor", k.vC, k.qk_done_c[ci])
        k.wait(tensor, "tensor", k.aC, k.v_done_c[ci])
        k.waitrel(tensor, "tensor", "ptr")
        for p in range(NP):
            mm = tensor.transpose(k.ptr[:, p * 128:(p + 1) * 128], ks_(p),
                                  k.c_identb[:, :])
        k.m_trk = k.inc(mm, k.pC)
    blk.tensor(pe1)

    def ve1(vector):
        k.wait(vector, "vector", k.pC, k.m_trk)
        k.waitrel(vector, "vector", "rksq_kq")
        v = vector.tensor_copy(rks4(0, 128), r4(k.ptr))
        k.e_kfm = k.inc(v, k.vC)
    blk.vector(ve1)

    def pe2(tensor):
        k.wait(tensor, "tensor", k.vC, k.e_kfm)
        for p in range(NP):
            mm = tensor.transpose(k.ptr[:, p * 128:(p + 1) * 128], qs_(p),
                                  k.c_identb[:, :])
        k.m_trq = k.inc(mm, k.pC)
    blk.tensor(pe2)

    def sc1(scalar):
        k.wait(scalar, "scalar", k.pC, k.m_trq)
        k.waitrel(scalar, "scalar", ("qring", ring))
        aa = scalar.copy(r4(k.qring[ring]), r4(k.ptr))
        k.e_qfm = k.inc(aa, k.aC)
    blk.scalar(sc1)
    k.rel["ptr"] = (k.aC, k.e_qfm)

    def ve1b(vector):
        k.wait(vector, "vector", k.aC, k.e_qfm)
        v = vector.tensor_copy(rks4(128, 128), r4(k.qring[ring]))
        k.e_qfm2 = k.inc(v, k.vC)
    blk.vector(ve1b)

    def pe3(tensor):
        k.wait(tensor, "tensor", k.vC, k.e_qfm2)
        for bank in range(4):
            k.waitrel(tensor, "tensor", ("bank", bank))
        for p in range(NP):
            mm = tensor.matmul(k.pquad[:, p * 512: p * 512 + 256],
                               rk(p, 0, 128), rk(p, 0, 256),
                               start=True, stop=True)
        k.m_ags = k.inc(mm, k.pC)
    blk.tensor(pe3)
    k.rel["rksq_kq"] = (k.pC, k.m_ags)

    def ve2(vector):
        k.wait(vector, "vector", k.pC, k.m_ags)
        mss = k.c_masku_s4[:, :].rearrange("p (f c) -> p f c", f=4)
        msi = k.c_masku_i4[:, :].rearrange("p (f c) -> p f c", f=4)
        k.inc(vector.tensor_tensor(r4(k.u4), pqv(0, 128), mss, op=ALU.mult),
              k.vC)
        k.waitrel(vector, "vector", ("gmring", ring))
        k.inc(vector.tensor_tensor(r4(k.gmring[ring]), pqv(128, 128), msi,
                                   op=ALU.mult), k.vC)
        vt = k.vtm[wv][:, cc * 512:(cc + 1) * 512].rearrange(
            "p (f c) -> p f c", f=4)
        kt = k.ktm[wv][:, cc * 512:(cc + 1) * 512].rearrange(
            "p (f c) -> p f c", f=4)
        k.inc(vector.tensor_copy(xpv(0, 0, 128), vt), k.vC)
        k.inc(vector.tensor_copy(xpv(0, 128, 128), kt), k.vC)
        k.wait(vector, "vector", k.vC, k.vC.n)
        k.inc(vector.tensor_tensor(r4(k.pt[0]), r4(k.u4), bview,
                                   op=ALU.mult), k.vC)
        v = vector.tensor_tensor(r4(k.dgn4), identv, bview, op=ALU.mult)
        k.e_rec0 = k.inc(v, k.vC)
    blk.vector(ve2)

    def pe4(tensor):
        k.wait(tensor, "tensor", k.vC, k.e_rec0)
        for p in range(NP):
            mm = tensor.matmul(k.pquad[:, p * 512: p * 512 + 128],
                               k.u4[:, p * 128:(p + 1) * 128],
                               k.dgn4[:, p * 128:(p + 1) * 128],
                               start=True, stop=True)
        k.m_p0 = k.inc(mm, k.pC)
    blk.tensor(pe4)

    def sc2(scalar):
        k.wait(scalar, "scalar", k.pC, k.m_p0)
        aa = scalar.copy(xpv(0, 256, 128), pqv(0, 128))
        k.e_p0 = k.inc(aa, k.aC)
    blk.scalar(sc2)

    for j in range(NLEV):
        cur, nxt = j % 2, 1 - j % 2

        def pej(tensor, j=j, cur=cur):
            if j == 0:
                k.wait(tensor, "tensor", k.aC, k.e_p0)
                k.wait(tensor, "tensor", k.vC, k.e_rec0)
            else:
                k.wait(tensor, "tensor", k.vC, k.e_ptcopy)
            w = 384 if j < NLEV - 1 else 256
            for p in range(NP):
                mm = tensor.matmul(k.pquad[:, p * 512: p * 512 + w],
                                   k.pt[cur][:, p * 128:(p + 1) * 128],
                                   k.xp[cur][:, p * 384: p * 384 + w],
                                   start=True, stop=True)
            k.m_mmx = k.inc(mm, k.pC)
            if j < NLEV - 1:
                for p in range(NP):
                    mm = tensor.matmul(
                        k.pquad[:, p * 512 + 384: p * 512 + 512],
                        k.xp[cur][:, p * 384 + 256: p * 384 + 384],
                        k.pt[cur][:, p * 128:(p + 1) * 128],
                        start=True, stop=True)
                k.m_mmpt = k.inc(mm, k.pC)
        blk.tensor(pej)

        def vej(vector, j=j, cur=cur, nxt=nxt):
            k.wait(vector, "vector", k.pC,
                   k.m_mmpt if j < NLEV - 1 else k.m_mmx)
            v = vector.tensor_tensor(xpv(nxt, 0, 256), pqv(0, 256),
                                     xpv(cur, 0, 256), op=ALU.add)
            k.e_xadd = k.inc(v, k.vC)
            if j < NLEV - 1:
                v = vector.tensor_copy(xpv(nxt, 256, 128), pqv(256, 128))
                k.e_pcopy = k.inc(v, k.vC)
                v = vector.tensor_copy(r4(k.pt[nxt]), pqv(384, 128))
                k.e_ptcopy = k.inc(v, k.vC)
        blk.vector(vej)

    fin = NLEV % 2

    def ve3(vector):
        k.wait(vector, "vector", k.vC, k.e_xadd)
        k.waitrel(vector, "vector", ("tvring", ring))
        v = vector.tensor_copy(r4(k.tvring[ring]), xpv(fin, 0, 128))
        k.e_tv = k.inc(v, k.vC)
    blk.vector(ve3)

    def pe5(tensor):
        k.wait(tensor, "tensor", k.vC, k.e_xadd)
        k.waitrel(tensor, "tensor", "ptr")
        for p in range(NP):
            mm = tensor.transpose(k.ptr[:, p * 128:(p + 1) * 128],
                                  k.xp[fin][:, p * 384 + 128: p * 384 + 256],
                                  k.c_identb[:, :])
        k.m_trtk = k.inc(mm, k.pC)
    blk.tensor(pe5)

    def ve4(vector):
        k.wait(vector, "vector", k.pC, k.m_trtk)
        k.waitrel(vector, "vector", ("tkring", ring))
        v = vector.tensor_copy(r4(k.tkring[ring]), r4(k.ptr))
        k.e_tk = k.inc(v, k.vC)
    blk.vector(ve4)
    k.rel["ptr"] = (k.vC, k.e_tk)
    for b in range(4):
        k.rel[("bank", b)] = (k.vC, k.e_xadd)
    k.rec1_done[ci] = (k.e_tv, k.e_tk, k.vC.n, k.aC.n)


def emit_rec2(k, ci):
    blk = k.block
    wv = (ci // 4) % 2
    cc = ci % 4
    ring = ci % 4
    ks_ = lambda p: k.ktm[wv][:, cc * 512 + p * 128: cc * 512 + (p + 1) * 128]
    rks4 = lambda o, w: k.rksq[:, :].rearrange(
        "p (f c) -> p f c", f=NP)[:, :, o:o + w]
    rk = lambda p, o, w: k.rksq[:, p * 384 + o: p * 384 + o + w]
    r4 = lambda t: t[:, :].rearrange("p (f c) -> p f c", f=NP)
    bview = k.btf[wv][:, cc * 4: cc * 4 + 4].unsqueeze(2).broadcast_to(
        [128, 4, 128])

    def pe1(tensor):
        e_tv, e_tk, vfin, afin = k.rec1_done[ci]
        k.wait(tensor, "tensor", k.vC, e_tk)
        k.wait(tensor, "tensor", k.vC, e_tv)
        k.wait(tensor, "tensor", k.aC, k.e_sbf)
        k.waitrel(tensor, "tensor", "pa")
        for p in range(NP):
            mm = tensor.matmul(k.pa[:, p * 128:(p + 1) * 128],
                               k.tkring[ring][:, p * 128:(p + 1) * 128],
                               rk(p, 256, 128),
                               start=True, stop=True)
        k.m_tks = k.inc(mm, k.pC)
    blk.tensor(pe1)

    def ve1(vector):
        k.wait(vector, "vector", k.pC, k.m_tks)
        v = vector.tensor_tensor(r4(k.dl4), r4(k.tvring[ring]),
                                 r4(k.pa), op=ALU.subtract)
        m = k.inc(v, k.vC)
        k.wait(vector, "vector", k.vC, m)
        v = vector.tensor_tensor(r4(k.dl4), r4(k.dl4), bview, op=ALU.mult)
        k.e_dl = k.inc(v, k.vC)
        k.rel[("tvring", ring)] = (k.vC, k.e_dl)
    blk.vector(ve1)

    def pe2(tensor):
        k.wait(tensor, "tensor", k.vC, k.e_dl)
        k.waitrel(tensor, "tensor", "pb")
        for p in range(NP):
            tensor.matmul(k.pb[:, p * 128:(p + 1) * 128],
                          k.qring[ring][:, p * 128:(p + 1) * 128],
                          rk(p, 256, 128), start=True, stop=False)
            mm = tensor.matmul(k.pb[:, p * 128:(p + 1) * 128],
                               k.gmring[ring][:, p * 128:(p + 1) * 128],
                               k.dl4[:, p * 128:(p + 1) * 128],
                               start=False, stop=True)
        k.m_o = k.inc(mm, k.pC)
        k.rel[("qring", ring)] = (k.pC, k.m_o)
        k.rel[("gmring", ring)] = (k.pC, k.m_o)
        for p in range(NP):
            mm = tensor.matmul(k.pa[:, p * 128:(p + 1) * 128],
                               ks_(p), k.dl4[:, p * 128:(p + 1) * 128],
                               start=True, stop=True)
        k.m_s = k.inc(mm, k.pC)
        k.rel[("tkring", ring)] = (k.pC, k.m_s)
    blk.tensor(pe2)

    def ve2(vector):
        k.wait(vector, "vector", k.pC, k.m_s)
        v = vector.tensor_tensor(k.s32[:, :], k.s32[:, :], k.pa[:, :],
                                 op=ALU.add)
        k.e_s32 = k.inc(v, k.vC)
    blk.vector(ve2)
    k.rel["pa"] = (k.vC, k.e_s32)

    def sc3(scalar):
        k.wait(scalar, "scalar", k.vC, k.e_s32)
        aa = scalar.copy(rks4(256, 128), r4(k.s32))
        k.e_sbf = k.inc(aa, k.aC)
        k.wait(scalar, "scalar", k.pC, k.m_o)
        k.wait(scalar, "scalar", k.aC, k.aC.n)
        for p in range(NP):
            aa = scalar.activation(k.osq[:, p * 128:(p + 1) * 128],
                                   k.pb[:, p * 128:(p + 1) * 128], AF.Square,
                                   accum_out=k.oss[:, p:p + 1])
        k.inc(aa, k.aC)
        k.wait(scalar, "scalar", k.aC, k.aC.n)
        aa = scalar.activation(k.orstd[:, 0:NP], k.oss[:, 0:NP], AF.Ln,
                               bias=k.c_eps[:, 0:1], scale=k.c_i128[:, 0:1])
        k.inc(aa, k.aC)
        k.wait(scalar, "scalar", k.aC, k.aC.n)
        aa = scalar.activation(k.orstd[:, 0:NP], k.orstd[:, 0:NP], AF.Exp,
                               scale=k.c_nhalf[:, 0:1])
        k.e_orstd = k.inc(aa, k.aC)
    blk.scalar(sc3)
    k.rel["rksq_s"] = (k.aC, k.e_sbf)

    def ve5(vector):
        k.wait(vector, "vector", k.aC, k.e_orstd)
        ov = k.orstd[:, 0:NP].unsqueeze(2).broadcast_to([128, 4, 128])
        v = vector.tensor_tensor(r4(k.onb), r4(k.pb), ov, op=ALU.mult)
        k.e_onb = k.inc(v, k.vC)
    blk.vector(ve5)
    k.rel["pb"] = (k.vC, k.e_onb)

    def pe6(tensor):
        k.wait(tensor, "tensor", k.vC, k.e_onb)
        k.waitrel(tensor, "tensor", "ptr")
        for p in range(NP):
            mm = tensor.transpose(k.ptr[:, p * 128:(p + 1) * 128],
                                  k.onb[:, p * 128:(p + 1) * 128],
                                  k.c_identb[:, :])
        k.m_otr = k.inc(mm, k.pC)
    blk.tensor(pe6)

    def ve6(vector):
        k.wait(vector, "vector", k.pC, k.m_otr)
        ob = k.obuf[(ci // 4) % 2]
        k.waitrel(vector, "vector", ("obuf", (ci // 4) % 2))
        dst = ob[:, :].rearrange("p (f c) -> p f c", f=4)[
            :, :, cc * 128:(cc + 1) * 128]
        v = vector.tensor_scalar(dst, r4(k.ptr), k.c_gam[:, 0:1], None,
                                 op0=ALU.mult)
        k.e_obuf = k.inc(v, k.vC)
        k.obuf_done = k.vC.n
    blk.vector(ve6)
    k.rel["ptr"] = (k.vC, k.e_obuf)


def emit_wo_group(k, gi):
    blk = k.block
    half = gi // 4
    slot = gi % 4
    ob = k.obuf[gi % 2]
    obuf_mark = k.obuf_done

    for j in range(KD):
        bname = "pa" if j % 2 == 0 else "pb"

        def pe(tensor, j=j, bname=bname):
            if j == 0:
                k.wait(tensor, "tensor", k.vC, obuf_mark)
            k.waitrel(tensor, "tensor", bname)
            bank = getattr(k, bname)
            for p in range(NP):
                mm = tensor.matmul(bank[:, :],
                                   k.wopb[:, p * 2048 + j * 128:
                                          p * 2048 + (j + 1) * 128],
                                   ob[:, p * 512:(p + 1) * 512],
                                   start=(p == 0), stop=(p == NP - 1))
            k.m_wog = k.inc(mm, k.pC)
        blk.tensor(pe)

        def ve(vector, j=j, bname=bname):
            k.wait(vector, "vector", k.pC, k.m_wog)
            k.waitrel(vector, "vector", ("rsstg", j % 2))
            v = vector.tensor_copy(k.rsstg[j % 2][:, :],
                                   getattr(k, bname)[:, :])
            k.m_rss = k.inc(v, k.vC)
            k.rel[bname] = (k.vC, k.m_rss)
        blk.vector(ve)

        def sy(sync, j=j):
            k.wait(sync, "sync", k.vC, k.m_rss)
            dst = k.rs_in[half][slot * Dm + j * 128:
                                slot * Dm + (j + 1) * 128, :]
            d = k.inc(sync.dma_start(out=dst, in_=k.rsstg[j % 2][:, :]),
                      k.rsp[j % 2], 16)
            k.rel[("rsstg", j % 2)] = (k.rsp[j % 2], d)
        blk.sync(sy)

    k.rel[("obuf", gi % 2)] = (k.pC, k.pC.n)
    if slot == 3:
        k.rs_stage_mark = getattr(k, "rs_stage_mark", {})
        k.rs_stage_mark[half] = (k.rs0.n, k.rs1.n)


def emit_bg(k, ci):
    n = 4 if ci < 29 else 999
    done = 0
    while k.bg_i < len(k.bg) and done < n:
        kind, i = k.bg[k.bg_i]
        if kind == "w2b" and k.bg_i_w2a_pending(ci):
            break
        if kind in ("w1", "w3"):
            emit_bg_cast(k, kind, i)
        elif kind == "w2a":
            emit_w2a(k, i)
        else:
            emit_w2b(k, i)
        k.bg_i += 1
        done += 1


def _w2a_pending(k, ci):
    return False


KB.bg_i_w2a_pending = lambda self, ci: (
    not hasattr(self, "w2a_final"))


def emit_bg_cast(k, kind, i):
    blk = k.block
    src = k.w1S if kind == "w1" else k.w3S
    dst = k.w1b_in if kind == "w1" else k.w3b_in

    def sfn(scalar):
        k.waitrel(scalar, "scalar", ("stgw", i % 2))
        k.bg_dma = k.inc(scalar.dma_start(
            out=k.stgw[i % 2][:, :],
            in_=src[i * 128:(i + 1) * 128, :]), k.sq[i % 2], 16)
    blk.scalar(sfn)

    def afn(vector):
        k.wait(vector, "vector", k.sq[i % 2], k.bg_dma)
        k.waitrel(vector, "vector", ("bgev", i % 2))
        aa = vector.tensor_copy(k.bgev[i % 2][:, :], k.stgw[i % 2][:, :])
        k.e_bgc = k.inc(aa, k.vC)
        k.rel[("stgw", i % 2)] = (k.vC, k.e_bgc)
    blk.vector(afn)

    def sfn2(scalar):
        k.wait(scalar, "scalar", k.vC, k.e_bgc)
        d = k.inc(scalar.dma_start(out=dst[i * 128:(i + 1) * 128, :],
                                   in_=k.bgev[i % 2][:, :]), k.bgp[i % 2], 16)
        k.rel[("bgev", i % 2)] = (k.bgp[i % 2], d)
    blk.scalar(sfn2)
    if kind == "w1" and i == 15:
        k.w1_out_mark = (k.bg0.n, k.bg1.n)
    if kind == "w3" and i == 15:
        k.w3_out_mark = (k.bg0.n, k.bg1.n)


def emit_w2a(k, g):
    blk = k.block

    def sfn(scalar):
        k.waitrel(scalar, "scalar", ("stgw", g % 2))
        k.w2a_dma = k.inc(scalar.dma_start(
            out=k.stgw[g % 2][:, 0:256],
            in_=k.w2S[g * 128:(g + 1) * 128, :]), k.sq[g % 2], 16)
    blk.scalar(sfn)

    def afn(vector):
        k.wait(vector, "vector", k.sq[g % 2], k.w2a_dma)
        k.waitrel(vector, "vector", ("w2cs", g % 2))
        aa = vector.tensor_copy(k.w2cs[g % 2][:, 0:256], k.stgw[g % 2][:, 0:256])
        k.e_w2c = k.inc(aa, k.vC)
        k.rel[("stgw", g % 2)] = (k.vC, k.e_w2c)
    blk.vector(afn)

    def pfn(tensor):
        k.wait(tensor, "tensor", k.vC, k.e_w2c)
        k.waitrel(tensor, "tensor", "pst")
        mm = tensor.matmul(k.pst[:, 0:256], k.c_h128b[:, :],
                           k.w2cs[g % 2][:, 0:256], start=True, stop=True)
        k.m_w2 = k.inc(mm, k.pC)
        k.rel[("w2cs", g % 2)] = (k.pC, k.m_w2)
    blk.tensor(pfn)

    def vfn(vector):
        k.wait(vector, "vector", k.pC, k.m_w2)
        k.waitrel(vector, "vector", ("w2ev", g % 2))
        v = vector.tensor_scalar(k.w2ev[g % 2][:, 0:256], k.pst[:, 0:256],
                                 RSH, None, op0=ALU.mult)
        k.e_w2e = k.inc(v, k.vC)
        k.rel["pst"] = (k.vC, k.e_w2e)
    blk.vector(vfn)

    def sfn2(scalar):
        k.wait(scalar, "scalar", k.vC, k.e_w2e)
        d = k.inc(scalar.dma_start(out=k.w2bA[g * 128:(g + 1) * 128, :],
                                   in_=k.w2ev[g % 2][:, 0:256]),
                  k.wep[g % 2], 16)
        k.rel[("w2ev", g % 2)] = (k.wep[g % 2], d)
    blk.scalar(sfn2)
    if g == 63:
        k.w2a_final = (k.we0.n, k.we1.n)


def emit_w2b(k, it):
    blk = k.block
    e0 = 2 * it

    def sfn(scalar):
        k.wait(scalar, "scalar", k.we0, k.w2a_final[0])
        k.wait(scalar, "scalar", k.we1, k.w2a_final[1])
        k.waitrel(scalar, "scalar", ("w2cs", it % 2))
        src = k.w2bA.rearrange("(g e) d -> e g d", g=64)[e0:e0 + 2]
        k.w2b_dma = k.inc(
            scalar.dma_start(out=k.w2cs[it % 2][:, 0:256], in_=src),
            k.wcp[it % 2], 16)
    blk.scalar(sfn)

    def pfn(tensor):
        k.wait(tensor, "tensor", k.wcp[it % 2], k.w2b_dma)
        k.waitrel(tensor, "tensor", "pst")
        mm = tensor.matmul(k.pst[:, 0:256], k.c_i2h64b[:, :],
                           k.w2cs[it % 2][:, 0:256], start=True, stop=True)
        k.m_w2 = k.inc(mm, k.pC)
        k.rel[("w2cs", it % 2)] = (k.pC, k.m_w2)
    blk.tensor(pfn)

    def vfn(vector):
        k.wait(vector, "vector", k.pC, k.m_w2)
        k.waitrel(vector, "vector", ("w2ev", it % 2))
        v = vector.tensor_copy(k.w2ev[it % 2][:, 0:256], k.pst[:, 0:256])
        k.e_w2e = k.inc(v, k.vC)
        k.rel["pst"] = (k.vC, k.e_w2e)
    blk.vector(vfn)

    def sfn2(scalar):
        k.wait(scalar, "scalar", k.vC, k.e_w2e)
        dst = k.w2p_in.rearrange("(g e) d -> e g d", g=64)[e0:e0 + 2]
        d = k.inc(scalar.dma_start(out=dst, in_=k.w2ev[it % 2][:, 0:256]),
                  k.wep[it % 2], 16)
        k.rel[("w2ev", it % 2)] = (k.wep[it % 2], d)
    blk.scalar(sfn2)
    if it == 63:
        k.w2p_final = (k.we0.n, k.we1.n)


def emit_collectives(k):
    blk = k.block
    grp8 = [list(range(8))]
    grp4 = [[0, 1, 2, 3], [4, 5, 6, 7]]

    def gp(gpsimd):
        gpsimd.wait_ge(k.bg0.sem, k.w1_out_mark[0])
        gpsimd.wait_ge(k.bg1.sem, k.w1_out_mark[1])
        cc = gpsimd.collective_compute(
            "AllGather", ALU.bypass, replica_groups=grp8,
            ins=[k.w1b_in[:, :]], outs=[k.w1b_out[:, :]])
        k.inc(cc, k.ccW1)
        gpsimd.wait_ge(k.bg0.sem, k.w3_out_mark[0])
        gpsimd.wait_ge(k.bg1.sem, k.w3_out_mark[1])
        cc = gpsimd.collective_compute(
            "AllGather", ALU.bypass, replica_groups=grp8,
            ins=[k.w3b_in[:, :]], outs=[k.w3b_out[:, :]])
        k.inc(cc, k.ccW3)
        gpsimd.wait_ge(k.rs0.sem, k.rs_stage_mark[0][0])
        gpsimd.wait_ge(k.rs1.sem, k.rs_stage_mark[0][1])
        cc = gpsimd.collective_compute(
            "ReduceScatter", ALU.add, replica_groups=grp4,
            ins=[k.rs_in[0][:, :]], outs=[k.rs_out[0][:, :]])
        k.inc(cc, k.cc0)
        gpsimd.wait_ge(k.we0.sem, k.w2p_final[0])
        gpsimd.wait_ge(k.we1.sem, k.w2p_final[1])
        cc = gpsimd.collective_compute(
            "AllGather", ALU.bypass, replica_groups=grp8,
            ins=[k.w2p_in[:, :]], outs=[k.w2p_out[:, :]])
        k.inc(cc, k.ccW2)
        gpsimd.wait_ge(k.rs0.sem, k.rs_stage_mark[1][0])
        gpsimd.wait_ge(k.rs1.sem, k.rs_stage_mark[1][1])
        cc = gpsimd.collective_compute(
            "ReduceScatter", ALU.add, replica_groups=grp4,
            ins=[k.rs_in[1][:, :]], outs=[k.rs_out[1][:, :]])
        k.inc(cc, k.cc1)
    blk.gpsimd(gp)


def emit_barrier(k):
    blk = k.block
    marks = [(k.vC, k.vC.n), (k.aC, k.aC.n), (k.pC, k.pC.n),
             (k.rs0, k.rs0.n), (k.rs1, k.rs1.n),
             (k.sx0, k.sx0.n), (k.sx1, k.sx1.n),
             (k.sq0, k.sq0.n), (k.sq1, k.sq1.n),
             (k.bg0, k.bg0.n), (k.bg1, k.bg1.n),
             (k.wc0, k.wc0.n), (k.wc1, k.wc1.n),
             (k.we0, k.we0.n), (k.we1, k.we1.n)]

    for engname in ("sync", "vector", "scalar", "tensor"):
        def fn(eng, engname=engname):
            for ctr, val in marks:
                k.wait(eng, engname, ctr, val)
        getattr(blk, engname)(fn)


def emit_ln1(k, tile):
    blk = k.block
    ccw = k.cc0 if tile == 0 else k.cc1
    x1 = k.x1[tile]
    x1b = k.x1b[tile]

    k.d_marks = {}

    VB_LN = k.vC.n

    def sy(sync):
        k.wait(sync, "sync", ccw, 1)
        for j in range(KD):
            if j >= 2:
                k.wait(sync, "sync", k.vC, VB_LN + (j - 2) + 1)
            else:
                k.waitrel(sync, "sync", ("xrs", j % 2))
            k.inc(sync.dma_start(
                out=x1[:, j * 512:(j + 1) * 512],
                in_=k.rs_out[tile][j * 128:(j + 1) * 128, :]),
                k.xrp[j % 2], 16)
            m = k.inc(sync.dma_start(
                out=k.xrs[j % 2][:, :],
                in_=k.xft[j * 128:(j + 1) * 128,
                          tile * 512:(tile + 1) * 512]), k.xrp[j % 2], 16)
            k.d_marks[j] = m
    blk.sync(sy)

    def ve(vector):
        for j in range(KD):
            k.wait(vector, "vector", k.xrp[j % 2], k.d_marks[j])
            v = vector.tensor_tensor(x1[:, j * 512:(j + 1) * 512],
                                     x1[:, j * 512:(j + 1) * 512],
                                     k.xrs[j % 2][:, :], op=ALU.add)
            m = k.inc(v, k.vC)
            assert m == VB_LN + j + 1, "ln1 mark prediction broken"
            k.rel[("xrs", j % 2)] = (k.vC, m)
            k.d_marks[j] = m
    blk.vector(ve)

    def prep(j):
        def af(scalar, j=j):
            k.wait(scalar, "scalar", k.vC, k.d_marks[j])
            k.waitrel(scalar, "scalar", ("preb", j % 2))
            aa = scalar.copy(k.preb[j % 2][:, :], x1[:, j * 512:(j + 1) * 512])
            k.e_preb = k.inc(aa, k.aC)
        blk.scalar(af)

        def vf(vector, j=j):
            k.wait(vector, "vector", k.vC, k.d_marks[j])
            k.waitrel(vector, "vector", ("sqb", j % 2))
            v = vector.tensor_tensor(k.sqb[j % 2][:, :],
                                     x1[:, j * 512:(j + 1) * 512],
                                     x1[:, j * 512:(j + 1) * 512],
                                     op=ALU.mult)
            k.e_sqb = k.inc(v, k.vC)
        blk.vector(vf)

        def pf(tensor, j=j):
            k.wait(tensor, "tensor", k.aC, k.e_preb)
            k.wait(tensor, "tensor", k.vC, k.e_sqb)
            if j == 0:
                k.waitrel(tensor, "tensor", "pst")
            mm = tensor.matmul(k.pst[0:1, 0:512], k.c_onescol_b[:, 0:1],
                               k.preb[j % 2][:, :],
                               start=(j == 0), stop=(j == KD - 1))
            m1 = k.inc(mm, k.pC)
            k.rel[("preb", j % 2)] = (k.pC, m1)
            mm = tensor.matmul(k.pa[0:1, 0:512], k.c_onescol_b[:, 0:1],
                               k.sqb[j % 2][:, :],
                               start=(j == 0), stop=(j == KD - 1))
            m2 = k.inc(mm, k.pC)
            k.rel[("sqb", j % 2)] = (k.pC, m2)
            if j == KD - 1:
                k.m_stats = m2
        blk.tensor(pf)
    for j in range(KD):
        prep(j)

    def ve2(vector):
        k.wait(vector, "vector", k.pC, k.m_stats)
        v = vector.tensor_scalar(k.stat[0:1, 0:512], k.pst[0:1, 0:512],
                                 1.0 / 2048.0, None, op0=ALU.mult)
        k.inc(v, k.vC)
        v = vector.tensor_scalar(k.stat[0:1, 512:1024], k.pa[0:1, 0:512],
                                 1.0 / 2048.0, None, op0=ALU.mult)
        m = k.inc(v, k.vC)
        k.wait(vector, "vector", k.vC, m)
        v = vector.tensor_tensor(k.lnt[0:1, 0:512], k.stat[0:1, 0:512],
                                 k.stat[0:1, 0:512], op=ALU.mult)
        m = k.inc(v, k.vC)
        k.wait(vector, "vector", k.vC, m)
        v = vector.tensor_tensor(k.lnt[0:1, 0:512], k.stat[0:1, 512:1024],
                                 k.lnt[0:1, 0:512], op=ALU.subtract)
        k.e_var = k.inc(v, k.vC)
        k.rel["pst"] = (k.vC, k.e_var)
        k.rel["pa"] = (k.vC, k.e_var)
    blk.vector(ve2)

    def sc2(scalar):
        k.wait(scalar, "scalar", k.vC, k.e_var)
        aa = scalar.activation(k.lnt[0:1, 0:512], k.lnt[0:1, 0:512], AF.Ln,
                               bias=k.c_eps[0:1, 0:1])
        m = k.inc(aa, k.aC)
        k.wait(scalar, "scalar", k.aC, m)
        aa = scalar.activation(k.lnt[0:1, 0:512], k.lnt[0:1, 0:512], AF.Exp,
                               scale=k.c_nhalf[0:1, 0:1])
        k.e_rstd = k.inc(aa, k.aC)
    blk.scalar(sc2)

    def pe2(tensor):
        k.wait(tensor, "tensor", k.aC, k.e_rstd)
        k.waitrel(tensor, "tensor", "pst")
        mm = tensor.matmul(k.pst[:, 0:512], k.c_onesrow_f[0:1, :],
                           k.lnt[0:1, 0:512], start=True, stop=True)
        k.m_bc1 = k.inc(mm, k.pC)
    blk.tensor(pe2)

    def ve3(vector):
        k.wait(vector, "vector", k.pC, k.m_bc1)
        v = vector.tensor_copy(k.abc[:, :], k.pst[:, 0:512])
        m = k.inc(v, k.vC)
        k.rel["pst"] = (k.vC, m)
        k.e_abc = m
    blk.vector(ve3)

    def pe3(tensor):
        k.wait(tensor, "tensor", k.vC, k.e_abc)
        k.waitrel(tensor, "tensor", "pst")
        mm = tensor.matmul(k.pst[:, 0:512], k.c_onesrow_f[0:1, :],
                           k.stat[0:1, 0:512], start=True, stop=True)
        k.m_bc2 = k.inc(mm, k.pC)
    blk.tensor(pe3)

    def ve4(vector):
        k.wait(vector, "vector", k.pC, k.m_bc2)
        v = vector.tensor_copy(k.bbc[:, :], k.pst[:, 0:512])
        m = k.inc(v, k.vC)
        k.rel["pst"] = (k.vC, m)
        k.wait(vector, "vector", k.vC, m)
        k.x1_marks = {}
        for j in range(KD):
            v = vector.tensor_tensor(x1[:, j * 512:(j + 1) * 512],
                                     x1[:, j * 512:(j + 1) * 512],
                                     k.bbc[:, :], op=ALU.subtract)
            m = k.inc(v, k.vC)
            k.wait(vector, "vector", k.vC, m)
            v = vector.tensor_tensor(x1[:, j * 512:(j + 1) * 512],
                                     x1[:, j * 512:(j + 1) * 512],
                                     k.abc[:, :], op=ALU.mult)
            k.x1_marks[j] = k.inc(v, k.vC)
    blk.vector(ve4)

    def sc3(scalar):
        for j in range(KD):
            k.wait(scalar, "scalar", k.vC, k.x1_marks[j])
            aa = scalar.copy(x1b[:, j * 512:(j + 1) * 512],
                             x1[:, j * 512:(j + 1) * 512])
            k.inc(aa, k.aC)
    blk.scalar(sc3)
    k.x1b_done = k.aC.n
    k.rel[("x1done", tile)] = (k.aC, k.aC.n)


def emit_ffn_quarter(k, quarter):
    blk = k.block
    j0 = quarter * 16

    for jj in range(16):
        j = j0 + jj
        r, jl = j // 8, j % 8
        sbi = j % 3
        buf = k.wsl[sbi]
        buf3 = k.wsl[3 + sbi]

        def syw(sync, j=j, r=r, jl=jl, buf=buf, buf3=buf3, sbi=sbi):
            k.waitrel(sync, "sync", ("wsl", sbi))
            pass
            k.inc(sync.dma_start(
                out=buf[:, :].rearrange("p (kk c) -> p kk c", c=128),
                in_=k.w1b_out[r * Dm:(r + 1) * Dm,
                              jl * 128:(jl + 1) * 128].rearrange(
                    "(kk p) c -> p kk c", p=128)), k.wsp[sbi], 16)
            m = k.inc(sync.dma_start(
                out=buf3[:, :].rearrange("p (kk c) -> p kk c", c=128),
                in_=k.w3b_out[r * Dm:(r + 1) * Dm,
                              jl * 128:(jl + 1) * 128].rearrange(
                    "(kk p) c -> p kk c", p=128)), k.wsp[sbi], 16)
            k.wslab_mark = m
        blk.sync(syw)

        def pe(tensor, j=j, jj=jj, buf=buf, buf3=buf3, sbi=sbi,
               wmark=None):
            if j == j0 and quarter == 0:
                k.wait(tensor, "tensor", k.ccW1, 1)
                k.wait(tensor, "tensor", k.ccW3, 1)
                k.wait(tensor, "tensor", k.aC, k.x1b_done)
            k.wait(tensor, "tensor", k.wsp[sbi], k.wslab_mark)
            k.up_marks = {}
            for tile in range(2):
                gbank = k.pquad[:, (2 * tile) * 512:(2 * tile) * 512 + 512]
                lbank = k.pquad[:, (2 * tile + 1) * 512:
                                (2 * tile + 1) * 512 + 512]
                k.waitrel(tensor, "tensor", ("bank", 2 * tile))
                for kk in range(KD):
                    mm = tensor.matmul(
                        gbank, buf[:, kk * 128:(kk + 1) * 128],
                        k.x1b[tile][:, kk * 512:(kk + 1) * 512],
                        start=(kk == 0), stop=(kk == KD - 1))
                k.up_marks[(tile, "g")] = k.inc(mm, k.pC)
                k.waitrel(tensor, "tensor", ("bank", 2 * tile + 1))
                for kk in range(KD):
                    mm = tensor.matmul(
                        lbank, buf3[:, kk * 128:(kk + 1) * 128],
                        k.x1b[tile][:, kk * 512:(kk + 1) * 512],
                        start=(kk == 0), stop=(kk == KD - 1))
                k.up_marks[(tile, "l")] = k.inc(mm, k.pC)
            k.rel[("wsl", sbi)] = (k.pC, k.pC.n)
        blk.tensor(pe)

        def sc(scalar, jj=jj):
            k.gsl_marks = {}
            for tile in range(2):
                k.waitrel(scalar, "scalar", ("gsl", tile))
                k.wait(scalar, "scalar", k.pC, k.up_marks[(tile, "g")])
                aa = scalar.activation(
                    k.gsl[:, tile * 512:(tile + 1) * 512],
                    k.pquad[:, (2 * tile) * 512:(2 * tile) * 512 + 512],
                    AF.Silu)
                m = k.inc(aa, k.aC)
                k.rel[("bank", 2 * tile)] = (k.aC, m)
                k.gsl_marks[tile] = m
        blk.scalar(sc)

        def ve(vector, jj=jj):
            for tile in range(2):
                k.wait(vector, "vector", k.aC, k.gsl_marks[tile])
                k.wait(vector, "vector", k.pC, k.up_marks[(tile, "l")])
                v = vector.tensor_tensor(
                    k.hid[:, jj * 1024 + tile * 512:
                          jj * 1024 + (tile + 1) * 512],
                    k.gsl[:, tile * 512:(tile + 1) * 512],
                    k.pquad[:, (2 * tile + 1) * 512:
                            (2 * tile + 1) * 512 + 512],
                    op=ALU.mult)
                m = k.inc(v, k.vC)
                k.rel[("bank", 2 * tile + 1)] = (k.vC, m)
                k.rel[("gsl", tile)] = (k.vC, m)
        blk.vector(ve)
    k.hid_done = k.vC.n

    k.dac_wr = getattr(k, "dac_wr", {})
    k.dac_rd = getattr(k, "dac_rd", {})
    k.x2_marks = getattr(k, "x2_marks", {})
    k.dawm = getattr(k, "dawm", {})
    for st in range(8):
        k.w2s_marks = {}
        k.dn_jj_marks = {}
        k.dn_marks = {}

        if quarter > 0:
            def syr(sync, st=st):
                for g in range(4):
                    tile, dd = g // 2, g % 2
                    dj = 2 * st + dd
                    wr = k.dac_wr[(quarter - 1, tile, dj)]
                    k.wait(sync, "sync", wr[0], wr[1])
                    k.waitrel(sync, "sync", ("dar", g))
                    k.inc(sync.dma_start(
                        out=k.dar[g][:, :],
                        in_=k.dacc[dj * 128:(dj + 1) * 128,
                                   tile * 512:(tile + 1) * 512]), k.daR, 16)
                k.dac_rd_set = k.daR.n
            blk.sync(syr)

        for jj in range(16):
            j2 = j0 + jj

            def syw2(sync, st=st, jj=jj, j2=j2):
                if st == 0 and quarter == 0 and jj == 0:
                    k.wait(sync, "sync", k.ccW2, 1)
                if jj >= 3:
                    k.wait(sync, "sync", k.pC, k.dn_jj_marks[jj - 3])
                else:
                    k.waitrel(sync, "sync", ("w2sl", jj % 3))
                m = k.inc(sync.dma_start(
                    out=k.w2sl[jj % 3][:, 0:256],
                    in_=k.w2p_out[st * HID + j2 * 128:
                                  st * HID + (j2 + 1) * 128, :]),
                    k.w2p_[jj % 3], 16)
                k.w2s_marks[jj] = m
            blk.sync(syw2)

            def ped(tensor, st=st, jj=jj):
                if st == 0 and jj == 0:
                    k.wait(tensor, "tensor", k.vC, k.hid_done)
                k.wait(tensor, "tensor", k.w2p_[jj % 3], k.w2s_marks[jj])
                for g in range(4):
                    tile, dd = g // 2, g % 2
                    if jj == 0:
                        k.waitrel(tensor, "tensor", ("bank", g))
                    bank = k.pquad[:, g * 512: g * 512 + 512]
                    mm = tensor.matmul(
                        bank, k.w2sl[jj % 3][:, dd * 128:(dd + 1) * 128],
                        k.hid[:, jj * 1024 + tile * 512:
                              jj * 1024 + (tile + 1) * 512],
                        start=(jj == 0), stop=(jj == 15))
                    if jj == 15:
                        k.dn_marks[g] = k.inc(mm, k.pC)
                if jj < 15:
                    k.dn_jj_marks[jj] = k.inc(mm, k.pC)
                else:
                    k.dn_jj_marks[jj] = k.dn_marks[3]
                    for b4 in range(3):
                        k.rel[("w2sl", b4)] = (k.pC, k.pC.n)
            blk.tensor(ped)

        def ved(vector, st=st):
            for g in range(4):
                tile, dd = g // 2, g % 2
                dj = 2 * st + dd
                k.wait(vector, "vector", k.pC, k.dn_marks[g])
                bank = k.pquad[:, g * 512: g * 512 + 512]
                if quarter == 0:
                    k.waitrel(vector, "vector", ("daw", g))
                    v = vector.tensor_copy(k.daw[g][:, :], bank)
                    m = k.inc(v, k.vC)
                    k.dawm[(tile, dj)] = m
                elif quarter < 3:
                    k.wait(vector, "vector", k.daR, k.dac_rd_set)
                    k.waitrel(vector, "vector", ("daw", g))
                    v = vector.tensor_tensor(k.daw[g][:, :], bank,
                                             k.dar[g][:, :], op=ALU.add)
                    m = k.inc(v, k.vC)
                    k.dawm[(tile, dj)] = m
                    k.rel[("dar", g)] = (k.vC, m)
                else:
                    k.wait(vector, "vector", k.daR, k.dac_rd_set)
                    x1 = k.x1[tile]
                    v = vector.tensor_tensor(k.dar[g][:, :], bank,
                                             k.dar[g][:, :], op=ALU.add)
                    m = k.inc(v, k.vC)
                    k.wait(vector, "vector", k.vC, m)
                    v = vector.tensor_tensor(
                        x1[:, dj * 512:(dj + 1) * 512],
                        x1[:, dj * 512:(dj + 1) * 512],
                        k.dar[g][:, :], op=ALU.add)
                    m = k.inc(v, k.vC)
                    k.x2_marks[(tile, dj)] = m
                    k.rel[("dar", g)] = (k.vC, m)
                k.rel[("bank", g)] = (k.vC, m)
        blk.vector(ved)

        if quarter < 3:
            def syd(sync, st=st):
                for g in range(4):
                    tile, dd = g // 2, g % 2
                    dj = 2 * st + dd
                    k.wait(sync, "sync", k.vC, k.dawm[(tile, dj)])
                    d = k.inc(sync.dma_start(
                        out=k.dacc[dj * 128:(dj + 1) * 128,
                                   tile * 512:(tile + 1) * 512],
                        in_=k.daw[g][:, :]), k.dwp[g], 16)
                    k.rel[("daw", g)] = (k.dwp[g], d)
                    k.dac_wr[(quarter, tile, dj)] = (k.dwp[g], d)
            blk.sync(syd)



def emit_ln2_out(k, tile):
    blk = k.block
    x1 = k.x1[tile]

    def prep(j):
        def af(scalar, j=j):
            k.wait(scalar, "scalar", k.vC, k.x2_marks[(tile, j)])
            k.waitrel(scalar, "scalar", ("preb", j % 2))
            aa = scalar.copy(k.preb[j % 2][:, :], x1[:, j * 512:(j + 1) * 512])
            k.e_preb = k.inc(aa, k.aC)
        blk.scalar(af)

        def vf(vector, j=j):
            k.wait(vector, "vector", k.vC, k.x2_marks[(tile, j)])
            k.waitrel(vector, "vector", ("sqb", j % 2))
            v = vector.tensor_tensor(k.sqb[j % 2][:, :],
                                     x1[:, j * 512:(j + 1) * 512],
                                     x1[:, j * 512:(j + 1) * 512],
                                     op=ALU.mult)
            k.e_sqb = k.inc(v, k.vC)
        blk.vector(vf)

        def pf(tensor, j=j):
            k.wait(tensor, "tensor", k.aC, k.e_preb)
            k.wait(tensor, "tensor", k.vC, k.e_sqb)
            if j == 0:
                k.waitrel(tensor, "tensor", "pst")
                k.waitrel(tensor, "tensor", ("bank", 0))
            mm = tensor.matmul(k.pst[0:1, 0:512], k.c_onescol_b[:, 0:1],
                               k.preb[j % 2][:, :],
                               start=(j == 0), stop=(j == KD - 1))
            m1 = k.inc(mm, k.pC)
            k.rel[("preb", j % 2)] = (k.pC, m1)
            mm = tensor.matmul(k.pquad[0:1, 0:512], k.c_onescol_b[:, 0:1],
                               k.sqb[j % 2][:, :],
                               start=(j == 0), stop=(j == KD - 1))
            m2 = k.inc(mm, k.pC)
            k.rel[("sqb", j % 2)] = (k.pC, m2)
            if j == KD - 1:
                k.m_stats = m2
        blk.tensor(pf)
    for j in range(KD):
        prep(j)

    def ve2(vector):
        k.wait(vector, "vector", k.pC, k.m_stats)
        v = vector.tensor_scalar(k.stat[0:1, 0:512], k.pst[0:1, 0:512],
                                 1.0 / 2048.0, None, op0=ALU.mult)
        k.inc(v, k.vC)
        v = vector.tensor_scalar(k.stat[0:1, 512:1024], k.pquad[0:1, 0:512],
                                 1.0 / 2048.0, None, op0=ALU.mult)
        m = k.inc(v, k.vC)
        k.wait(vector, "vector", k.vC, m)
        v = vector.tensor_tensor(k.lnt[0:1, 0:512], k.stat[0:1, 0:512],
                                 k.stat[0:1, 0:512], op=ALU.mult)
        m = k.inc(v, k.vC)
        k.wait(vector, "vector", k.vC, m)
        v = vector.tensor_tensor(k.lnt[0:1, 0:512], k.stat[0:1, 512:1024],
                                 k.lnt[0:1, 0:512], op=ALU.subtract)
        k.e_var = k.inc(v, k.vC)
        k.rel["pst"] = (k.vC, k.e_var)
        k.rel[("bank", 0)] = (k.vC, k.e_var)
    blk.vector(ve2)

    def sc2(scalar):
        k.wait(scalar, "scalar", k.vC, k.e_var)
        aa = scalar.activation(k.lnt[0:1, 0:512], k.lnt[0:1, 0:512], AF.Ln,
                               bias=k.c_eps[0:1, 0:1])
        m = k.inc(aa, k.aC)
        k.wait(scalar, "scalar", k.aC, m)
        aa = scalar.activation(k.lnt[0:1, 0:512], k.lnt[0:1, 0:512], AF.Exp,
                               scale=k.c_nhalf[0:1, 0:1])
        k.e_rstd = k.inc(aa, k.aC)
    blk.scalar(sc2)

    def pe2(tensor):
        k.wait(tensor, "tensor", k.aC, k.e_rstd)
        k.waitrel(tensor, "tensor", "pst")
        mm = tensor.matmul(k.pst[:, 0:512], k.c_onesrow_f[0:1, :],
                           k.lnt[0:1, 0:512], start=True, stop=True)
        k.m_bc1 = k.inc(mm, k.pC)
    blk.tensor(pe2)

    def ve3(vector):
        k.wait(vector, "vector", k.pC, k.m_bc1)
        v = vector.tensor_copy(k.abc[:, :], k.pst[:, 0:512])
        m = k.inc(v, k.vC)
        k.rel["pst"] = (k.vC, m)
        k.e_abc = m
    blk.vector(ve3)

    def pe3(tensor):
        k.wait(tensor, "tensor", k.vC, k.e_abc)
        k.waitrel(tensor, "tensor", "pst")
        mm = tensor.matmul(k.pst[:, 0:512], k.c_onesrow_f[0:1, :],
                           k.stat[0:1, 0:512], start=True, stop=True)
        k.m_bc2 = k.inc(mm, k.pC)
    blk.tensor(pe3)

    def ve4(vector):
        k.wait(vector, "vector", k.pC, k.m_bc2)
        v = vector.tensor_copy(k.bbc[:, :], k.pst[:, 0:512])
        m = k.inc(v, k.vC)
        k.rel["pst"] = (k.vC, m)
        k.wait(vector, "vector", k.vC, m)
        k.o_marks = {}
        for j in range(KD):
            v = vector.tensor_tensor(x1[:, j * 512:(j + 1) * 512],
                                     x1[:, j * 512:(j + 1) * 512],
                                     k.bbc[:, :], op=ALU.subtract)
            m = k.inc(v, k.vC)
            k.wait(vector, "vector", k.vC, m)
            v = vector.tensor_tensor(x1[:, j * 512:(j + 1) * 512],
                                     x1[:, j * 512:(j + 1) * 512],
                                     k.abc[:, :], op=ALU.mult)
            k.o_marks[j] = k.inc(v, k.vC)
    blk.vector(ve4)

    def syo(sync):
        for j in range(KD):
            k.wait(sync, "sync", k.vC, k.o_marks[j])
            k.inc(sync.dma_start(
                out=k.out.rearrange("(kk p) t -> p kk t", p=128)[
                    :, j, tile * 512:(tile + 1) * 512],
                in_=x1[:, j * 512:(j + 1) * 512]), k.dOut, 16)
    blk.sync(syo)


def kernel(L0_attn_hidden_states, L0_attn_Wq, L0_attn_Wk, L0_attn_Wv,
           L0_attn_Wb, L0_attn_o_norm_weight, L0_attn_Wo,
           L0_ffn_W1, L0_ffn_W2, L0_ffn_W3):
    x = np.ascontiguousarray(L0_attn_hidden_states, dtype=np.float32)
    consts = host_consts()
    nc = build_nc()

    w1t = np.ascontiguousarray(L0_ffn_W1.T)
    w3t = np.ascontiguousarray(L0_ffn_W3.T)
    w2t = np.ascontiguousarray(L0_ffn_W2.T)

    in_maps = []
    for c in range(8):
        b, hg = c // 4, c % 4
        xb_ = x[b]
        xt = np.ascontiguousarray(xb_.T)
        slots = np.concatenate([xb_[512 * hg:512 * (hg + 1)],
                                xb_[2048 + 512 * hg:2048 + 512 * (hg + 1)]])
        xft = np.ascontiguousarray(slots.T)
        m = {
            "xt": xt, "xft": xft,
            "wq": np.ascontiguousarray(L0_attn_Wq[:, 512 * hg:512 * (hg + 1)]),
            "wk": np.ascontiguousarray(L0_attn_Wk[:, 512 * hg:512 * (hg + 1)]),
            "wv": np.ascontiguousarray(L0_attn_Wv[:, 512 * hg:512 * (hg + 1)]),
            "wb4": np.ascontiguousarray(L0_attn_Wb[:, 4 * hg:4 * (hg + 1)]),
            "gam": np.ascontiguousarray(
                L0_attn_o_norm_weight.reshape(128, 1)),
            "woP": np.ascontiguousarray(L0_attn_Wo[512 * hg:512 * (hg + 1)]),
            "w1S": np.ascontiguousarray(w1t[:, 1024 * c:1024 * (c + 1)]),
            "w3S": np.ascontiguousarray(w3t[:, 1024 * c:1024 * (c + 1)]),
            "w2S": np.ascontiguousarray(w2t[:, 256 * c:256 * (c + 1)]),
        }
        m.update(consts)
        in_maps.append(m)

    res = run_bass_kernel_spmd(nc, in_maps, core_ids=list(range(8)))

    out = np.empty((B, L, Dm), np.float32)
    for c in range(8):
        b, hg = c // 4, c % 4
        o = np.asarray(res.results[c]["out"])
        o = o.reshape(KD, 128, 2, 512)
        t0 = o[:, :, 0].reshape(Dm, 512).T
        t1 = o[:, :, 1].reshape(Dm, 512).T
        out[b, 512 * hg:512 * (hg + 1)] = t0
        out[b, 2048 + 512 * hg:2048 + 512 * (hg + 1)] = t1
    return out


# version 06
# speedup vs baseline: 1.0601x; 1.0601x over previous
import numpy as np
import ml_dtypes
from contextlib import ExitStack

import concourse.bass as bass
import concourse.mybir as mybir
from concourse.bass_utils import run_bass_kernel_spmd

F32 = mybir.dt.float32
BF16 = mybir.dt.bfloat16
AF = mybir.ActivationFunctionType
ALU = mybir.AluOpType
BF = ml_dtypes.bfloat16

B, L, Dm = 2, 4096, 2048
NH, DH = 16, 128
HID = 8192
C = 128
NCH = L // C
NP = 4
KD = Dm // 128
KH = HID // 128
NLEV = 7
EPS = 1e-5
RSH = float(1.0 / np.sqrt(HID))
DHS = float(DH ** -0.5)
TAPS = False


class Ctr:
    __slots__ = ("sem", "name", "n")

    def __init__(self, sem, name):
        self.sem, self.name, self.n = sem, name, 0


class KB:
    def __init__(self):
        self.wcache = {x: {} for x in
                       ("sync", "vector", "scalar", "tensor", "gpsimd")}
        self.rel = {}

    def wait(self, eng, engname, ctr, val):
        if val <= 0:
            return
        cache = self.wcache[engname]
        if cache.get(ctr.name, 0) >= val:
            return
        eng.wait_ge(ctr.sem, val)
        cache[ctr.name] = val

    def waitrel(self, eng, engname, key):
        r = self.rel.get(key)
        if r:
            self.wait(eng, engname, r[0], r[1])

    @staticmethod
    def inc(instr, ctr, kk=1):
        instr.then_inc(ctr.sem, kk)
        ctr.n += kk
        return ctr.n


class RecBlk:

    def __init__(self):
        self.secs = []

    def tensor(self, fn):
        self.secs.append(("tensor", fn))

    def vector(self, fn):
        self.secs.append(("vector", fn))

    def scalar(self, fn):
        self.secs.append(("scalar", fn))

    def sync(self, fn):
        self.secs.append(("sync", fn))

    def gpsimd(self, fn):
        self.secs.append(("gpsimd", fn))


def record_secs(k, emit_fn, ci):
    real = k.block
    rec = RecBlk()
    k.block = rec
    try:
        emit_fn(k, ci)
    finally:
        k.block = real
    return rec.secs


def play_merged(k, r1, r2):
    n1, n2 = len(r1), len(r2)
    i = j = 0
    while i < n1 or j < n2:
        if j >= n2 or (i < n1 and i * n2 <= j * n1):
            eng, fn = r1[i]
            i += 1
        else:
            eng, fn = r2[j]
            j += 1
        getattr(k.block, eng)(fn)


def host_consts():
    i = np.arange(128)
    h128 = np.where((np.bitwise_count(i[:, None] & i[None, :]) % 2) == 0,
                    1.0, -1.0)
    g = np.arange(64)
    h64 = np.where((np.bitwise_count(g[:, None] & g[None, :]) % 2) == 0,
                   1.0, -1.0)
    i2h64 = np.kron(np.eye(2), h64)
    mask_s = -np.triu(np.ones((128, 128), np.float32), 1)
    mask_i = np.triu(np.ones((128, 128), np.float32), 0)
    return {
        "identb": np.eye(128).astype(BF),
        "h128b": h128.astype(BF),
        "i2h64b": i2h64.astype(BF),
        "masku_s4": np.tile(mask_s, (1, 4)).astype(np.float32),
        "masku_i4": np.tile(mask_i, (1, 4)).astype(np.float32),
        "onescol_b": np.ones((128, 1), BF),
        "onesrow_f": np.ones((1, 128), np.float32),
    }


def build_nc():
    nc = bass.Bass("TRN2", target_bir_lowering=False)
    k = KB()
    k.nc = nc

    di = lambda n, s, dt: nc.dram_tensor(n, s, dt, kind="ExternalInput").ap()
    do = lambda n, s, dt: nc.dram_tensor(n, s, dt, kind="ExternalOutput").ap()
    dd = lambda n, s, dt: nc.dram_tensor(n, s, dt).ap()

    k.xt = di("xt", [Dm, L], F32)
    k.xft = di("xft", [Dm, 1024], F32)
    k.wq = di("wq", [Dm, 512], F32)
    k.wk = di("wk", [Dm, 512], F32)
    k.wv = di("wv", [Dm, 512], F32)
    k.wb4 = di("wb4", [Dm, 4], F32)
    k.gam = di("gam", [128, 1], F32)
    k.woP = di("woP", [512, Dm], F32)
    k.w1S = di("w1S", [Dm, 1024], F32)
    k.w3S = di("w3S", [Dm, 1024], F32)
    k.w2S = di("w2S", [HID, 256], F32)
    k.identb = di("identb", [128, 128], BF16)
    k.h128b = di("h128b", [128, 128], BF16)
    k.i2h64b = di("i2h64b", [128, 128], BF16)
    k.masku_s4 = di("masku_s4", [128, 512], F32)
    k.masku_i4 = di("masku_i4", [128, 512], F32)
    k.onescol_b = di("onescol_b", [128, 1], BF16)
    k.onesrow_f = di("onesrow_f", [1, 128], F32)

    k.out = do("out", [Dm, 1024], F32)
    if TAPS:
        k.tap_o = do("tap_o", [512, L], BF16)

    k.w1b_in = dd("w1b_in", [Dm, 1024], BF16)
    k.w1b_out = dd("w1b_out", [8 * Dm, 1024], BF16)
    k.w3b_in = dd("w3b_in", [Dm, 1024], BF16)
    k.w3b_out = dd("w3b_out", [8 * Dm, 1024], BF16)
    k.w2bA = dd("w2bA", [HID, 256], BF16)
    k.w2p_in = dd("w2p_in", [HID, 256], BF16)
    k.w2p_out = dd("w2p_out", [8 * HID, 256], BF16)
    k.rs_in = [dd(f"rs_in{h}", [4 * Dm, 512], F32) for h in range(2)]
    k.rs_out = [dd(f"rs_out{h}", [Dm, 512], F32) for h in range(2)]
    k.dacc = dd("dacc", [Dm, 1024], F32)

    with ExitStack() as ctx:
        E = ctx.enter_context
        sb = lambda n, s, dt: E(nc.sbuf_tensor(n, s, dt))

        k.c_identb = sb("c_identb", [128, 128], BF16)
        k.c_h128b = sb("c_h128b", [128, 128], BF16)
        k.c_i2h64b = sb("c_i2h64b", [128, 128], BF16)
        k.c_masku_s4 = sb("c_masku_s4", [128, 512], F32)
        k.c_masku_i4 = sb("c_masku_i4", [128, 512], F32)
        k.c_onescol_b = sb("c_onescol_b", [128, 1], BF16)
        k.c_onesrow_f = sb("c_onesrow_f", [1, 128], F32)
        k.c_gam = sb("c_gam", [128, 1], F32)
        k.c_eps = sb("c_eps", [128, 1], F32)
        k.c_nhalf = sb("c_nhalf", [128, 1], F32)
        k.c_i128 = sb("c_i128", [128, 1], F32)
        k.c_i2048 = sb("c_i2048", [128, 1], F32)

        k.pquad = E(nc.psum_tensor("pquad", [128, 2048], F32))
        k.pa = E(nc.psum_tensor("pa", [128, 512], F32))
        k.pb = E(nc.psum_tensor("pb", [128, 512], F32))
        k.ptr = E(nc.psum_tensor("ptr", [128, 512], BF16))
        k.pst = E(nc.psum_tensor("pst", [128, 512], F32))

        names = ["dC", "dOut", "daR", "vC", "aC", "pC",
                 "ccW1", "ccW3", "ccW2", "cc0", "cc1"]
        for base in ("sq", "sx", "bg", "wc", "we", "rs", "ws", "w2", "xr"):
            names += [base + "0", base + "1"]
        names += ["dw0", "dw1", "dw2", "dw3", "ws2", "ws3", "w22", "w23"]
        for nm in names:
            setattr(k, nm, Ctr(E(nc.semaphore(nm)), nm))
        k.sq = [k.sq0, k.sq1]
        k.sx = [k.sx0, k.sx1]
        k.bgp = [k.bg0, k.bg1]
        k.wcp = [k.wc0, k.wc1]
        k.wep = [k.we0, k.we1]
        k.rsp = [k.rs0, k.rs1]
        k.wsp = [k.ws0, k.ws1, k.ws2, k.ws3]
        k.w2p_ = [k.w20, k.w21, k.w22, k.w23]
        k.xrp = [k.xr0, k.xr1]
        k.dwp = [k.dw0, k.dw1, k.dw2, k.dw3]

        k.block = E(nc.Block())

        with ExitStack() as bc_ctx:
            Eb = bc_ctx.enter_context
            sbb = lambda n, s, dt: Eb(nc.sbuf_tensor(n, s, dt))
            k.stgw = [sbb(f"stgw{i}", [128, 1024], F32) for i in range(2)]
            k.stgx = [sbb(f"stgx{i}", [128, 1024], F32) for i in range(2)]
            k.w2cs = [sbb(f"w2cs{i}", [128, 256], BF16) for i in range(2)]
            k.w2ev = [sbb(f"w2ev{i}", [128, 256], BF16) for i in range(2)]
            k.bgev = [sbb(f"bgev{i}", [128, 1024], BF16) for i in range(2)]
            k.wqb = sbb("wqb", [128, KD * 512], BF16)
            k.wkb = sbb("wkb", [128, KD * 512], BF16)
            k.wvb = sbb("wvb", [128, KD * 512], BF16)
            k.wbb = sbb("wbb", [128, KD * 4], BF16)
            k.wopb = sbb("wopb", [128, NP * Dm], BF16)
            k.xb = [sbb(f"xb{i}", [128, KD * 512], BF16) for i in range(2)]
            k.ktm = [sbb(f"ktm{i}", [128, 4 * 512], BF16) for i in range(2)]
            k.qtm = [sbb(f"qtm{i}", [128, 4 * 512], BF16) for i in range(2)]
            k.vtm = [sbb(f"vtm{i}", [128, 4 * 512], BF16) for i in range(2)]
            k.btf = [sbb(f"btf{i}", [128, 4 * 4], F32) for i in range(2)]
            k.qf = sbb("qf", [128, 512], F32)
            k.kf = sbb("kf", [128, 512], F32)
            k.qsq = sbb("qsq", [128, 512], F32)
            k.qss = sbb("qss", [128, 8], F32)
            k.s32 = sbb("s32", [128, NP * 128], F32)
            k.rksq = sbb("rksq", [128, NP * 384], BF16)
            k.u4 = sbb("u4", [128, NP * 128], BF16)
            k.gm4 = sbb("gm4", [128, NP * 128], BF16)
            k.dgn4 = sbb("dgn4", [128, NP * 128], BF16)
            k.xp = [sbb(f"xp{j}", [128, NP * 384], BF16) for j in range(2)]
            k.tvring = [sbb(f"tvr{i}", [128, NP * 128], BF16) for i in range(4)]
            k.tkring = [sbb(f"tkr{i}", [128, NP * 128], BF16) for i in range(4)]
            k.gmring = [sbb(f"gmr{i}", [128, NP * 128], BF16) for i in range(4)]
            k.qring = [sbb(f"qr{i}", [128, NP * 128], BF16) for i in range(4)]
            k.pt = [sbb(f"pt{j}", [128, NP * 128], BF16) for j in range(2)]
            k.dl4 = sbb("dl4", [128, NP * 128], BF16)
            k.osq = sbb("osq", [128, NP * 128], F32)
            k.oss = sbb("oss", [128, NP], F32)
            k.orstd = sbb("orstd", [128, NP], F32)
            k.onb = sbb("onb", [128, NP * 128], BF16)
            k.obuf = [sbb(f"obuf{i}", [128, NP * 512], BF16) for i in range(2)]
            k.rsstg = [sbb(f"rsstg{i}", [128, 512], F32) for i in range(2)]

            k.qk_done_c = {}
            k.v_done_c = {}
            k.rec1_done = {}
            k.e_sbf = 0
            emit_phase_a(k)
            for ci in range(NCH + 3):
                if ci < NCH:
                    if ci % 4 == 0 and ci // 4 + 1 < 8:
                        emit_x_slab(k, ci // 4 + 1)
                    emit_proj(k, ci)
                r1 = (record_secs(k, emit_rec1, ci - 1)
                      if 1 <= ci < NCH + 1 else [])
                r2 = record_secs(k, emit_rec2, ci - 3) if ci >= 3 else []
                play_merged(k, r1, r2)
                if ci >= 3:
                    if (ci - 3) % 4 == 3:
                        emit_wo_group(k, (ci - 3) // 4)
                if ci < NCH:
                    emit_bg(k, ci)
            emit_collectives(k)
            emit_barrier(k)

        with ExitStack() as d_ctx:
            Ed = d_ctx.enter_context
            sbd = lambda n, s, dt: Ed(nc.sbuf_tensor(n, s, dt))
            k.x1 = [sbd(f"x1_{t}", [128, KD * 512], F32) for t in range(2)]
            k.x1b = [sbd(f"x1b_{t}", [128, KD * 512], BF16) for t in range(2)]
            k.hid = sbd("hid", [128, 16 * 1024], BF16)
            k.wsl = [sbd(f"wsl{i}", [128, 16 * 128], BF16) for i in range(6)]
            k.w2sl = [sbd(f"w2sl{i}", [128, 2048], BF16) for i in range(3)]
            k.xrs = [sbd(f"xrs{i}", [128, 512], F32) for i in range(2)]
            k.dar = [sbd(f"dar{i}", [128, 512], F32) for i in range(4)]
            k.daw = [sbd(f"daw{i}", [128, 512], F32) for i in range(4)]
            k.preb = [sbd(f"preb{i}", [128, 512], BF16) for i in range(2)]
            k.sqb = [sbd(f"sqb{i}", [128, 512], BF16) for i in range(2)]
            k.stat = sbd("stat", [1, 1024], F32)
            k.lnt = sbd("lnt", [1, 512], F32)
            k.abc = sbd("abc", [128, 512], F32)
            k.bbc = sbd("bbc", [128, 512], F32)
            k.gsl = sbd("gsl", [128, 1024], BF16)

            for tile in range(2):
                emit_ln1(k, tile)
            for quarter in range(4):
                emit_ffn_quarter(k, quarter)
            for tile in range(2):
                emit_ln2_out(k, tile)

        def sync_end(sync):
            k.wait(sync, "sync", k.dOut, k.dOut.n)
        k.block.sync(sync_end)

        def vec_end(vector):
            k.wait(vector, "vector", k.dOut, k.dOut.n)
        k.block.vector(vec_end)
    return nc


def emit_phase_a(k):
    blk = k.block

    def sync_consts(sync):
        d = lambda o, i: k.inc(sync.dma_start(out=o, in_=i), k.dC, 16)
        d(k.c_identb[:, :], k.identb)
        d(k.c_h128b[:, :], k.h128b)
        d(k.c_i2h64b[:, :], k.i2h64b)
        d(k.c_masku_s4[:, :], k.masku_s4)
        d(k.c_masku_i4[:, :], k.masku_i4)
        d(k.c_onescol_b[:, :], k.onescol_b)
        d(k.c_onesrow_f[:, :], k.onesrow_f)
        d(k.c_gam[:, :], k.gam)
    blk.sync(sync_consts)

    def gp_consts(gpsimd):
        gpsimd.memset(k.c_eps[:, :], EPS)
        gpsimd.memset(k.c_nhalf[:, :], -0.5)
        gpsimd.memset(k.c_i128[:, :], 1.0 / 128.0)
        m = gpsimd.memset(k.c_i2048[:, :], 1.0 / 2048.0)
        k.inc(m, k.dC, 16)
    blk.gpsimd(gp_consts)
    k.CONSTS = k.dC.n

    def vec_zero(vector):
        k.inc(vector.memset(k.s32[:, :], 0.0), k.vC)
        z = k.rksq[:, :].rearrange("p (f c) -> p f c", f=NP)[:, :, 256:384]
        k.inc(vector.memset(z, 0.0), k.vC)
    blk.vector(vec_zero)
    k.SZERO = k.vC.n

    jobs = []
    for (src, dst, width) in ((k.wq, "wqb", 512), (k.wk, "wkb", 512),
                              (k.wv, "wvb", 512), (k.wb4, "wbb", 4)):
        for kk in range(KD):
            jobs.append((src, dst, width, kk, kk))
    for kk in range(NP):
        jobs.append((k.woP, "wopb", 1024, 2 * kk, kk))
        jobs.append((k.woP, "wopb", 1024, 2 * kk + 1, kk))
    k.qw_marks = {}

    qw_marks = {}
    VBASE = k.vC.n

    def scalar_qw(scalar):
        for t, (src, dst, width, sl, kk) in enumerate(jobs):
            if t >= 2:
                k.wait(scalar, "scalar", k.vC, VBASE + t - 1)
            if dst == "wopb":
                half = sl % 2
                src_ap = src[kk * 128:(kk + 1) * 128,
                             half * 1024:(half + 1) * 1024]
            else:
                src_ap = src[sl * 128:(sl + 1) * 128, :]
            qw_marks[t] = k.inc(
                scalar.dma_start(out=k.stgw[t % 2][:, 0:width], in_=src_ap),
                k.sq[t % 2], 16)
    blk.scalar(scalar_qw)

    def vec_qw(vector):
        for t, (src, dst, width, sl, kk) in enumerate(jobs):
            k.wait(vector, "vector", k.sq[t % 2], qw_marks[t])
            dt_ = getattr(k, dst)
            if dst == "wopb":
                half = sl % 2
                dv = dt_[:, kk * 2048 + half * 1024: kk * 2048 + (half + 1) * 1024]
            else:
                dv = dt_[:, sl * width:(sl + 1) * width]
            m = k.inc(vector.tensor_copy(dv, k.stgw[t % 2][:, 0:width]), k.vC)
            k.rel[("stgw", t % 2)] = (k.vC, m)
    blk.vector(vec_qw)
    k.QW_RDY = k.vC.n

    k.xcast_marks = {}
    k.xslab_done = {}
    k.proj_done = {}
    emit_x_slab(k, 0)

    k.bg = ([("w1", i) for i in range(16)] + [("w3", i) for i in range(16)] +
            [("w2a", i) for i in range(64)] + [("w2b", i) for i in range(64)])
    k.bg_i = 0


def emit_x_slab(k, a):
    blk = k.block
    base_t = a * 8
    xmarks = {}
    VBASE = k.vC.n

    def sync_fn(sync):
        for i in range(8):
            t = base_t + i
            if i >= 2:
                k.wait(sync, "sync", k.vC, VBASE + (i - 2) + 1)
            elif a >= 1:
                k.wait(sync, "sync", k.vC, k.xcast_marks[base_t - 2 + i])
            src = k.xt[:, a * 512:(a + 1) * 512].rearrange(
                "(kk p) t -> p kk t", p=128)[:, 2 * i:2 * i + 2, :]
            dst = k.stgx[t % 2][:, :].rearrange("p (a t) -> p a t", a=2)
            xmarks[t] = k.inc(sync.dma_start(out=dst, in_=src),
                              k.sx[t % 2], 16)

    def vec_fn(vector):
        for i in range(8):
            t = base_t + i
            if i == 0 and a >= 2:
                k.wait(vector, "vector", k.pC, k.proj_done[a - 2])
            k.wait(vector, "vector", k.sx[t % 2], xmarks[t])
            m = k.inc(vector.tensor_copy(
                k.xb[a % 2][:, 2 * i * 512:(2 * i + 2) * 512],
                k.stgx[t % 2][:, :]), k.vC)
            assert m == VBASE + i + 1, "xcast mark prediction broken"
            k.xcast_marks[t] = m
    blk.sync(sync_fn)
    blk.vector(vec_fn)
    k.xslab_done[a] = k.vC.n


def emit_proj(k, ci):
    blk = k.block
    a = ci // 4
    wv = a % 2
    cc = ci % 4
    slab = k.xb[wv]

    def pe(tensor):
        k.wait(tensor, "tensor", k.dC, k.CONSTS)
        k.wait(tensor, "tensor", k.vC, k.xslab_done[a])
        k.wait(tensor, "tensor", k.vC, k.QW_RDY)
        k.wait(tensor, "tensor", k.vC, k.SZERO)
        marks = {}
        for (wname, bank) in (("wqb", 0), ("wkb", 1), ("wvb", 2),
                              ("wbb", 3)):
            k.waitrel(tensor, "tensor", ("bank", bank))
            w = getattr(k, wname)
            width = 4 if wname == "wbb" else 512
            outp = k.pquad[:, bank * 512: bank * 512 + width]
            for kk in range(KD):
                mm = tensor.matmul(
                    outp,
                    slab[:, kk * 512 + cc * 128: kk * 512 + cc * 128 + 128],
                    w[:, kk * width:(kk + 1) * width],
                    start=(kk == 0), stop=(kk == KD - 1))
            marks[wname] = k.inc(mm, k.pC)
        k.pj = marks
    blk.tensor(pe)

    def vec(vector):
        k.wait(vector, "vector", k.dC, k.CONSTS)
        k.wait(vector, "vector", k.pC, k.pj["wqb"])
        e_q = k.inc(vector.tensor_copy(k.qf[:, :], k.pquad[:, 0:512]), k.vC)
        k.wait(vector, "vector", k.pC, k.pj["wkb"])
        e_k = k.inc(vector.tensor_copy(k.kf[:, :], k.pquad[:, 512:1024]),
                    k.vC)
        k.rel[("bank", 0)] = (k.vC, e_q)
        k.rel[("bank", 1)] = (k.vC, e_k)
        k.ev_qk = (e_q, e_k)
    blk.vector(vec)

    def sc(scalar):
        e_q, e_k = k.ev_qk
        k.wait(scalar, "scalar", k.dC, k.CONSTS)
        k.wait(scalar, "scalar", k.aC, k.aC.n)
        k.wait(scalar, "scalar", k.vC, e_q)
        for h in range(4):
            aa = scalar.activation(k.qsq[:, h * 128:(h + 1) * 128],
                                   k.qf[:, h * 128:(h + 1) * 128], AF.Square,
                                   accum_out=k.qss[:, h:h + 1])
        mq = k.inc(aa, k.aC)
        k.wait(scalar, "scalar", k.aC, mq)
        k.wait(scalar, "scalar", k.vC, e_k)
        for h in range(4):
            aa = scalar.activation(k.qsq[:, h * 128:(h + 1) * 128],
                                   k.kf[:, h * 128:(h + 1) * 128], AF.Square,
                                   accum_out=k.qss[:, 4 + h:5 + h])
        k.inc(aa, k.aC)
        k.wait(scalar, "scalar", k.pC, k.pj["wvb"])
        ev = k.inc(scalar.activation(k.vtm[wv][:, cc * 512:(cc + 1) * 512],
                                     k.pquad[:, 1024:1536], AF.Silu), k.aC)
        k.wait(scalar, "scalar", k.pC, k.pj["wbb"])
        eb = k.inc(scalar.activation(k.btf[wv][:, cc * 4:(cc + 1) * 4],
                                     k.pquad[:, 1536:1540], AF.Sigmoid),
                   k.aC)
        k.rel[("bank", 2)] = (k.aC, ev)
        k.rel[("bank", 3)] = (k.aC, eb)
        k.wait(scalar, "scalar", k.aC, k.aC.n)
        aa = scalar.activation(k.qss[:, 0:8], k.qss[:, 0:8], AF.Sqrt)
        k.sq_done = k.inc(aa, k.aC)
        k.v_done = ev
        k.b_done = eb
    blk.scalar(sc)

    def vec2(vector):
        k.wait(vector, "vector", k.aC, k.sq_done)
        v = vector.tensor_single_scalar(k.qss[:, 0:8], k.qss[:, 0:8], 1e-12,
                                        ALU.max)
        m = k.inc(v, k.vC)
        k.wait(vector, "vector", k.vC, m)
        v = vector.reciprocal(k.qss[:, 0:8], k.qss[:, 0:8])
        m = k.inc(v, k.vC)
        k.wait(vector, "vector", k.vC, m)
        for h in range(4):
            v = vector.tensor_scalar(
                k.qtm[wv][:, cc * 512 + h * 128: cc * 512 + (h + 1) * 128],
                k.qf[:, h * 128:(h + 1) * 128],
                k.qss[:, h:h + 1], DHS, op0=ALU.mult, op1=ALU.mult)
            k.inc(v, k.vC)
        for h in range(4):
            v = vector.tensor_scalar(
                k.ktm[wv][:, cc * 512 + h * 128: cc * 512 + (h + 1) * 128],
                k.kf[:, h * 128:(h + 1) * 128],
                k.qss[:, 4 + h:5 + h], None, op0=ALU.mult)
            k.inc(v, k.vC)
    blk.vector(vec2)
    k.qk_done_c[ci] = k.vC.n
    k.v_done_c[ci] = k.v_done
    if ci % 4 == 3:
        k.proj_done[a] = k.pC.n


def emit_rec1(k, ci):
    blk = k.block
    wv = (ci // 4) % 2
    cc = ci % 4
    ring = ci % 4
    ks_ = lambda p: k.ktm[wv][:, cc * 512 + p * 128: cc * 512 + (p + 1) * 128]
    qs_ = lambda p: k.qtm[wv][:, cc * 512 + p * 128: cc * 512 + (p + 1) * 128]
    rk = lambda p, o, w: k.rksq[:, p * 384 + o: p * 384 + o + w]
    pqv = lambda o, w: k.pquad[:, :].rearrange(
        "p (f c) -> p f c", f=4)[:, :, o:o + w]
    r4 = lambda t: t[:, :].rearrange("p (f c) -> p f c", f=NP)
    rks4 = lambda o, w: k.rksq[:, :].rearrange(
        "p (f c) -> p f c", f=NP)[:, :, o:o + w]
    xpv = lambda j, o, w: k.xp[j][:, :].rearrange(
        "p (f c) -> p f c", f=NP)[:, :, o:o + w]
    bview = k.btf[wv][:, cc * 4: cc * 4 + 4].unsqueeze(2).broadcast_to(
        [128, 4, 128])
    identv = k.c_identb[:, :].unsqueeze(1).broadcast_to([128, 4, 128])

    def pe1(tensor):
        k.wait(tensor, "tensor", k.dC, k.CONSTS)
        k.wait(tensor, "tensor", k.vC, k.qk_done_c[ci])
        k.wait(tensor, "tensor", k.aC, k.v_done_c[ci])
        k.waitrel(tensor, "tensor", "ptr")
        for p in range(NP):
            mm = tensor.transpose(k.ptr[:, p * 128:(p + 1) * 128], ks_(p),
                                  k.c_identb[:, :])
        k.m_trk = k.inc(mm, k.pC)
    blk.tensor(pe1)

    def ve1(vector):
        k.wait(vector, "vector", k.pC, k.m_trk)
        k.waitrel(vector, "vector", "rksq_kq")
        v = vector.tensor_copy(rks4(0, 128), r4(k.ptr))
        k.e_kfm = k.inc(v, k.vC)
    blk.vector(ve1)

    def pe2(tensor):
        k.wait(tensor, "tensor", k.vC, k.e_kfm)
        for p in range(NP):
            mm = tensor.transpose(k.ptr[:, p * 128:(p + 1) * 128], qs_(p),
                                  k.c_identb[:, :])
        k.m_trq = k.inc(mm, k.pC)
    blk.tensor(pe2)

    def sc1(scalar):
        k.wait(scalar, "scalar", k.pC, k.m_trq)
        k.waitrel(scalar, "scalar", ("qring", ring))
        aa = scalar.copy(r4(k.qring[ring]), r4(k.ptr))
        k.e_qfm = k.inc(aa, k.aC)
        k.rel["ptr"] = (k.aC, k.e_qfm)
    blk.scalar(sc1)

    def ve1b(vector):
        k.wait(vector, "vector", k.aC, k.e_qfm)
        v = vector.tensor_copy(rks4(128, 128), r4(k.qring[ring]))
        k.e_qfm2 = k.inc(v, k.vC)
    blk.vector(ve1b)

    def pe3(tensor):
        k.wait(tensor, "tensor", k.vC, k.e_qfm2)
        for bank in range(4):
            k.waitrel(tensor, "tensor", ("bank", bank))
        for p in range(NP):
            mm = tensor.matmul(k.pquad[:, p * 512: p * 512 + 256],
                               rk(p, 0, 128), rk(p, 0, 256),
                               start=True, stop=True)
        k.m_ags = k.inc(mm, k.pC)
        k.rel["rksq_kq"] = (k.pC, k.m_ags)
    blk.tensor(pe3)

    def ve2(vector):
        k.wait(vector, "vector", k.pC, k.m_ags)
        mss = k.c_masku_s4[:, :].rearrange("p (f c) -> p f c", f=4)
        msi = k.c_masku_i4[:, :].rearrange("p (f c) -> p f c", f=4)
        k.inc(vector.tensor_tensor(r4(k.u4), pqv(0, 128), mss, op=ALU.mult),
              k.vC)
        k.waitrel(vector, "vector", ("gmring", ring))
        k.inc(vector.tensor_tensor(r4(k.gmring[ring]), pqv(128, 128), msi,
                                   op=ALU.mult), k.vC)
        vt = k.vtm[wv][:, cc * 512:(cc + 1) * 512].rearrange(
            "p (f c) -> p f c", f=4)
        kt = k.ktm[wv][:, cc * 512:(cc + 1) * 512].rearrange(
            "p (f c) -> p f c", f=4)
        k.inc(vector.tensor_copy(xpv(0, 0, 128), vt), k.vC)
        k.inc(vector.tensor_copy(xpv(0, 128, 128), kt), k.vC)
        k.wait(vector, "vector", k.vC, k.vC.n)
        k.inc(vector.tensor_tensor(r4(k.pt[0]), r4(k.u4), bview,
                                   op=ALU.mult), k.vC)
        v = vector.tensor_tensor(r4(k.dgn4), identv, bview, op=ALU.mult)
        k.e_rec0 = k.inc(v, k.vC)
    blk.vector(ve2)

    def pe4(tensor):
        k.wait(tensor, "tensor", k.vC, k.e_rec0)
        for p in range(NP):
            mm = tensor.matmul(k.pquad[:, p * 512: p * 512 + 128],
                               k.u4[:, p * 128:(p + 1) * 128],
                               k.dgn4[:, p * 128:(p + 1) * 128],
                               start=True, stop=True)
        k.m_p0 = k.inc(mm, k.pC)
    blk.tensor(pe4)

    def sc2(scalar):
        k.wait(scalar, "scalar", k.pC, k.m_p0)
        aa = scalar.copy(xpv(0, 256, 128), pqv(0, 128))
        k.e_p0 = k.inc(aa, k.aC)
    blk.scalar(sc2)

    for j in range(NLEV):
        cur, nxt = j % 2, 1 - j % 2

        def pej(tensor, j=j, cur=cur):
            if j == 0:
                k.wait(tensor, "tensor", k.aC, k.e_p0)
                k.wait(tensor, "tensor", k.vC, k.e_rec0)
            else:
                k.wait(tensor, "tensor", k.vC, k.e_ptcopy)
            w = 384 if j < NLEV - 1 else 256
            for p in range(NP):
                mm = tensor.matmul(k.pquad[:, p * 512: p * 512 + w],
                                   k.pt[cur][:, p * 128:(p + 1) * 128],
                                   k.xp[cur][:, p * 384: p * 384 + w],
                                   start=True, stop=True)
            k.m_mmx = k.inc(mm, k.pC)
            if j < NLEV - 1:
                for p in range(NP):
                    mm = tensor.matmul(
                        k.pquad[:, p * 512 + 384: p * 512 + 512],
                        k.xp[cur][:, p * 384 + 256: p * 384 + 384],
                        k.pt[cur][:, p * 128:(p + 1) * 128],
                        start=True, stop=True)
                k.m_mmpt = k.inc(mm, k.pC)
        blk.tensor(pej)

        def vej(vector, j=j, cur=cur, nxt=nxt):
            k.wait(vector, "vector", k.pC,
                   k.m_mmpt if j < NLEV - 1 else k.m_mmx)
            v = vector.tensor_tensor(xpv(nxt, 0, 256), pqv(0, 256),
                                     xpv(cur, 0, 256), op=ALU.add)
            k.e_xadd = k.inc(v, k.vC)
            if j < NLEV - 1:
                v = vector.tensor_copy(xpv(nxt, 256, 128), pqv(256, 128))
                k.e_pcopy = k.inc(v, k.vC)
                v = vector.tensor_copy(r4(k.pt[nxt]), pqv(384, 128))
                k.e_ptcopy = k.inc(v, k.vC)
        blk.vector(vej)

    fin = NLEV % 2

    def ve3(vector):
        k.wait(vector, "vector", k.vC, k.e_xadd)
        k.waitrel(vector, "vector", ("tvring", ring))
        v = vector.tensor_copy(r4(k.tvring[ring]), xpv(fin, 0, 128))
        k.e_tv = k.inc(v, k.vC)
    blk.vector(ve3)

    def pe5(tensor):
        k.wait(tensor, "tensor", k.vC, k.e_xadd)
        k.waitrel(tensor, "tensor", "ptr")
        for p in range(NP):
            mm = tensor.transpose(k.ptr[:, p * 128:(p + 1) * 128],
                                  k.xp[fin][:, p * 384 + 128: p * 384 + 256],
                                  k.c_identb[:, :])
        k.m_trtk = k.inc(mm, k.pC)
    blk.tensor(pe5)

    def ve4(vector):
        k.wait(vector, "vector", k.pC, k.m_trtk)
        k.waitrel(vector, "vector", ("tkring", ring))
        v = vector.tensor_copy(r4(k.tkring[ring]), r4(k.ptr))
        k.e_tk = k.inc(v, k.vC)
        k.rel["ptr"] = (k.vC, k.e_tk)
        for b in range(4):
            k.rel[("bank", b)] = (k.vC, k.e_xadd)
        k.rec1_done[ci] = (k.e_tv, k.e_tk, k.vC.n, k.aC.n)
    blk.vector(ve4)


def emit_rec2(k, ci):
    blk = k.block
    wv = (ci // 4) % 2
    cc = ci % 4
    ring = ci % 4
    ks_ = lambda p: k.ktm[wv][:, cc * 512 + p * 128: cc * 512 + (p + 1) * 128]
    rks4 = lambda o, w: k.rksq[:, :].rearrange(
        "p (f c) -> p f c", f=NP)[:, :, o:o + w]
    rk = lambda p, o, w: k.rksq[:, p * 384 + o: p * 384 + o + w]
    r4 = lambda t: t[:, :].rearrange("p (f c) -> p f c", f=NP)
    bview = k.btf[wv][:, cc * 4: cc * 4 + 4].unsqueeze(2).broadcast_to(
        [128, 4, 128])

    def pe1(tensor):
        e_tv, e_tk, vfin, afin = k.rec1_done[ci]
        k.wait(tensor, "tensor", k.vC, e_tk)
        k.wait(tensor, "tensor", k.vC, e_tv)
        k.wait(tensor, "tensor", k.aC, k.e_sbf)
        k.waitrel(tensor, "tensor", "pa")
        for p in range(NP):
            mm = tensor.matmul(k.pa[:, p * 128:(p + 1) * 128],
                               k.tkring[ring][:, p * 128:(p + 1) * 128],
                               rk(p, 256, 128),
                               start=True, stop=True)
        k.m_tks = k.inc(mm, k.pC)
    blk.tensor(pe1)

    def ve1(vector):
        k.wait(vector, "vector", k.pC, k.m_tks)
        v = vector.tensor_tensor(r4(k.dl4), r4(k.tvring[ring]),
                                 r4(k.pa), op=ALU.subtract)
        m = k.inc(v, k.vC)
        k.wait(vector, "vector", k.vC, m)
        v = vector.tensor_tensor(r4(k.dl4), r4(k.dl4), bview, op=ALU.mult)
        k.e_dl = k.inc(v, k.vC)
        k.rel[("tvring", ring)] = (k.vC, k.e_dl)
    blk.vector(ve1)

    def pe2(tensor):
        k.wait(tensor, "tensor", k.vC, k.e_dl)
        k.waitrel(tensor, "tensor", "pb")
        for p in range(NP):
            tensor.matmul(k.pb[:, p * 128:(p + 1) * 128],
                          k.qring[ring][:, p * 128:(p + 1) * 128],
                          rk(p, 256, 128), start=True, stop=False)
            mm = tensor.matmul(k.pb[:, p * 128:(p + 1) * 128],
                               k.gmring[ring][:, p * 128:(p + 1) * 128],
                               k.dl4[:, p * 128:(p + 1) * 128],
                               start=False, stop=True)
        k.m_o = k.inc(mm, k.pC)
        k.rel[("qring", ring)] = (k.pC, k.m_o)
        k.rel[("gmring", ring)] = (k.pC, k.m_o)
        for p in range(NP):
            mm = tensor.matmul(k.pa[:, p * 128:(p + 1) * 128],
                               ks_(p), k.dl4[:, p * 128:(p + 1) * 128],
                               start=True, stop=True)
        k.m_s = k.inc(mm, k.pC)
        k.rel[("tkring", ring)] = (k.pC, k.m_s)
    blk.tensor(pe2)

    def ve2(vector):
        k.wait(vector, "vector", k.pC, k.m_s)
        v = vector.tensor_tensor(k.s32[:, :], k.s32[:, :], k.pa[:, :],
                                 op=ALU.add)
        k.e_s32 = k.inc(v, k.vC)
        k.rel["pa"] = (k.vC, k.e_s32)
    blk.vector(ve2)

    def sc3(scalar):
        k.wait(scalar, "scalar", k.vC, k.e_s32)
        aa = scalar.copy(rks4(256, 128), r4(k.s32))
        k.e_sbf = k.inc(aa, k.aC)
        k.wait(scalar, "scalar", k.pC, k.m_o)
        k.wait(scalar, "scalar", k.aC, k.aC.n)
        for p in range(NP):
            aa = scalar.activation(k.osq[:, p * 128:(p + 1) * 128],
                                   k.pb[:, p * 128:(p + 1) * 128], AF.Square,
                                   accum_out=k.oss[:, p:p + 1])
        k.inc(aa, k.aC)
        k.wait(scalar, "scalar", k.aC, k.aC.n)
        aa = scalar.activation(k.orstd[:, 0:NP], k.oss[:, 0:NP], AF.Ln,
                               bias=k.c_eps[:, 0:1], scale=k.c_i128[:, 0:1])
        k.inc(aa, k.aC)
        k.wait(scalar, "scalar", k.aC, k.aC.n)
        aa = scalar.activation(k.orstd[:, 0:NP], k.orstd[:, 0:NP], AF.Exp,
                               scale=k.c_nhalf[:, 0:1])
        k.e_orstd = k.inc(aa, k.aC)
        k.rel["rksq_s"] = (k.aC, k.e_sbf)
    blk.scalar(sc3)

    def ve5(vector):
        k.wait(vector, "vector", k.aC, k.e_orstd)
        ov = k.orstd[:, 0:NP].unsqueeze(2).broadcast_to([128, 4, 128])
        v = vector.tensor_tensor(r4(k.onb), r4(k.pb), ov, op=ALU.mult)
        k.e_onb = k.inc(v, k.vC)
        k.rel["pb"] = (k.vC, k.e_onb)
    blk.vector(ve5)

    def pe6(tensor):
        k.wait(tensor, "tensor", k.vC, k.e_onb)
        k.waitrel(tensor, "tensor", "ptr")
        for p in range(NP):
            mm = tensor.transpose(k.ptr[:, p * 128:(p + 1) * 128],
                                  k.onb[:, p * 128:(p + 1) * 128],
                                  k.c_identb[:, :])
        k.m_otr = k.inc(mm, k.pC)
    blk.tensor(pe6)

    def ve6(vector):
        k.wait(vector, "vector", k.pC, k.m_otr)
        ob = k.obuf[(ci // 4) % 2]
        k.waitrel(vector, "vector", ("obuf", (ci // 4) % 2))
        dst = ob[:, :].rearrange("p (f c) -> p f c", f=4)[
            :, :, cc * 128:(cc + 1) * 128]
        v = vector.tensor_scalar(dst, r4(k.ptr), k.c_gam[:, 0:1], None,
                                 op0=ALU.mult)
        k.e_obuf = k.inc(v, k.vC)
        k.obuf_done = k.vC.n
        k.rel["ptr"] = (k.vC, k.e_obuf)
    blk.vector(ve6)


def emit_wo_group(k, gi):
    blk = k.block
    half = gi // 4
    slot = gi % 4
    ob = k.obuf[gi % 2]
    obuf_mark = k.obuf_done

    for j in range(KD):
        bname = "pa" if j % 2 == 0 else "pb"

        def pe(tensor, j=j, bname=bname):
            if j == 0:
                k.wait(tensor, "tensor", k.vC, obuf_mark)
            k.waitrel(tensor, "tensor", bname)
            bank = getattr(k, bname)
            for p in range(NP):
                mm = tensor.matmul(bank[:, :],
                                   k.wopb[:, p * 2048 + j * 128:
                                          p * 2048 + (j + 1) * 128],
                                   ob[:, p * 512:(p + 1) * 512],
                                   start=(p == 0), stop=(p == NP - 1))
            k.m_wog = k.inc(mm, k.pC)
        blk.tensor(pe)

        def ve(vector, j=j, bname=bname):
            k.wait(vector, "vector", k.pC, k.m_wog)
            k.waitrel(vector, "vector", ("rsstg", j % 2))
            v = vector.tensor_copy(k.rsstg[j % 2][:, :],
                                   getattr(k, bname)[:, :])
            k.m_rss = k.inc(v, k.vC)
            k.rel[bname] = (k.vC, k.m_rss)
        blk.vector(ve)

        def sy(sync, j=j):
            k.wait(sync, "sync", k.vC, k.m_rss)
            dst = k.rs_in[half][slot * Dm + j * 128:
                                slot * Dm + (j + 1) * 128, :]
            d = k.inc(sync.dma_start(out=dst, in_=k.rsstg[j % 2][:, :]),
                      k.rsp[j % 2], 16)
            k.rel[("rsstg", j % 2)] = (k.rsp[j % 2], d)
        blk.sync(sy)

    k.rel[("obuf", gi % 2)] = (k.pC, k.pC.n)
    if slot == 3:
        k.rs_stage_mark = getattr(k, "rs_stage_mark", {})
        k.rs_stage_mark[half] = (k.rs0.n, k.rs1.n)


def emit_bg(k, ci):
    n = 4 if ci < 29 else 999
    done = 0
    while k.bg_i < len(k.bg) and done < n:
        kind, i = k.bg[k.bg_i]
        if kind == "w2b" and k.bg_i_w2a_pending(ci):
            break
        if kind in ("w1", "w3"):
            emit_bg_cast(k, kind, i)
        elif kind == "w2a":
            emit_w2a(k, i)
        else:
            emit_w2b(k, i)
        k.bg_i += 1
        done += 1


def _w2a_pending(k, ci):
    return False


KB.bg_i_w2a_pending = lambda self, ci: (
    not hasattr(self, "w2a_final"))


def emit_bg_cast(k, kind, i):
    blk = k.block
    src = k.w1S if kind == "w1" else k.w3S
    dst = k.w1b_in if kind == "w1" else k.w3b_in

    def sfn(scalar):
        k.waitrel(scalar, "scalar", ("stgw", i % 2))
        k.bg_dma = k.inc(scalar.dma_start(
            out=k.stgw[i % 2][:, :],
            in_=src[i * 128:(i + 1) * 128, :]), k.sq[i % 2], 16)
    blk.scalar(sfn)

    def afn(vector):
        k.wait(vector, "vector", k.sq[i % 2], k.bg_dma)
        k.waitrel(vector, "vector", ("bgev", i % 2))
        aa = vector.tensor_copy(k.bgev[i % 2][:, :], k.stgw[i % 2][:, :])
        k.e_bgc = k.inc(aa, k.vC)
        k.rel[("stgw", i % 2)] = (k.vC, k.e_bgc)
    blk.vector(afn)

    def sfn2(scalar):
        k.wait(scalar, "scalar", k.vC, k.e_bgc)
        d = k.inc(scalar.dma_start(out=dst[i * 128:(i + 1) * 128, :],
                                   in_=k.bgev[i % 2][:, :]), k.bgp[i % 2], 16)
        k.rel[("bgev", i % 2)] = (k.bgp[i % 2], d)
    blk.scalar(sfn2)
    if kind == "w1" and i == 15:
        k.w1_out_mark = (k.bg0.n, k.bg1.n)
    if kind == "w3" and i == 15:
        k.w3_out_mark = (k.bg0.n, k.bg1.n)


def emit_w2a(k, g):
    blk = k.block

    def sfn(scalar):
        k.waitrel(scalar, "scalar", ("stgw", g % 2))
        k.w2a_dma = k.inc(scalar.dma_start(
            out=k.stgw[g % 2][:, 0:256],
            in_=k.w2S[g * 128:(g + 1) * 128, :]), k.sq[g % 2], 16)
    blk.scalar(sfn)

    def afn(vector):
        k.wait(vector, "vector", k.sq[g % 2], k.w2a_dma)
        k.waitrel(vector, "vector", ("w2cs", g % 2))
        aa = vector.tensor_copy(k.w2cs[g % 2][:, 0:256], k.stgw[g % 2][:, 0:256])
        k.e_w2c = k.inc(aa, k.vC)
        k.rel[("stgw", g % 2)] = (k.vC, k.e_w2c)
    blk.vector(afn)

    def pfn(tensor):
        k.wait(tensor, "tensor", k.vC, k.e_w2c)
        k.waitrel(tensor, "tensor", "pst")
        mm = tensor.matmul(k.pst[:, 0:256], k.c_h128b[:, :],
                           k.w2cs[g % 2][:, 0:256], start=True, stop=True)
        k.m_w2 = k.inc(mm, k.pC)
        k.rel[("w2cs", g % 2)] = (k.pC, k.m_w2)
    blk.tensor(pfn)

    def vfn(vector):
        k.wait(vector, "vector", k.pC, k.m_w2)
        k.waitrel(vector, "vector", ("w2ev", g % 2))
        v = vector.tensor_scalar(k.w2ev[g % 2][:, 0:256], k.pst[:, 0:256],
                                 RSH, None, op0=ALU.mult)
        k.e_w2e = k.inc(v, k.vC)
        k.rel["pst"] = (k.vC, k.e_w2e)
    blk.vector(vfn)

    def sfn2(scalar):
        k.wait(scalar, "scalar", k.vC, k.e_w2e)
        d = k.inc(scalar.dma_start(out=k.w2bA[g * 128:(g + 1) * 128, :],
                                   in_=k.w2ev[g % 2][:, 0:256]),
                  k.wep[g % 2], 16)
        k.rel[("w2ev", g % 2)] = (k.wep[g % 2], d)
    blk.scalar(sfn2)
    if g == 63:
        k.w2a_final = (k.we0.n, k.we1.n)


def emit_w2b(k, it):
    blk = k.block
    e0 = 2 * it

    def sfn(scalar):
        k.wait(scalar, "scalar", k.we0, k.w2a_final[0])
        k.wait(scalar, "scalar", k.we1, k.w2a_final[1])
        k.waitrel(scalar, "scalar", ("w2cs", it % 2))
        src = k.w2bA.rearrange("(g e) d -> e g d", g=64)[e0:e0 + 2]
        k.w2b_dma = k.inc(
            scalar.dma_start(out=k.w2cs[it % 2][:, 0:256], in_=src),
            k.wcp[it % 2], 16)
    blk.scalar(sfn)

    def pfn(tensor):
        k.wait(tensor, "tensor", k.wcp[it % 2], k.w2b_dma)
        k.waitrel(tensor, "tensor", "pst")
        mm = tensor.matmul(k.pst[:, 0:256], k.c_i2h64b[:, :],
                           k.w2cs[it % 2][:, 0:256], start=True, stop=True)
        k.m_w2 = k.inc(mm, k.pC)
        k.rel[("w2cs", it % 2)] = (k.pC, k.m_w2)
    blk.tensor(pfn)

    def vfn(vector):
        k.wait(vector, "vector", k.pC, k.m_w2)
        k.waitrel(vector, "vector", ("w2ev", it % 2))
        v = vector.tensor_copy(k.w2ev[it % 2][:, 0:256], k.pst[:, 0:256])
        k.e_w2e = k.inc(v, k.vC)
        k.rel["pst"] = (k.vC, k.e_w2e)
    blk.vector(vfn)

    def sfn2(scalar):
        k.wait(scalar, "scalar", k.vC, k.e_w2e)
        dst = k.w2p_in.rearrange("(g e) d -> e g d", g=64)[e0:e0 + 2]
        d = k.inc(scalar.dma_start(out=dst, in_=k.w2ev[it % 2][:, 0:256]),
                  k.wep[it % 2], 16)
        k.rel[("w2ev", it % 2)] = (k.wep[it % 2], d)
    blk.scalar(sfn2)
    if it == 63:
        k.w2p_final = (k.we0.n, k.we1.n)


def emit_collectives(k):
    blk = k.block
    grp8 = [list(range(8))]
    grp4 = [[0, 1, 2, 3], [4, 5, 6, 7]]

    def gp(gpsimd):
        gpsimd.wait_ge(k.bg0.sem, k.w1_out_mark[0])
        gpsimd.wait_ge(k.bg1.sem, k.w1_out_mark[1])
        cc = gpsimd.collective_compute(
            "AllGather", ALU.bypass, replica_groups=grp8,
            ins=[k.w1b_in[:, :]], outs=[k.w1b_out[:, :]])
        k.inc(cc, k.ccW1)
        gpsimd.wait_ge(k.bg0.sem, k.w3_out_mark[0])
        gpsimd.wait_ge(k.bg1.sem, k.w3_out_mark[1])
        cc = gpsimd.collective_compute(
            "AllGather", ALU.bypass, replica_groups=grp8,
            ins=[k.w3b_in[:, :]], outs=[k.w3b_out[:, :]])
        k.inc(cc, k.ccW3)
        gpsimd.wait_ge(k.rs0.sem, k.rs_stage_mark[0][0])
        gpsimd.wait_ge(k.rs1.sem, k.rs_stage_mark[0][1])
        cc = gpsimd.collective_compute(
            "ReduceScatter", ALU.add, replica_groups=grp4,
            ins=[k.rs_in[0][:, :]], outs=[k.rs_out[0][:, :]])
        k.inc(cc, k.cc0)
        gpsimd.wait_ge(k.we0.sem, k.w2p_final[0])
        gpsimd.wait_ge(k.we1.sem, k.w2p_final[1])
        cc = gpsimd.collective_compute(
            "AllGather", ALU.bypass, replica_groups=grp8,
            ins=[k.w2p_in[:, :]], outs=[k.w2p_out[:, :]])
        k.inc(cc, k.ccW2)
        gpsimd.wait_ge(k.rs0.sem, k.rs_stage_mark[1][0])
        gpsimd.wait_ge(k.rs1.sem, k.rs_stage_mark[1][1])
        cc = gpsimd.collective_compute(
            "ReduceScatter", ALU.add, replica_groups=grp4,
            ins=[k.rs_in[1][:, :]], outs=[k.rs_out[1][:, :]])
        k.inc(cc, k.cc1)
    blk.gpsimd(gp)


def emit_barrier(k):
    blk = k.block
    marks = [(k.vC, k.vC.n), (k.aC, k.aC.n), (k.pC, k.pC.n),
             (k.rs0, k.rs0.n), (k.rs1, k.rs1.n),
             (k.sx0, k.sx0.n), (k.sx1, k.sx1.n),
             (k.sq0, k.sq0.n), (k.sq1, k.sq1.n),
             (k.bg0, k.bg0.n), (k.bg1, k.bg1.n),
             (k.wc0, k.wc0.n), (k.wc1, k.wc1.n),
             (k.we0, k.we0.n), (k.we1, k.we1.n)]

    for engname in ("sync", "vector", "scalar", "tensor"):
        def fn(eng, engname=engname):
            for ctr, val in marks:
                k.wait(eng, engname, ctr, val)
        getattr(blk, engname)(fn)


def emit_ln1(k, tile):
    blk = k.block
    ccw = k.cc0 if tile == 0 else k.cc1
    x1 = k.x1[tile]
    x1b = k.x1b[tile]

    k.d_marks = {}

    VB_LN = k.vC.n

    def sy(sync):
        k.wait(sync, "sync", ccw, 1)
        for j in range(KD):
            if j >= 2:
                k.wait(sync, "sync", k.vC, VB_LN + (j - 2) + 1)
            else:
                k.waitrel(sync, "sync", ("xrs", j % 2))
            k.inc(sync.dma_start(
                out=x1[:, j * 512:(j + 1) * 512],
                in_=k.rs_out[tile][j * 128:(j + 1) * 128, :]),
                k.xrp[j % 2], 16)
            m = k.inc(sync.dma_start(
                out=k.xrs[j % 2][:, :],
                in_=k.xft[j * 128:(j + 1) * 128,
                          tile * 512:(tile + 1) * 512]), k.xrp[j % 2], 16)
            k.d_marks[j] = m
    blk.sync(sy)

    def ve(vector):
        for j in range(KD):
            k.wait(vector, "vector", k.xrp[j % 2], k.d_marks[j])
            v = vector.tensor_tensor(x1[:, j * 512:(j + 1) * 512],
                                     x1[:, j * 512:(j + 1) * 512],
                                     k.xrs[j % 2][:, :], op=ALU.add)
            m = k.inc(v, k.vC)
            assert m == VB_LN + j + 1, "ln1 mark prediction broken"
            k.rel[("xrs", j % 2)] = (k.vC, m)
            k.d_marks[j] = m
    blk.vector(ve)

    def prep(j):
        def af(scalar, j=j):
            k.wait(scalar, "scalar", k.vC, k.d_marks[j])
            k.waitrel(scalar, "scalar", ("preb", j % 2))
            aa = scalar.copy(k.preb[j % 2][:, :], x1[:, j * 512:(j + 1) * 512])
            k.e_preb = k.inc(aa, k.aC)
        blk.scalar(af)

        def vf(vector, j=j):
            k.wait(vector, "vector", k.vC, k.d_marks[j])
            k.waitrel(vector, "vector", ("sqb", j % 2))
            v = vector.tensor_tensor(k.sqb[j % 2][:, :],
                                     x1[:, j * 512:(j + 1) * 512],
                                     x1[:, j * 512:(j + 1) * 512],
                                     op=ALU.mult)
            k.e_sqb = k.inc(v, k.vC)
        blk.vector(vf)

        def pf(tensor, j=j):
            k.wait(tensor, "tensor", k.aC, k.e_preb)
            k.wait(tensor, "tensor", k.vC, k.e_sqb)
            if j == 0:
                k.waitrel(tensor, "tensor", "pst")
            mm = tensor.matmul(k.pst[0:1, 0:512], k.c_onescol_b[:, 0:1],
                               k.preb[j % 2][:, :],
                               start=(j == 0), stop=(j == KD - 1))
            m1 = k.inc(mm, k.pC)
            k.rel[("preb", j % 2)] = (k.pC, m1)
            mm = tensor.matmul(k.pa[0:1, 0:512], k.c_onescol_b[:, 0:1],
                               k.sqb[j % 2][:, :],
                               start=(j == 0), stop=(j == KD - 1))
            m2 = k.inc(mm, k.pC)
            k.rel[("sqb", j % 2)] = (k.pC, m2)
            if j == KD - 1:
                k.m_stats = m2
        blk.tensor(pf)
    for j in range(KD):
        prep(j)

    def ve2(vector):
        k.wait(vector, "vector", k.pC, k.m_stats)
        v = vector.tensor_scalar(k.stat[0:1, 0:512], k.pst[0:1, 0:512],
                                 1.0 / 2048.0, None, op0=ALU.mult)
        k.inc(v, k.vC)
        v = vector.tensor_scalar(k.stat[0:1, 512:1024], k.pa[0:1, 0:512],
                                 1.0 / 2048.0, None, op0=ALU.mult)
        m = k.inc(v, k.vC)
        k.wait(vector, "vector", k.vC, m)
        v = vector.tensor_tensor(k.lnt[0:1, 0:512], k.stat[0:1, 0:512],
                                 k.stat[0:1, 0:512], op=ALU.mult)
        m = k.inc(v, k.vC)
        k.wait(vector, "vector", k.vC, m)
        v = vector.tensor_tensor(k.lnt[0:1, 0:512], k.stat[0:1, 512:1024],
                                 k.lnt[0:1, 0:512], op=ALU.subtract)
        k.e_var = k.inc(v, k.vC)
        k.rel["pst"] = (k.vC, k.e_var)
        k.rel["pa"] = (k.vC, k.e_var)
    blk.vector(ve2)

    def sc2(scalar):
        k.wait(scalar, "scalar", k.vC, k.e_var)
        aa = scalar.activation(k.lnt[0:1, 0:512], k.lnt[0:1, 0:512], AF.Ln,
                               bias=k.c_eps[0:1, 0:1])
        m = k.inc(aa, k.aC)
        k.wait(scalar, "scalar", k.aC, m)
        aa = scalar.activation(k.lnt[0:1, 0:512], k.lnt[0:1, 0:512], AF.Exp,
                               scale=k.c_nhalf[0:1, 0:1])
        k.e_rstd = k.inc(aa, k.aC)
    blk.scalar(sc2)

    def pe2(tensor):
        k.wait(tensor, "tensor", k.aC, k.e_rstd)
        k.waitrel(tensor, "tensor", "pst")
        mm = tensor.matmul(k.pst[:, 0:512], k.c_onesrow_f[0:1, :],
                           k.lnt[0:1, 0:512], start=True, stop=True)
        k.m_bc1 = k.inc(mm, k.pC)
    blk.tensor(pe2)

    def ve3(vector):
        k.wait(vector, "vector", k.pC, k.m_bc1)
        v = vector.tensor_copy(k.abc[:, :], k.pst[:, 0:512])
        m = k.inc(v, k.vC)
        k.rel["pst"] = (k.vC, m)
        k.e_abc = m
    blk.vector(ve3)

    def pe3(tensor):
        k.wait(tensor, "tensor", k.vC, k.e_abc)
        k.waitrel(tensor, "tensor", "pst")
        mm = tensor.matmul(k.pst[:, 0:512], k.c_onesrow_f[0:1, :],
                           k.stat[0:1, 0:512], start=True, stop=True)
        k.m_bc2 = k.inc(mm, k.pC)
    blk.tensor(pe3)

    def ve4(vector):
        k.wait(vector, "vector", k.pC, k.m_bc2)
        v = vector.tensor_copy(k.bbc[:, :], k.pst[:, 0:512])
        m = k.inc(v, k.vC)
        k.rel["pst"] = (k.vC, m)
        k.wait(vector, "vector", k.vC, m)
        k.x1_marks = {}
        for j in range(KD):
            v = vector.tensor_tensor(x1[:, j * 512:(j + 1) * 512],
                                     x1[:, j * 512:(j + 1) * 512],
                                     k.bbc[:, :], op=ALU.subtract)
            m = k.inc(v, k.vC)
            k.wait(vector, "vector", k.vC, m)
            v = vector.tensor_tensor(x1[:, j * 512:(j + 1) * 512],
                                     x1[:, j * 512:(j + 1) * 512],
                                     k.abc[:, :], op=ALU.mult)
            k.x1_marks[j] = k.inc(v, k.vC)
    blk.vector(ve4)

    def sc3(scalar):
        for j in range(KD):
            k.wait(scalar, "scalar", k.vC, k.x1_marks[j])
            aa = scalar.copy(x1b[:, j * 512:(j + 1) * 512],
                             x1[:, j * 512:(j + 1) * 512])
            k.inc(aa, k.aC)
    blk.scalar(sc3)
    k.x1b_done = k.aC.n
    k.rel[("x1done", tile)] = (k.aC, k.aC.n)


def emit_ffn_quarter(k, quarter):
    blk = k.block
    j0 = quarter * 16

    for jj in range(16):
        j = j0 + jj
        r, jl = j // 8, j % 8
        sbi = j % 3
        buf = k.wsl[sbi]
        buf3 = k.wsl[3 + sbi]

        def syw(sync, j=j, r=r, jl=jl, buf=buf, buf3=buf3, sbi=sbi):
            k.waitrel(sync, "sync", ("wsl", sbi))
            pass
            k.inc(sync.dma_start(
                out=buf[:, :].rearrange("p (kk c) -> p kk c", c=128),
                in_=k.w1b_out[r * Dm:(r + 1) * Dm,
                              jl * 128:(jl + 1) * 128].rearrange(
                    "(kk p) c -> p kk c", p=128)), k.wsp[sbi], 16)
            m = k.inc(sync.dma_start(
                out=buf3[:, :].rearrange("p (kk c) -> p kk c", c=128),
                in_=k.w3b_out[r * Dm:(r + 1) * Dm,
                              jl * 128:(jl + 1) * 128].rearrange(
                    "(kk p) c -> p kk c", p=128)), k.wsp[sbi], 16)
            k.wslab_mark = m
        blk.sync(syw)

        def pe(tensor, j=j, jj=jj, buf=buf, buf3=buf3, sbi=sbi,
               wmark=None):
            if j == j0 and quarter == 0:
                k.wait(tensor, "tensor", k.ccW1, 1)
                k.wait(tensor, "tensor", k.ccW3, 1)
                k.wait(tensor, "tensor", k.aC, k.x1b_done)
            k.wait(tensor, "tensor", k.wsp[sbi], k.wslab_mark)
            k.up_marks = {}
            for tile in range(2):
                gbank = k.pquad[:, (2 * tile) * 512:(2 * tile) * 512 + 512]
                lbank = k.pquad[:, (2 * tile + 1) * 512:
                                (2 * tile + 1) * 512 + 512]
                k.waitrel(tensor, "tensor", ("bank", 2 * tile))
                for kk in range(KD):
                    mm = tensor.matmul(
                        gbank, buf[:, kk * 128:(kk + 1) * 128],
                        k.x1b[tile][:, kk * 512:(kk + 1) * 512],
                        start=(kk == 0), stop=(kk == KD - 1))
                k.up_marks[(tile, "g")] = k.inc(mm, k.pC)
                k.waitrel(tensor, "tensor", ("bank", 2 * tile + 1))
                for kk in range(KD):
                    mm = tensor.matmul(
                        lbank, buf3[:, kk * 128:(kk + 1) * 128],
                        k.x1b[tile][:, kk * 512:(kk + 1) * 512],
                        start=(kk == 0), stop=(kk == KD - 1))
                k.up_marks[(tile, "l")] = k.inc(mm, k.pC)
            k.rel[("wsl", sbi)] = (k.pC, k.pC.n)
        blk.tensor(pe)

        def sc(scalar, jj=jj):
            k.gsl_marks = {}
            for tile in range(2):
                k.waitrel(scalar, "scalar", ("gsl", tile))
                k.wait(scalar, "scalar", k.pC, k.up_marks[(tile, "g")])
                aa = scalar.activation(
                    k.gsl[:, tile * 512:(tile + 1) * 512],
                    k.pquad[:, (2 * tile) * 512:(2 * tile) * 512 + 512],
                    AF.Silu)
                m = k.inc(aa, k.aC)
                k.rel[("bank", 2 * tile)] = (k.aC, m)
                k.gsl_marks[tile] = m
        blk.scalar(sc)

        def ve(vector, jj=jj):
            for tile in range(2):
                k.wait(vector, "vector", k.aC, k.gsl_marks[tile])
                k.wait(vector, "vector", k.pC, k.up_marks[(tile, "l")])
                v = vector.tensor_tensor(
                    k.hid[:, jj * 1024 + tile * 512:
                          jj * 1024 + (tile + 1) * 512],
                    k.gsl[:, tile * 512:(tile + 1) * 512],
                    k.pquad[:, (2 * tile + 1) * 512:
                            (2 * tile + 1) * 512 + 512],
                    op=ALU.mult)
                m = k.inc(v, k.vC)
                k.rel[("bank", 2 * tile + 1)] = (k.vC, m)
                k.rel[("gsl", tile)] = (k.vC, m)
        blk.vector(ve)
    k.hid_done = k.vC.n

    k.dac_wr = getattr(k, "dac_wr", {})
    k.dac_rd = getattr(k, "dac_rd", {})
    k.x2_marks = getattr(k, "x2_marks", {})
    k.dawm = getattr(k, "dawm", {})
    for st in range(8):
        k.w2s_marks = {}
        k.dn_jj_marks = {}
        k.dn_marks = {}

        if quarter > 0:
            def syr(sync, st=st):
                for g in range(4):
                    tile, dd = g // 2, g % 2
                    dj = 2 * st + dd
                    wr = k.dac_wr[(quarter - 1, tile, dj)]
                    k.wait(sync, "sync", wr[0], wr[1])
                    k.waitrel(sync, "sync", ("dar", g))
                    k.inc(sync.dma_start(
                        out=k.dar[g][:, :],
                        in_=k.dacc[dj * 128:(dj + 1) * 128,
                                   tile * 512:(tile + 1) * 512]), k.daR, 16)
                k.dac_rd_set = k.daR.n
            blk.sync(syr)

        for jj in range(16):
            j2 = j0 + jj

            def syw2(sync, st=st, jj=jj, j2=j2):
                if st == 0 and quarter == 0 and jj == 0:
                    k.wait(sync, "sync", k.ccW2, 1)
                if jj >= 3:
                    k.wait(sync, "sync", k.pC, k.dn_jj_marks[jj - 3])
                else:
                    k.waitrel(sync, "sync", ("w2sl", jj % 3))
                m = k.inc(sync.dma_start(
                    out=k.w2sl[jj % 3][:, 0:256],
                    in_=k.w2p_out[st * HID + j2 * 128:
                                  st * HID + (j2 + 1) * 128, :]),
                    k.w2p_[jj % 3], 16)
                k.w2s_marks[jj] = m
            blk.sync(syw2)

            def ped(tensor, st=st, jj=jj):
                if st == 0 and jj == 0:
                    k.wait(tensor, "tensor", k.vC, k.hid_done)
                k.wait(tensor, "tensor", k.w2p_[jj % 3], k.w2s_marks[jj])
                for g in range(4):
                    tile, dd = g // 2, g % 2
                    if jj == 0:
                        k.waitrel(tensor, "tensor", ("bank", g))
                    bank = k.pquad[:, g * 512: g * 512 + 512]
                    mm = tensor.matmul(
                        bank, k.w2sl[jj % 3][:, dd * 128:(dd + 1) * 128],
                        k.hid[:, jj * 1024 + tile * 512:
                              jj * 1024 + (tile + 1) * 512],
                        start=(jj == 0), stop=(jj == 15))
                    if jj == 15:
                        k.dn_marks[g] = k.inc(mm, k.pC)
                if jj < 15:
                    k.dn_jj_marks[jj] = k.inc(mm, k.pC)
                else:
                    k.dn_jj_marks[jj] = k.dn_marks[3]
                    for b4 in range(3):
                        k.rel[("w2sl", b4)] = (k.pC, k.pC.n)
            blk.tensor(ped)

        def ved(vector, st=st):
            for g in range(4):
                tile, dd = g // 2, g % 2
                dj = 2 * st + dd
                k.wait(vector, "vector", k.pC, k.dn_marks[g])
                bank = k.pquad[:, g * 512: g * 512 + 512]
                if quarter == 0:
                    k.waitrel(vector, "vector", ("daw", g))
                    v = vector.tensor_copy(k.daw[g][:, :], bank)
                    m = k.inc(v, k.vC)
                    k.dawm[(tile, dj)] = m
                elif quarter < 3:
                    k.wait(vector, "vector", k.daR, k.dac_rd_set)
                    k.waitrel(vector, "vector", ("daw", g))
                    v = vector.tensor_tensor(k.daw[g][:, :], bank,
                                             k.dar[g][:, :], op=ALU.add)
                    m = k.inc(v, k.vC)
                    k.dawm[(tile, dj)] = m
                    k.rel[("dar", g)] = (k.vC, m)
                else:
                    k.wait(vector, "vector", k.daR, k.dac_rd_set)
                    x1 = k.x1[tile]
                    v = vector.tensor_tensor(k.dar[g][:, :], bank,
                                             k.dar[g][:, :], op=ALU.add)
                    m = k.inc(v, k.vC)
                    k.wait(vector, "vector", k.vC, m)
                    v = vector.tensor_tensor(
                        x1[:, dj * 512:(dj + 1) * 512],
                        x1[:, dj * 512:(dj + 1) * 512],
                        k.dar[g][:, :], op=ALU.add)
                    m = k.inc(v, k.vC)
                    k.x2_marks[(tile, dj)] = m
                    k.rel[("dar", g)] = (k.vC, m)
                k.rel[("bank", g)] = (k.vC, m)
        blk.vector(ved)

        if quarter < 3:
            def syd(sync, st=st):
                for g in range(4):
                    tile, dd = g // 2, g % 2
                    dj = 2 * st + dd
                    k.wait(sync, "sync", k.vC, k.dawm[(tile, dj)])
                    d = k.inc(sync.dma_start(
                        out=k.dacc[dj * 128:(dj + 1) * 128,
                                   tile * 512:(tile + 1) * 512],
                        in_=k.daw[g][:, :]), k.dwp[g], 16)
                    k.rel[("daw", g)] = (k.dwp[g], d)
                    k.dac_wr[(quarter, tile, dj)] = (k.dwp[g], d)
            blk.sync(syd)



def emit_ln2_out(k, tile):
    blk = k.block
    x1 = k.x1[tile]

    def prep(j):
        def af(scalar, j=j):
            k.wait(scalar, "scalar", k.vC, k.x2_marks[(tile, j)])
            k.waitrel(scalar, "scalar", ("preb", j % 2))
            aa = scalar.copy(k.preb[j % 2][:, :], x1[:, j * 512:(j + 1) * 512])
            k.e_preb = k.inc(aa, k.aC)
        blk.scalar(af)

        def vf(vector, j=j):
            k.wait(vector, "vector", k.vC, k.x2_marks[(tile, j)])
            k.waitrel(vector, "vector", ("sqb", j % 2))
            v = vector.tensor_tensor(k.sqb[j % 2][:, :],
                                     x1[:, j * 512:(j + 1) * 512],
                                     x1[:, j * 512:(j + 1) * 512],
                                     op=ALU.mult)
            k.e_sqb = k.inc(v, k.vC)
        blk.vector(vf)

        def pf(tensor, j=j):
            k.wait(tensor, "tensor", k.aC, k.e_preb)
            k.wait(tensor, "tensor", k.vC, k.e_sqb)
            if j == 0:
                k.waitrel(tensor, "tensor", "pst")
                k.waitrel(tensor, "tensor", ("bank", 0))
            mm = tensor.matmul(k.pst[0:1, 0:512], k.c_onescol_b[:, 0:1],
                               k.preb[j % 2][:, :],
                               start=(j == 0), stop=(j == KD - 1))
            m1 = k.inc(mm, k.pC)
            k.rel[("preb", j % 2)] = (k.pC, m1)
            mm = tensor.matmul(k.pquad[0:1, 0:512], k.c_onescol_b[:, 0:1],
                               k.sqb[j % 2][:, :],
                               start=(j == 0), stop=(j == KD - 1))
            m2 = k.inc(mm, k.pC)
            k.rel[("sqb", j % 2)] = (k.pC, m2)
            if j == KD - 1:
                k.m_stats = m2
        blk.tensor(pf)
    for j in range(KD):
        prep(j)

    def ve2(vector):
        k.wait(vector, "vector", k.pC, k.m_stats)
        v = vector.tensor_scalar(k.stat[0:1, 0:512], k.pst[0:1, 0:512],
                                 1.0 / 2048.0, None, op0=ALU.mult)
        k.inc(v, k.vC)
        v = vector.tensor_scalar(k.stat[0:1, 512:1024], k.pquad[0:1, 0:512],
                                 1.0 / 2048.0, None, op0=ALU.mult)
        m = k.inc(v, k.vC)
        k.wait(vector, "vector", k.vC, m)
        v = vector.tensor_tensor(k.lnt[0:1, 0:512], k.stat[0:1, 0:512],
                                 k.stat[0:1, 0:512], op=ALU.mult)
        m = k.inc(v, k.vC)
        k.wait(vector, "vector", k.vC, m)
        v = vector.tensor_tensor(k.lnt[0:1, 0:512], k.stat[0:1, 512:1024],
                                 k.lnt[0:1, 0:512], op=ALU.subtract)
        k.e_var = k.inc(v, k.vC)
        k.rel["pst"] = (k.vC, k.e_var)
        k.rel[("bank", 0)] = (k.vC, k.e_var)
    blk.vector(ve2)

    def sc2(scalar):
        k.wait(scalar, "scalar", k.vC, k.e_var)
        aa = scalar.activation(k.lnt[0:1, 0:512], k.lnt[0:1, 0:512], AF.Ln,
                               bias=k.c_eps[0:1, 0:1])
        m = k.inc(aa, k.aC)
        k.wait(scalar, "scalar", k.aC, m)
        aa = scalar.activation(k.lnt[0:1, 0:512], k.lnt[0:1, 0:512], AF.Exp,
                               scale=k.c_nhalf[0:1, 0:1])
        k.e_rstd = k.inc(aa, k.aC)
    blk.scalar(sc2)

    def pe2(tensor):
        k.wait(tensor, "tensor", k.aC, k.e_rstd)
        k.waitrel(tensor, "tensor", "pst")
        mm = tensor.matmul(k.pst[:, 0:512], k.c_onesrow_f[0:1, :],
                           k.lnt[0:1, 0:512], start=True, stop=True)
        k.m_bc1 = k.inc(mm, k.pC)
    blk.tensor(pe2)

    def ve3(vector):
        k.wait(vector, "vector", k.pC, k.m_bc1)
        v = vector.tensor_copy(k.abc[:, :], k.pst[:, 0:512])
        m = k.inc(v, k.vC)
        k.rel["pst"] = (k.vC, m)
        k.e_abc = m
    blk.vector(ve3)

    def pe3(tensor):
        k.wait(tensor, "tensor", k.vC, k.e_abc)
        k.waitrel(tensor, "tensor", "pst")
        mm = tensor.matmul(k.pst[:, 0:512], k.c_onesrow_f[0:1, :],
                           k.stat[0:1, 0:512], start=True, stop=True)
        k.m_bc2 = k.inc(mm, k.pC)
    blk.tensor(pe3)

    def ve4(vector):
        k.wait(vector, "vector", k.pC, k.m_bc2)
        v = vector.tensor_copy(k.bbc[:, :], k.pst[:, 0:512])
        m = k.inc(v, k.vC)
        k.rel["pst"] = (k.vC, m)
        k.wait(vector, "vector", k.vC, m)
        k.o_marks = {}
        for j in range(KD):
            v = vector.tensor_tensor(x1[:, j * 512:(j + 1) * 512],
                                     x1[:, j * 512:(j + 1) * 512],
                                     k.bbc[:, :], op=ALU.subtract)
            m = k.inc(v, k.vC)
            k.wait(vector, "vector", k.vC, m)
            v = vector.tensor_tensor(x1[:, j * 512:(j + 1) * 512],
                                     x1[:, j * 512:(j + 1) * 512],
                                     k.abc[:, :], op=ALU.mult)
            k.o_marks[j] = k.inc(v, k.vC)
    blk.vector(ve4)

    def syo(sync):
        for j in range(KD):
            k.wait(sync, "sync", k.vC, k.o_marks[j])
            k.inc(sync.dma_start(
                out=k.out.rearrange("(kk p) t -> p kk t", p=128)[
                    :, j, tile * 512:(tile + 1) * 512],
                in_=x1[:, j * 512:(j + 1) * 512]), k.dOut, 16)
    blk.sync(syo)


def kernel(L0_attn_hidden_states, L0_attn_Wq, L0_attn_Wk, L0_attn_Wv,
           L0_attn_Wb, L0_attn_o_norm_weight, L0_attn_Wo,
           L0_ffn_W1, L0_ffn_W2, L0_ffn_W3):
    x = np.ascontiguousarray(L0_attn_hidden_states, dtype=np.float32)
    consts = host_consts()
    nc = build_nc()

    w1t = np.ascontiguousarray(L0_ffn_W1.T)
    w3t = np.ascontiguousarray(L0_ffn_W3.T)
    w2t = np.ascontiguousarray(L0_ffn_W2.T)

    in_maps = []
    for c in range(8):
        b, hg = c // 4, c % 4
        xb_ = x[b]
        xt = np.ascontiguousarray(xb_.T)
        slots = np.concatenate([xb_[512 * hg:512 * (hg + 1)],
                                xb_[2048 + 512 * hg:2048 + 512 * (hg + 1)]])
        xft = np.ascontiguousarray(slots.T)
        m = {
            "xt": xt, "xft": xft,
            "wq": np.ascontiguousarray(L0_attn_Wq[:, 512 * hg:512 * (hg + 1)]),
            "wk": np.ascontiguousarray(L0_attn_Wk[:, 512 * hg:512 * (hg + 1)]),
            "wv": np.ascontiguousarray(L0_attn_Wv[:, 512 * hg:512 * (hg + 1)]),
            "wb4": np.ascontiguousarray(L0_attn_Wb[:, 4 * hg:4 * (hg + 1)]),
            "gam": np.ascontiguousarray(
                L0_attn_o_norm_weight.reshape(128, 1)),
            "woP": np.ascontiguousarray(L0_attn_Wo[512 * hg:512 * (hg + 1)]),
            "w1S": np.ascontiguousarray(w1t[:, 1024 * c:1024 * (c + 1)]),
            "w3S": np.ascontiguousarray(w3t[:, 1024 * c:1024 * (c + 1)]),
            "w2S": np.ascontiguousarray(w2t[:, 256 * c:256 * (c + 1)]),
        }
        m.update(consts)
        in_maps.append(m)

    res = run_bass_kernel_spmd(nc, in_maps, core_ids=list(range(8)))

    out = np.empty((B, L, Dm), np.float32)
    for c in range(8):
        b, hg = c // 4, c % 4
        o = np.asarray(res.results[c]["out"])
        o = o.reshape(KD, 128, 2, 512)
        t0 = o[:, :, 0].reshape(Dm, 512).T
        t1 = o[:, :, 1].reshape(Dm, 512).T
        out[b, 512 * hg:512 * (hg + 1)] = t0
        out[b, 2048 + 512 * hg:2048 + 512 * (hg + 1)] = t1
    return out


# version 07 (final)
# speedup vs baseline: 1.0663x; 1.0059x over previous
import numpy as np
import ml_dtypes
from contextlib import ExitStack

import concourse.bass as bass
import concourse.mybir as mybir
from concourse.bass_utils import run_bass_kernel_spmd

F32 = mybir.dt.float32
BF16 = mybir.dt.bfloat16
AF = mybir.ActivationFunctionType
ALU = mybir.AluOpType
BF = ml_dtypes.bfloat16

B, L, Dm = 2, 4096, 2048
NH, DH = 16, 128
HID = 8192
C = 128
NCH = L // C
NP = 4
KD = Dm // 128
KH = HID // 128
NLEV = 7
EPS = 1e-5
RSH = float(1.0 / np.sqrt(HID))
DHS = float(DH ** -0.5)
TAPS = False


class Ctr:
    __slots__ = ("sem", "name", "n")

    def __init__(self, sem, name):
        self.sem, self.name, self.n = sem, name, 0


class KB:
    def __init__(self):
        self.wcache = {x: {} for x in
                       ("sync", "vector", "scalar", "tensor", "gpsimd")}
        self.rel = {}

    def wait(self, eng, engname, ctr, val):
        if val <= 0:
            return
        cache = self.wcache[engname]
        if cache.get(ctr.name, 0) >= val:
            return
        eng.wait_ge(ctr.sem, val)
        cache[ctr.name] = val

    def waitrel(self, eng, engname, key):
        r = self.rel.get(key)
        if r:
            self.wait(eng, engname, r[0], r[1])

    @staticmethod
    def inc(instr, ctr, kk=1):
        instr.then_inc(ctr.sem, kk)
        ctr.n += kk
        return ctr.n


class RecBlk:

    def __init__(self):
        self.secs = []

    def tensor(self, fn):
        self.secs.append(("tensor", fn))

    def vector(self, fn):
        self.secs.append(("vector", fn))

    def scalar(self, fn):
        self.secs.append(("scalar", fn))

    def sync(self, fn):
        self.secs.append(("sync", fn))

    def gpsimd(self, fn):
        self.secs.append(("gpsimd", fn))


def record_secs(k, emit_fn, ci):
    real = k.block
    rec = RecBlk()
    k.block = rec
    try:
        emit_fn(k, ci)
    finally:
        k.block = real
    return rec.secs


def play_merged(k, r1, r2):
    n1, n2 = len(r1), len(r2)
    i = j = 0
    while i < n1 or j < n2:
        if j >= n2 or (i < n1 and i * n2 <= j * n1):
            eng, fn = r1[i]
            i += 1
        else:
            eng, fn = r2[j]
            j += 1
        getattr(k.block, eng)(fn)


def host_consts():
    i = np.arange(128)
    h128 = np.where((np.bitwise_count(i[:, None] & i[None, :]) % 2) == 0,
                    1.0, -1.0)
    g = np.arange(64)
    h64 = np.where((np.bitwise_count(g[:, None] & g[None, :]) % 2) == 0,
                   1.0, -1.0)
    i2h64 = np.kron(np.eye(2), h64)
    mask_s = -np.triu(np.ones((128, 128), np.float32), 1)
    mask_i = np.triu(np.ones((128, 128), np.float32), 0)
    return {
        "identb": np.eye(128).astype(BF),
        "h128b": h128.astype(BF),
        "i2h64b": i2h64.astype(BF),
        "masku_s4": np.tile(mask_s, (1, 4)).astype(np.float32),
        "masku_i4": np.tile(mask_i, (1, 4)).astype(np.float32),
        "onescol_b": np.ones((128, 1), BF),
        "onesrow_f": np.ones((1, 128), np.float32),
    }


def build_nc():
    nc = bass.Bass("TRN2", target_bir_lowering=False)
    k = KB()
    k.nc = nc

    di = lambda n, s, dt: nc.dram_tensor(n, s, dt, kind="ExternalInput").ap()
    do = lambda n, s, dt: nc.dram_tensor(n, s, dt, kind="ExternalOutput").ap()
    dd = lambda n, s, dt: nc.dram_tensor(n, s, dt).ap()

    k.xt = di("xt", [Dm, L], F32)
    k.xft = di("xft", [Dm, 1024], F32)
    k.wq = di("wq", [Dm, 512], F32)
    k.wk = di("wk", [Dm, 512], F32)
    k.wv = di("wv", [Dm, 512], F32)
    k.wb4 = di("wb4", [Dm, 4], F32)
    k.gam = di("gam", [128, 1], F32)
    k.woP = di("woP", [512, Dm], F32)
    k.w1S = di("w1S", [Dm, 1024], F32)
    k.w3S = di("w3S", [Dm, 1024], F32)
    k.w2S = di("w2S", [HID, 256], F32)
    k.identb = di("identb", [128, 128], BF16)
    k.h128b = di("h128b", [128, 128], BF16)
    k.i2h64b = di("i2h64b", [128, 128], BF16)
    k.masku_s4 = di("masku_s4", [128, 512], F32)
    k.masku_i4 = di("masku_i4", [128, 512], F32)
    k.onescol_b = di("onescol_b", [128, 1], BF16)
    k.onesrow_f = di("onesrow_f", [1, 128], F32)

    k.out = do("out", [Dm, 1024], F32)
    if TAPS:
        k.tap_o = do("tap_o", [512, L], BF16)

    k.w1b_in = dd("w1b_in", [Dm, 1024], BF16)
    k.w1b_out = dd("w1b_out", [8 * Dm, 1024], BF16)
    k.w3b_in = dd("w3b_in", [Dm, 1024], BF16)
    k.w3b_out = dd("w3b_out", [8 * Dm, 1024], BF16)
    k.w2bA = dd("w2bA", [HID, 256], BF16)
    k.w2p_in = dd("w2p_in", [HID, 256], BF16)
    k.w2p_out = dd("w2p_out", [8 * HID, 256], BF16)
    k.rs_in = [dd(f"rs_in{h}", [4 * Dm, 512], F32) for h in range(2)]
    k.rs_out = [dd(f"rs_out{h}", [Dm, 512], F32) for h in range(2)]
    k.dacc = dd("dacc", [Dm, 1024], F32)

    with ExitStack() as ctx:
        E = ctx.enter_context
        sb = lambda n, s, dt: E(nc.sbuf_tensor(n, s, dt))

        k.c_identb = sb("c_identb", [128, 128], BF16)
        k.c_h128b = sb("c_h128b", [128, 128], BF16)
        k.c_i2h64b = sb("c_i2h64b", [128, 128], BF16)
        k.c_masku_s4 = sb("c_masku_s4", [128, 512], F32)
        k.c_masku_i4 = sb("c_masku_i4", [128, 512], F32)
        k.c_onescol_b = sb("c_onescol_b", [128, 1], BF16)
        k.c_onesrow_f = sb("c_onesrow_f", [1, 128], F32)
        k.c_gam = sb("c_gam", [128, 1], F32)
        k.c_eps = sb("c_eps", [128, 1], F32)
        k.c_nhalf = sb("c_nhalf", [128, 1], F32)
        k.c_i128 = sb("c_i128", [128, 1], F32)
        k.c_i2048 = sb("c_i2048", [128, 1], F32)

        k.pquad = E(nc.psum_tensor("pquad", [128, 2048], F32))
        k.pa = E(nc.psum_tensor("pa", [128, 512], F32))
        k.pb = E(nc.psum_tensor("pb", [128, 512], F32))
        k.ptr = E(nc.psum_tensor("ptr", [128, 512], BF16))
        k.pst = E(nc.psum_tensor("pst", [128, 512], F32))

        names = ["dC", "dOut", "daR", "vC", "aC", "pC",
                 "ccW1", "ccW3", "ccW2", "cc0", "cc1"]
        for base in ("sq", "sx", "bg", "wc", "we", "rs", "ws", "w2", "xr"):
            names += [base + "0", base + "1"]
        names += ["dw0", "dw1", "dw2", "dw3", "ws2", "ws3", "w22", "w23"]
        for nm in names:
            setattr(k, nm, Ctr(E(nc.semaphore(nm)), nm))
        k.sq = [k.sq0, k.sq1]
        k.sx = [k.sx0, k.sx1]
        k.bgp = [k.bg0, k.bg1]
        k.wcp = [k.wc0, k.wc1]
        k.wep = [k.we0, k.we1]
        k.rsp = [k.rs0, k.rs1]
        k.wsp = [k.ws0, k.ws1, k.ws2, k.ws3]
        k.w2p_ = [k.w20, k.w21, k.w22, k.w23]
        k.xrp = [k.xr0, k.xr1]
        k.dwp = [k.dw0, k.dw1, k.dw2, k.dw3]

        k.block = E(nc.Block())

        with ExitStack() as bc_ctx:
            Eb = bc_ctx.enter_context
            sbb = lambda n, s, dt: Eb(nc.sbuf_tensor(n, s, dt))
            k.stgw = [sbb(f"stgw{i}", [128, 1024], F32) for i in range(2)]
            k.stgx = [sbb(f"stgx{i}", [128, 1024], F32) for i in range(2)]
            k.w2cs = [sbb(f"w2cs{i}", [128, 256], BF16) for i in range(2)]
            k.w2ev = [sbb(f"w2ev{i}", [128, 256], BF16) for i in range(2)]
            k.bgev = [sbb(f"bgev{i}", [128, 1024], BF16) for i in range(2)]
            k.wqb = sbb("wqb", [128, KD * 512], BF16)
            k.wkb = sbb("wkb", [128, KD * 512], BF16)
            k.wvb = sbb("wvb", [128, KD * 512], BF16)
            k.wbb = sbb("wbb", [128, KD * 4], BF16)
            k.wopb = sbb("wopb", [128, NP * Dm], BF16)
            k.xb = [sbb(f"xb{i}", [128, KD * 512], BF16) for i in range(2)]
            k.ktm = [sbb(f"ktm{i}", [128, 4 * 512], BF16) for i in range(2)]
            k.qtm = [sbb(f"qtm{i}", [128, 4 * 512], BF16) for i in range(2)]
            k.vtm = [sbb(f"vtm{i}", [128, 4 * 512], BF16) for i in range(2)]
            k.btf = [sbb(f"btf{i}", [128, 4 * 4], F32) for i in range(2)]
            k.qf = sbb("qf", [128, 512], F32)
            k.kf = sbb("kf", [128, 512], F32)
            k.qsq = sbb("qsq", [128, 512], F32)
            k.qss = sbb("qss", [128, 8], F32)
            k.s32 = sbb("s32", [128, NP * 128], F32)
            k.rksq = sbb("rksq", [128, NP * 384], BF16)
            k.u4 = sbb("u4", [128, NP * 128], BF16)
            k.gm4 = sbb("gm4", [128, NP * 128], BF16)
            k.dgn4 = sbb("dgn4", [128, NP * 128], BF16)
            k.xp = [sbb(f"xp{j}", [128, NP * 384], BF16) for j in range(2)]
            k.tvring = [sbb(f"tvr{i}", [128, NP * 128], BF16) for i in range(4)]
            k.tkring = [sbb(f"tkr{i}", [128, NP * 128], BF16) for i in range(4)]
            k.gmring = [sbb(f"gmr{i}", [128, NP * 128], BF16) for i in range(4)]
            k.qring = [sbb(f"qr{i}", [128, NP * 128], BF16) for i in range(4)]
            k.pt = [sbb(f"pt{j}", [128, NP * 128], BF16) for j in range(2)]
            k.dl4 = sbb("dl4", [128, NP * 128], BF16)
            k.osq = sbb("osq", [128, NP * 128], F32)
            k.oss = sbb("oss", [128, NP], F32)
            k.orstd = sbb("orstd", [128, NP], F32)
            k.onb = sbb("onb", [128, NP * 128], BF16)
            k.obuf = [sbb(f"obuf{i}", [128, NP * 512], BF16) for i in range(2)]
            k.rsstg = [sbb(f"rsstg{i}", [128, 512], F32) for i in range(2)]

            k.qk_done_c = {}
            k.v_done_c = {}
            k.rec1_done = {}
            k.e_sbf = 0
            emit_phase_a(k)
            for ci in range(NCH + 3):
                p0 = []
                if ci < NCH:
                    if ci % 4 == 0 and ci // 4 + 1 < 8:
                        emit_x_slab(k, ci // 4 + 1)
                    p0 = record_secs(k, emit_proj, ci)
                r1 = (record_secs(k, emit_rec1, ci - 1)
                      if 1 <= ci < NCH + 1 else [])
                r2 = record_secs(k, emit_rec2, ci - 3) if ci >= 3 else []
                play_merged(k, p0 + r1, r2)
                if ci >= 3:
                    if (ci - 3) % 4 == 3:
                        emit_wo_group(k, (ci - 3) // 4)
                if ci < NCH:
                    emit_bg(k, ci)
            emit_collectives(k)
            emit_barrier(k)

        with ExitStack() as d_ctx:
            Ed = d_ctx.enter_context
            sbd = lambda n, s, dt: Ed(nc.sbuf_tensor(n, s, dt))
            k.x1 = [sbd(f"x1_{t}", [128, KD * 512], F32) for t in range(2)]
            k.x1b = [sbd(f"x1b_{t}", [128, KD * 512], BF16) for t in range(2)]
            k.hid = sbd("hid", [128, 16 * 1024], BF16)
            k.wsl = [sbd(f"wsl{i}", [128, 16 * 128], BF16) for i in range(6)]
            k.w2sl = [sbd(f"w2sl{i}", [128, 2048], BF16) for i in range(3)]
            k.xrs = [sbd(f"xrs{i}", [128, 512], F32) for i in range(2)]
            k.dar = [sbd(f"dar{i}", [128, 512], F32) for i in range(4)]
            k.daw = [sbd(f"daw{i}", [128, 512], F32) for i in range(4)]
            k.preb = [sbd(f"preb{i}", [128, 512], BF16) for i in range(2)]
            k.sqb = [sbd(f"sqb{i}", [128, 512], BF16) for i in range(2)]
            k.stat = sbd("stat", [1, 1024], F32)
            k.lnt = sbd("lnt", [1, 512], F32)
            k.abc = sbd("abc", [128, 512], F32)
            k.bbc = sbd("bbc", [128, 512], F32)
            k.gsl = sbd("gsl", [128, 1024], BF16)

            for tile in range(2):
                emit_ln1(k, tile)
            for quarter in range(4):
                emit_ffn_quarter(k, quarter)
            for tile in range(2):
                emit_ln2_out(k, tile)

        def sync_end(sync):
            k.wait(sync, "sync", k.dOut, k.dOut.n)
        k.block.sync(sync_end)

        def vec_end(vector):
            k.wait(vector, "vector", k.dOut, k.dOut.n)
        k.block.vector(vec_end)
    return nc


def emit_phase_a(k):
    blk = k.block

    def sync_consts(sync):
        d = lambda o, i: k.inc(sync.dma_start(out=o, in_=i), k.dC, 16)
        d(k.c_identb[:, :], k.identb)
        d(k.c_h128b[:, :], k.h128b)
        d(k.c_i2h64b[:, :], k.i2h64b)
        d(k.c_masku_s4[:, :], k.masku_s4)
        d(k.c_masku_i4[:, :], k.masku_i4)
        d(k.c_onescol_b[:, :], k.onescol_b)
        d(k.c_onesrow_f[:, :], k.onesrow_f)
        d(k.c_gam[:, :], k.gam)
    blk.sync(sync_consts)

    def gp_consts(gpsimd):
        gpsimd.memset(k.c_eps[:, :], EPS)
        gpsimd.memset(k.c_nhalf[:, :], -0.5)
        gpsimd.memset(k.c_i128[:, :], 1.0 / 128.0)
        m = gpsimd.memset(k.c_i2048[:, :], 1.0 / 2048.0)
        k.inc(m, k.dC, 16)
    blk.gpsimd(gp_consts)
    k.CONSTS = k.dC.n

    def vec_zero(vector):
        k.inc(vector.memset(k.s32[:, :], 0.0), k.vC)
        z = k.rksq[:, :].rearrange("p (f c) -> p f c", f=NP)[:, :, 256:384]
        k.inc(vector.memset(z, 0.0), k.vC)
    blk.vector(vec_zero)
    k.SZERO = k.vC.n

    jobs = []
    for (src, dst, width) in ((k.wq, "wqb", 512), (k.wk, "wkb", 512),
                              (k.wv, "wvb", 512), (k.wb4, "wbb", 4)):
        for kk in range(KD):
            jobs.append((src, dst, width, kk, kk))
    for kk in range(NP):
        jobs.append((k.woP, "wopb", 1024, 2 * kk, kk))
        jobs.append((k.woP, "wopb", 1024, 2 * kk + 1, kk))
    k.qw_marks = {}

    qw_marks = {}
    VBASE = k.vC.n

    def scalar_qw(scalar):
        for t, (src, dst, width, sl, kk) in enumerate(jobs):
            if t >= 2:
                k.wait(scalar, "scalar", k.vC, VBASE + t - 1)
            if dst == "wopb":
                half = sl % 2
                src_ap = src[kk * 128:(kk + 1) * 128,
                             half * 1024:(half + 1) * 1024]
            else:
                src_ap = src[sl * 128:(sl + 1) * 128, :]
            qw_marks[t] = k.inc(
                scalar.dma_start(out=k.stgw[t % 2][:, 0:width], in_=src_ap),
                k.sq[t % 2], 16)
    blk.scalar(scalar_qw)

    def vec_qw(vector):
        for t, (src, dst, width, sl, kk) in enumerate(jobs):
            k.wait(vector, "vector", k.sq[t % 2], qw_marks[t])
            dt_ = getattr(k, dst)
            if dst == "wopb":
                half = sl % 2
                dv = dt_[:, kk * 2048 + half * 1024: kk * 2048 + (half + 1) * 1024]
            else:
                dv = dt_[:, sl * width:(sl + 1) * width]
            m = k.inc(vector.tensor_copy(dv, k.stgw[t % 2][:, 0:width]), k.vC)
            k.rel[("stgw", t % 2)] = (k.vC, m)
    blk.vector(vec_qw)
    k.QW_RDY = k.vC.n

    k.xcast_marks = {}
    k.xslab_done = {}
    k.proj_done = {}
    emit_x_slab(k, 0)

    k.bg = ([("w1", i) for i in range(16)] + [("w3", i) for i in range(16)] +
            [("w2a", i) for i in range(64)] + [("w2b", i) for i in range(64)])
    k.bg_i = 0


def emit_x_slab(k, a):
    blk = k.block
    base_t = a * 8
    xmarks = {}
    VBASE = k.vC.n

    def sync_fn(sync):
        for i in range(8):
            t = base_t + i
            if i >= 2:
                k.wait(sync, "sync", k.vC, VBASE + (i - 2) + 1)
            elif a >= 1:
                k.wait(sync, "sync", k.vC, k.xcast_marks[base_t - 2 + i])
            src = k.xt[:, a * 512:(a + 1) * 512].rearrange(
                "(kk p) t -> p kk t", p=128)[:, 2 * i:2 * i + 2, :]
            dst = k.stgx[t % 2][:, :].rearrange("p (a t) -> p a t", a=2)
            xmarks[t] = k.inc(sync.dma_start(out=dst, in_=src),
                              k.sx[t % 2], 16)

    def vec_fn(vector):
        for i in range(8):
            t = base_t + i
            if i == 0 and a >= 2:
                k.wait(vector, "vector", k.pC, k.proj_done[a - 2])
            k.wait(vector, "vector", k.sx[t % 2], xmarks[t])
            m = k.inc(vector.tensor_copy(
                k.xb[a % 2][:, 2 * i * 512:(2 * i + 2) * 512],
                k.stgx[t % 2][:, :]), k.vC)
            assert m == VBASE + i + 1, "xcast mark prediction broken"
            k.xcast_marks[t] = m
    blk.sync(sync_fn)
    blk.vector(vec_fn)
    k.xslab_done[a] = k.vC.n


def emit_proj(k, ci):
    blk = k.block
    a = ci // 4
    wv = a % 2
    cc = ci % 4
    slab = k.xb[wv]

    def pe(tensor):
        k.wait(tensor, "tensor", k.dC, k.CONSTS)
        k.wait(tensor, "tensor", k.vC, k.xslab_done[a])
        k.wait(tensor, "tensor", k.vC, k.QW_RDY)
        k.wait(tensor, "tensor", k.vC, k.SZERO)
        marks = {}
        for (wname, bank) in (("wqb", 0), ("wkb", 1), ("wvb", 2),
                              ("wbb", 3)):
            k.waitrel(tensor, "tensor", ("bank", bank))
            w = getattr(k, wname)
            width = 4 if wname == "wbb" else 512
            outp = k.pquad[:, bank * 512: bank * 512 + width]
            for kk in range(KD):
                mm = tensor.matmul(
                    outp,
                    slab[:, kk * 512 + cc * 128: kk * 512 + cc * 128 + 128],
                    w[:, kk * width:(kk + 1) * width],
                    start=(kk == 0), stop=(kk == KD - 1))
            marks[wname] = k.inc(mm, k.pC)
        k.pj = marks
    blk.tensor(pe)

    def vec(vector):
        k.wait(vector, "vector", k.dC, k.CONSTS)
        k.wait(vector, "vector", k.pC, k.pj["wqb"])
        e_q = k.inc(vector.tensor_copy(k.qf[:, :], k.pquad[:, 0:512]), k.vC)
        k.wait(vector, "vector", k.pC, k.pj["wkb"])
        e_k = k.inc(vector.tensor_copy(k.kf[:, :], k.pquad[:, 512:1024]),
                    k.vC)
        k.rel[("bank", 0)] = (k.vC, e_q)
        k.rel[("bank", 1)] = (k.vC, e_k)
        k.ev_qk = (e_q, e_k)
    blk.vector(vec)

    def sc(scalar):
        e_q, e_k = k.ev_qk
        k.wait(scalar, "scalar", k.dC, k.CONSTS)
        k.wait(scalar, "scalar", k.aC, k.aC.n)
        k.wait(scalar, "scalar", k.vC, e_q)
        for h in range(4):
            aa = scalar.activation(k.qsq[:, h * 128:(h + 1) * 128],
                                   k.qf[:, h * 128:(h + 1) * 128], AF.Square,
                                   accum_out=k.qss[:, h:h + 1])
        mq = k.inc(aa, k.aC)
        k.wait(scalar, "scalar", k.aC, mq)
        k.wait(scalar, "scalar", k.vC, e_k)
        for h in range(4):
            aa = scalar.activation(k.qsq[:, h * 128:(h + 1) * 128],
                                   k.kf[:, h * 128:(h + 1) * 128], AF.Square,
                                   accum_out=k.qss[:, 4 + h:5 + h])
        k.inc(aa, k.aC)
        k.wait(scalar, "scalar", k.pC, k.pj["wvb"])
        ev = k.inc(scalar.activation(k.vtm[wv][:, cc * 512:(cc + 1) * 512],
                                     k.pquad[:, 1024:1536], AF.Silu), k.aC)
        k.wait(scalar, "scalar", k.pC, k.pj["wbb"])
        eb = k.inc(scalar.activation(k.btf[wv][:, cc * 4:(cc + 1) * 4],
                                     k.pquad[:, 1536:1540], AF.Sigmoid),
                   k.aC)
        k.rel[("bank", 2)] = (k.aC, ev)
        k.rel[("bank", 3)] = (k.aC, eb)
        k.wait(scalar, "scalar", k.aC, k.aC.n)
        aa = scalar.activation(k.qss[:, 0:8], k.qss[:, 0:8], AF.Sqrt)
        k.sq_done = k.inc(aa, k.aC)
        k.v_done = ev
        k.b_done = eb
    blk.scalar(sc)

    def vec2(vector):
        k.wait(vector, "vector", k.aC, k.sq_done)
        v = vector.tensor_single_scalar(k.qss[:, 0:8], k.qss[:, 0:8], 1e-12,
                                        ALU.max)
        m = k.inc(v, k.vC)
        k.wait(vector, "vector", k.vC, m)
        v = vector.reciprocal(k.qss[:, 0:8], k.qss[:, 0:8])
        m = k.inc(v, k.vC)
        k.wait(vector, "vector", k.vC, m)
        for h in range(4):
            v = vector.tensor_scalar(
                k.qtm[wv][:, cc * 512 + h * 128: cc * 512 + (h + 1) * 128],
                k.qf[:, h * 128:(h + 1) * 128],
                k.qss[:, h:h + 1], DHS, op0=ALU.mult, op1=ALU.mult)
            k.inc(v, k.vC)
        for h in range(4):
            v = vector.tensor_scalar(
                k.ktm[wv][:, cc * 512 + h * 128: cc * 512 + (h + 1) * 128],
                k.kf[:, h * 128:(h + 1) * 128],
                k.qss[:, 4 + h:5 + h], None, op0=ALU.mult)
            k.inc(v, k.vC)
        k.qk_done_c[ci] = k.vC.n
        k.v_done_c[ci] = k.v_done
        if ci % 4 == 3:
            k.proj_done[a] = k.pC.n
    blk.vector(vec2)


def emit_rec1(k, ci):
    blk = k.block
    wv = (ci // 4) % 2
    cc = ci % 4
    ring = ci % 4
    ks_ = lambda p: k.ktm[wv][:, cc * 512 + p * 128: cc * 512 + (p + 1) * 128]
    qs_ = lambda p: k.qtm[wv][:, cc * 512 + p * 128: cc * 512 + (p + 1) * 128]
    rk = lambda p, o, w: k.rksq[:, p * 384 + o: p * 384 + o + w]
    pqv = lambda o, w: k.pquad[:, :].rearrange(
        "p (f c) -> p f c", f=4)[:, :, o:o + w]
    r4 = lambda t: t[:, :].rearrange("p (f c) -> p f c", f=NP)
    rks4 = lambda o, w: k.rksq[:, :].rearrange(
        "p (f c) -> p f c", f=NP)[:, :, o:o + w]
    xpv = lambda j, o, w: k.xp[j][:, :].rearrange(
        "p (f c) -> p f c", f=NP)[:, :, o:o + w]
    bview = k.btf[wv][:, cc * 4: cc * 4 + 4].unsqueeze(2).broadcast_to(
        [128, 4, 128])
    identv = k.c_identb[:, :].unsqueeze(1).broadcast_to([128, 4, 128])

    def pe1(tensor):
        k.wait(tensor, "tensor", k.dC, k.CONSTS)
        k.wait(tensor, "tensor", k.vC, k.qk_done_c[ci])
        k.wait(tensor, "tensor", k.aC, k.v_done_c[ci])
        k.waitrel(tensor, "tensor", "ptr")
        for p in range(NP):
            mm = tensor.transpose(k.ptr[:, p * 128:(p + 1) * 128], ks_(p),
                                  k.c_identb[:, :])
        k.m_trk = k.inc(mm, k.pC)
    blk.tensor(pe1)

    def ve1(vector):
        k.wait(vector, "vector", k.pC, k.m_trk)
        k.waitrel(vector, "vector", "rksq_kq")
        v = vector.tensor_copy(rks4(0, 128), r4(k.ptr))
        k.e_kfm = k.inc(v, k.vC)
    blk.vector(ve1)

    def pe2(tensor):
        k.wait(tensor, "tensor", k.vC, k.e_kfm)
        for p in range(NP):
            mm = tensor.transpose(k.ptr[:, p * 128:(p + 1) * 128], qs_(p),
                                  k.c_identb[:, :])
        k.m_trq = k.inc(mm, k.pC)
    blk.tensor(pe2)

    def sc1(scalar):
        k.wait(scalar, "scalar", k.pC, k.m_trq)
        k.waitrel(scalar, "scalar", ("qring", ring))
        aa = scalar.copy(r4(k.qring[ring]), r4(k.ptr))
        k.e_qfm = k.inc(aa, k.aC)
        k.rel["ptr"] = (k.aC, k.e_qfm)
    blk.scalar(sc1)

    def ve1b(vector):
        k.wait(vector, "vector", k.aC, k.e_qfm)
        v = vector.tensor_copy(rks4(128, 128), r4(k.qring[ring]))
        k.e_qfm2 = k.inc(v, k.vC)
    blk.vector(ve1b)

    def pe3(tensor):
        k.wait(tensor, "tensor", k.vC, k.e_qfm2)
        for bank in range(4):
            k.waitrel(tensor, "tensor", ("bank", bank))
        for p in range(NP):
            mm = tensor.matmul(k.pquad[:, p * 512: p * 512 + 256],
                               rk(p, 0, 128), rk(p, 0, 256),
                               start=True, stop=True)
        k.m_ags = k.inc(mm, k.pC)
        k.rel["rksq_kq"] = (k.pC, k.m_ags)
    blk.tensor(pe3)

    def ve2(vector):
        k.wait(vector, "vector", k.pC, k.m_ags)
        mss = k.c_masku_s4[:, :].rearrange("p (f c) -> p f c", f=4)
        msi = k.c_masku_i4[:, :].rearrange("p (f c) -> p f c", f=4)
        k.inc(vector.tensor_tensor(r4(k.u4), pqv(0, 128), mss, op=ALU.mult),
              k.vC)
        k.waitrel(vector, "vector", ("gmring", ring))
        k.inc(vector.tensor_tensor(r4(k.gmring[ring]), pqv(128, 128), msi,
                                   op=ALU.mult), k.vC)
        vt = k.vtm[wv][:, cc * 512:(cc + 1) * 512].rearrange(
            "p (f c) -> p f c", f=4)
        kt = k.ktm[wv][:, cc * 512:(cc + 1) * 512].rearrange(
            "p (f c) -> p f c", f=4)
        k.inc(vector.tensor_copy(xpv(0, 0, 128), vt), k.vC)
        k.inc(vector.tensor_copy(xpv(0, 128, 128), kt), k.vC)
        k.wait(vector, "vector", k.vC, k.vC.n)
        k.inc(vector.tensor_tensor(r4(k.pt[0]), r4(k.u4), bview,
                                   op=ALU.mult), k.vC)
        v = vector.tensor_tensor(r4(k.dgn4), identv, bview, op=ALU.mult)
        k.e_rec0 = k.inc(v, k.vC)
    blk.vector(ve2)

    def pe4(tensor):
        k.wait(tensor, "tensor", k.vC, k.e_rec0)
        for p in range(NP):
            mm = tensor.matmul(k.pquad[:, p * 512: p * 512 + 128],
                               k.u4[:, p * 128:(p + 1) * 128],
                               k.dgn4[:, p * 128:(p + 1) * 128],
                               start=True, stop=True)
        k.m_p0 = k.inc(mm, k.pC)
    blk.tensor(pe4)

    def sc2(scalar):
        k.wait(scalar, "scalar", k.pC, k.m_p0)
        aa = scalar.copy(xpv(0, 256, 128), pqv(0, 128))
        k.e_p0 = k.inc(aa, k.aC)
    blk.scalar(sc2)

    for j in range(NLEV):
        cur, nxt = j % 2, 1 - j % 2

        def pej(tensor, j=j, cur=cur):
            if j == 0:
                k.wait(tensor, "tensor", k.aC, k.e_p0)
                k.wait(tensor, "tensor", k.vC, k.e_rec0)
            else:
                k.wait(tensor, "tensor", k.vC, k.e_ptcopy)
            w = 384 if j < NLEV - 1 else 256
            for p in range(NP):
                mm = tensor.matmul(k.pquad[:, p * 512: p * 512 + w],
                                   k.pt[cur][:, p * 128:(p + 1) * 128],
                                   k.xp[cur][:, p * 384: p * 384 + w],
                                   start=True, stop=True)
            k.m_mmx = k.inc(mm, k.pC)
            if j < NLEV - 1:
                for p in range(NP):
                    mm = tensor.matmul(
                        k.pquad[:, p * 512 + 384: p * 512 + 512],
                        k.xp[cur][:, p * 384 + 256: p * 384 + 384],
                        k.pt[cur][:, p * 128:(p + 1) * 128],
                        start=True, stop=True)
                k.m_mmpt = k.inc(mm, k.pC)
        blk.tensor(pej)

        def vej(vector, j=j, cur=cur, nxt=nxt):
            k.wait(vector, "vector", k.pC,
                   k.m_mmpt if j < NLEV - 1 else k.m_mmx)
            v = vector.tensor_tensor(xpv(nxt, 0, 256), pqv(0, 256),
                                     xpv(cur, 0, 256), op=ALU.add)
            k.e_xadd = k.inc(v, k.vC)
            if j < NLEV - 1:
                v = vector.tensor_copy(xpv(nxt, 256, 128), pqv(256, 128))
                k.e_pcopy = k.inc(v, k.vC)
                v = vector.tensor_copy(r4(k.pt[nxt]), pqv(384, 128))
                k.e_ptcopy = k.inc(v, k.vC)
        blk.vector(vej)

    fin = NLEV % 2

    def ve3(vector):
        k.wait(vector, "vector", k.vC, k.e_xadd)
        k.waitrel(vector, "vector", ("tvring", ring))
        v = vector.tensor_copy(r4(k.tvring[ring]), xpv(fin, 0, 128))
        k.e_tv = k.inc(v, k.vC)
    blk.vector(ve3)

    def pe5(tensor):
        k.wait(tensor, "tensor", k.vC, k.e_xadd)
        k.waitrel(tensor, "tensor", "ptr")
        for p in range(NP):
            mm = tensor.transpose(k.ptr[:, p * 128:(p + 1) * 128],
                                  k.xp[fin][:, p * 384 + 128: p * 384 + 256],
                                  k.c_identb[:, :])
        k.m_trtk = k.inc(mm, k.pC)
    blk.tensor(pe5)

    def ve4(vector):
        k.wait(vector, "vector", k.pC, k.m_trtk)
        k.waitrel(vector, "vector", ("tkring", ring))
        v = vector.tensor_copy(r4(k.tkring[ring]), r4(k.ptr))
        k.e_tk = k.inc(v, k.vC)
        k.rel["ptr"] = (k.vC, k.e_tk)
        for b in range(4):
            k.rel[("bank", b)] = (k.vC, k.e_xadd)
        k.rec1_done[ci] = (k.e_tv, k.e_tk, k.vC.n, k.aC.n)
    blk.vector(ve4)


def emit_rec2(k, ci):
    blk = k.block
    wv = (ci // 4) % 2
    cc = ci % 4
    ring = ci % 4
    ks_ = lambda p: k.ktm[wv][:, cc * 512 + p * 128: cc * 512 + (p + 1) * 128]
    rks4 = lambda o, w: k.rksq[:, :].rearrange(
        "p (f c) -> p f c", f=NP)[:, :, o:o + w]
    rk = lambda p, o, w: k.rksq[:, p * 384 + o: p * 384 + o + w]
    r4 = lambda t: t[:, :].rearrange("p (f c) -> p f c", f=NP)
    bview = k.btf[wv][:, cc * 4: cc * 4 + 4].unsqueeze(2).broadcast_to(
        [128, 4, 128])

    def pe1(tensor):
        e_tv, e_tk, vfin, afin = k.rec1_done[ci]
        k.wait(tensor, "tensor", k.vC, e_tk)
        k.wait(tensor, "tensor", k.vC, e_tv)
        k.wait(tensor, "tensor", k.aC, k.e_sbf)
        k.waitrel(tensor, "tensor", "pa")
        for p in range(NP):
            mm = tensor.matmul(k.pa[:, p * 128:(p + 1) * 128],
                               k.tkring[ring][:, p * 128:(p + 1) * 128],
                               rk(p, 256, 128),
                               start=True, stop=True)
        k.m_tks = k.inc(mm, k.pC)
    blk.tensor(pe1)

    def ve1(vector):
        k.wait(vector, "vector", k.pC, k.m_tks)
        v = vector.tensor_tensor(r4(k.dl4), r4(k.tvring[ring]),
                                 r4(k.pa), op=ALU.subtract)
        m = k.inc(v, k.vC)
        k.wait(vector, "vector", k.vC, m)
        v = vector.tensor_tensor(r4(k.dl4), r4(k.dl4), bview, op=ALU.mult)
        k.e_dl = k.inc(v, k.vC)
        k.rel[("tvring", ring)] = (k.vC, k.e_dl)
    blk.vector(ve1)

    def pe2(tensor):
        k.wait(tensor, "tensor", k.vC, k.e_dl)
        k.waitrel(tensor, "tensor", "pb")
        for p in range(NP):
            tensor.matmul(k.pb[:, p * 128:(p + 1) * 128],
                          k.qring[ring][:, p * 128:(p + 1) * 128],
                          rk(p, 256, 128), start=True, stop=False)
            mm = tensor.matmul(k.pb[:, p * 128:(p + 1) * 128],
                               k.gmring[ring][:, p * 128:(p + 1) * 128],
                               k.dl4[:, p * 128:(p + 1) * 128],
                               start=False, stop=True)
        k.m_o = k.inc(mm, k.pC)
        k.rel[("qring", ring)] = (k.pC, k.m_o)
        k.rel[("gmring", ring)] = (k.pC, k.m_o)
        for p in range(NP):
            mm = tensor.matmul(k.pa[:, p * 128:(p + 1) * 128],
                               ks_(p), k.dl4[:, p * 128:(p + 1) * 128],
                               start=True, stop=True)
        k.m_s = k.inc(mm, k.pC)
        k.rel[("tkring", ring)] = (k.pC, k.m_s)
    blk.tensor(pe2)

    def ve2(vector):
        k.wait(vector, "vector", k.pC, k.m_s)
        v = vector.tensor_tensor(k.s32[:, :], k.s32[:, :], k.pa[:, :],
                                 op=ALU.add)
        k.e_s32 = k.inc(v, k.vC)
        k.rel["pa"] = (k.vC, k.e_s32)
    blk.vector(ve2)

    def sc3(scalar):
        k.wait(scalar, "scalar", k.vC, k.e_s32)
        aa = scalar.copy(rks4(256, 128), r4(k.s32))
        k.e_sbf = k.inc(aa, k.aC)
        k.wait(scalar, "scalar", k.pC, k.m_o)
        k.wait(scalar, "scalar", k.aC, k.aC.n)
        for p in range(NP):
            aa = scalar.activation(k.osq[:, p * 128:(p + 1) * 128],
                                   k.pb[:, p * 128:(p + 1) * 128], AF.Square,
                                   accum_out=k.oss[:, p:p + 1])
        k.inc(aa, k.aC)
        k.wait(scalar, "scalar", k.aC, k.aC.n)
        aa = scalar.activation(k.orstd[:, 0:NP], k.oss[:, 0:NP], AF.Ln,
                               bias=k.c_eps[:, 0:1], scale=k.c_i128[:, 0:1])
        k.inc(aa, k.aC)
        k.wait(scalar, "scalar", k.aC, k.aC.n)
        aa = scalar.activation(k.orstd[:, 0:NP], k.orstd[:, 0:NP], AF.Exp,
                               scale=k.c_nhalf[:, 0:1])
        k.e_orstd = k.inc(aa, k.aC)
        k.rel["rksq_s"] = (k.aC, k.e_sbf)
    blk.scalar(sc3)

    def ve5(vector):
        k.wait(vector, "vector", k.aC, k.e_orstd)
        ov = k.orstd[:, 0:NP].unsqueeze(2).broadcast_to([128, 4, 128])
        v = vector.tensor_tensor(r4(k.onb), r4(k.pb), ov, op=ALU.mult)
        k.e_onb = k.inc(v, k.vC)
        k.rel["pb"] = (k.vC, k.e_onb)
    blk.vector(ve5)

    def pe6(tensor):
        k.wait(tensor, "tensor", k.vC, k.e_onb)
        k.waitrel(tensor, "tensor", "ptr")
        for p in range(NP):
            mm = tensor.transpose(k.ptr[:, p * 128:(p + 1) * 128],
                                  k.onb[:, p * 128:(p + 1) * 128],
                                  k.c_identb[:, :])
        k.m_otr = k.inc(mm, k.pC)
    blk.tensor(pe6)

    def ve6(vector):
        k.wait(vector, "vector", k.pC, k.m_otr)
        ob = k.obuf[(ci // 4) % 2]
        k.waitrel(vector, "vector", ("obuf", (ci // 4) % 2))
        dst = ob[:, :].rearrange("p (f c) -> p f c", f=4)[
            :, :, cc * 128:(cc + 1) * 128]
        v = vector.tensor_scalar(dst, r4(k.ptr), k.c_gam[:, 0:1], None,
                                 op0=ALU.mult)
        k.e_obuf = k.inc(v, k.vC)
        k.obuf_done = k.vC.n
        k.rel["ptr"] = (k.vC, k.e_obuf)
    blk.vector(ve6)


def emit_wo_group(k, gi):
    blk = k.block
    half = gi // 4
    slot = gi % 4
    ob = k.obuf[gi % 2]
    obuf_mark = k.obuf_done

    for j in range(KD):
        bname = "pa" if j % 2 == 0 else "pb"

        def pe(tensor, j=j, bname=bname):
            if j == 0:
                k.wait(tensor, "tensor", k.vC, obuf_mark)
            k.waitrel(tensor, "tensor", bname)
            bank = getattr(k, bname)
            for p in range(NP):
                mm = tensor.matmul(bank[:, :],
                                   k.wopb[:, p * 2048 + j * 128:
                                          p * 2048 + (j + 1) * 128],
                                   ob[:, p * 512:(p + 1) * 512],
                                   start=(p == 0), stop=(p == NP - 1))
            k.m_wog = k.inc(mm, k.pC)
        blk.tensor(pe)

        def ve(vector, j=j, bname=bname):
            k.wait(vector, "vector", k.pC, k.m_wog)
            k.waitrel(vector, "vector", ("rsstg", j % 2))
            v = vector.tensor_copy(k.rsstg[j % 2][:, :],
                                   getattr(k, bname)[:, :])
            k.m_rss = k.inc(v, k.vC)
            k.rel[bname] = (k.vC, k.m_rss)
        blk.vector(ve)

        def sy(sync, j=j):
            k.wait(sync, "sync", k.vC, k.m_rss)
            dst = k.rs_in[half][slot * Dm + j * 128:
                                slot * Dm + (j + 1) * 128, :]
            d = k.inc(sync.dma_start(out=dst, in_=k.rsstg[j % 2][:, :]),
                      k.rsp[j % 2], 16)
            k.rel[("rsstg", j % 2)] = (k.rsp[j % 2], d)
        blk.sync(sy)

    k.rel[("obuf", gi % 2)] = (k.pC, k.pC.n)
    if slot == 3:
        k.rs_stage_mark = getattr(k, "rs_stage_mark", {})
        k.rs_stage_mark[half] = (k.rs0.n, k.rs1.n)


def emit_bg(k, ci):
    n = 4 if ci < 29 else 999
    done = 0
    while k.bg_i < len(k.bg) and done < n:
        kind, i = k.bg[k.bg_i]
        if kind == "w2b" and k.bg_i_w2a_pending(ci):
            break
        if kind in ("w1", "w3"):
            emit_bg_cast(k, kind, i)
        elif kind == "w2a":
            emit_w2a(k, i)
        else:
            emit_w2b(k, i)
        k.bg_i += 1
        done += 1


def _w2a_pending(k, ci):
    return False


KB.bg_i_w2a_pending = lambda self, ci: (
    not hasattr(self, "w2a_final"))


def emit_bg_cast(k, kind, i):
    blk = k.block
    src = k.w1S if kind == "w1" else k.w3S
    dst = k.w1b_in if kind == "w1" else k.w3b_in

    def sfn(scalar):
        k.waitrel(scalar, "scalar", ("stgw", i % 2))
        k.bg_dma = k.inc(scalar.dma_start(
            out=k.stgw[i % 2][:, :],
            in_=src[i * 128:(i + 1) * 128, :]), k.sq[i % 2], 16)
    blk.scalar(sfn)

    def afn(vector):
        k.wait(vector, "vector", k.sq[i % 2], k.bg_dma)
        k.waitrel(vector, "vector", ("bgev", i % 2))
        aa = vector.tensor_copy(k.bgev[i % 2][:, :], k.stgw[i % 2][:, :])
        k.e_bgc = k.inc(aa, k.vC)
        k.rel[("stgw", i % 2)] = (k.vC, k.e_bgc)
    blk.vector(afn)

    def sfn2(scalar):
        k.wait(scalar, "scalar", k.vC, k.e_bgc)
        d = k.inc(scalar.dma_start(out=dst[i * 128:(i + 1) * 128, :],
                                   in_=k.bgev[i % 2][:, :]), k.bgp[i % 2], 16)
        k.rel[("bgev", i % 2)] = (k.bgp[i % 2], d)
    blk.scalar(sfn2)
    if kind == "w1" and i == 15:
        k.w1_out_mark = (k.bg0.n, k.bg1.n)
    if kind == "w3" and i == 15:
        k.w3_out_mark = (k.bg0.n, k.bg1.n)


def emit_w2a(k, g):
    blk = k.block

    def sfn(scalar):
        k.waitrel(scalar, "scalar", ("stgw", g % 2))
        k.w2a_dma = k.inc(scalar.dma_start(
            out=k.stgw[g % 2][:, 0:256],
            in_=k.w2S[g * 128:(g + 1) * 128, :]), k.sq[g % 2], 16)
    blk.scalar(sfn)

    def afn(vector):
        k.wait(vector, "vector", k.sq[g % 2], k.w2a_dma)
        k.waitrel(vector, "vector", ("w2cs", g % 2))
        aa = vector.tensor_copy(k.w2cs[g % 2][:, 0:256], k.stgw[g % 2][:, 0:256])
        k.e_w2c = k.inc(aa, k.vC)
        k.rel[("stgw", g % 2)] = (k.vC, k.e_w2c)
    blk.vector(afn)

    def pfn(tensor):
        k.wait(tensor, "tensor", k.vC, k.e_w2c)
        k.waitrel(tensor, "tensor", "pst")
        mm = tensor.matmul(k.pst[:, 0:256], k.c_h128b[:, :],
                           k.w2cs[g % 2][:, 0:256], start=True, stop=True)
        k.m_w2 = k.inc(mm, k.pC)
        k.rel[("w2cs", g % 2)] = (k.pC, k.m_w2)
    blk.tensor(pfn)

    def vfn(vector):
        k.wait(vector, "vector", k.pC, k.m_w2)
        k.waitrel(vector, "vector", ("w2ev", g % 2))
        v = vector.tensor_scalar(k.w2ev[g % 2][:, 0:256], k.pst[:, 0:256],
                                 RSH, None, op0=ALU.mult)
        k.e_w2e = k.inc(v, k.vC)
        k.rel["pst"] = (k.vC, k.e_w2e)
    blk.vector(vfn)

    def sfn2(scalar):
        k.wait(scalar, "scalar", k.vC, k.e_w2e)
        d = k.inc(scalar.dma_start(out=k.w2bA[g * 128:(g + 1) * 128, :],
                                   in_=k.w2ev[g % 2][:, 0:256]),
                  k.wep[g % 2], 16)
        k.rel[("w2ev", g % 2)] = (k.wep[g % 2], d)
    blk.scalar(sfn2)
    if g == 63:
        k.w2a_final = (k.we0.n, k.we1.n)


def emit_w2b(k, it):
    blk = k.block
    e0 = 2 * it

    def sfn(scalar):
        k.wait(scalar, "scalar", k.we0, k.w2a_final[0])
        k.wait(scalar, "scalar", k.we1, k.w2a_final[1])
        k.waitrel(scalar, "scalar", ("w2cs", it % 2))
        src = k.w2bA.rearrange("(g e) d -> e g d", g=64)[e0:e0 + 2]
        k.w2b_dma = k.inc(
            scalar.dma_start(out=k.w2cs[it % 2][:, 0:256], in_=src),
            k.wcp[it % 2], 16)
    blk.scalar(sfn)

    def pfn(tensor):
        k.wait(tensor, "tensor", k.wcp[it % 2], k.w2b_dma)
        k.waitrel(tensor, "tensor", "pst")
        mm = tensor.matmul(k.pst[:, 0:256], k.c_i2h64b[:, :],
                           k.w2cs[it % 2][:, 0:256], start=True, stop=True)
        k.m_w2 = k.inc(mm, k.pC)
        k.rel[("w2cs", it % 2)] = (k.pC, k.m_w2)
    blk.tensor(pfn)

    def vfn(vector):
        k.wait(vector, "vector", k.pC, k.m_w2)
        k.waitrel(vector, "vector", ("w2ev", it % 2))
        v = vector.tensor_copy(k.w2ev[it % 2][:, 0:256], k.pst[:, 0:256])
        k.e_w2e = k.inc(v, k.vC)
        k.rel["pst"] = (k.vC, k.e_w2e)
    blk.vector(vfn)

    def sfn2(scalar):
        k.wait(scalar, "scalar", k.vC, k.e_w2e)
        dst = k.w2p_in.rearrange("(g e) d -> e g d", g=64)[e0:e0 + 2]
        d = k.inc(scalar.dma_start(out=dst, in_=k.w2ev[it % 2][:, 0:256]),
                  k.wep[it % 2], 16)
        k.rel[("w2ev", it % 2)] = (k.wep[it % 2], d)
    blk.scalar(sfn2)
    if it == 63:
        k.w2p_final = (k.we0.n, k.we1.n)


def emit_collectives(k):
    blk = k.block
    grp8 = [list(range(8))]
    grp4 = [[0, 1, 2, 3], [4, 5, 6, 7]]

    def gp(gpsimd):
        gpsimd.wait_ge(k.bg0.sem, k.w1_out_mark[0])
        gpsimd.wait_ge(k.bg1.sem, k.w1_out_mark[1])
        cc = gpsimd.collective_compute(
            "AllGather", ALU.bypass, replica_groups=grp8,
            ins=[k.w1b_in[:, :]], outs=[k.w1b_out[:, :]])
        k.inc(cc, k.ccW1)
        gpsimd.wait_ge(k.bg0.sem, k.w3_out_mark[0])
        gpsimd.wait_ge(k.bg1.sem, k.w3_out_mark[1])
        cc = gpsimd.collective_compute(
            "AllGather", ALU.bypass, replica_groups=grp8,
            ins=[k.w3b_in[:, :]], outs=[k.w3b_out[:, :]])
        k.inc(cc, k.ccW3)
        gpsimd.wait_ge(k.rs0.sem, k.rs_stage_mark[0][0])
        gpsimd.wait_ge(k.rs1.sem, k.rs_stage_mark[0][1])
        cc = gpsimd.collective_compute(
            "ReduceScatter", ALU.add, replica_groups=grp4,
            ins=[k.rs_in[0][:, :]], outs=[k.rs_out[0][:, :]])
        k.inc(cc, k.cc0)
        gpsimd.wait_ge(k.we0.sem, k.w2p_final[0])
        gpsimd.wait_ge(k.we1.sem, k.w2p_final[1])
        cc = gpsimd.collective_compute(
            "AllGather", ALU.bypass, replica_groups=grp8,
            ins=[k.w2p_in[:, :]], outs=[k.w2p_out[:, :]])
        k.inc(cc, k.ccW2)
        gpsimd.wait_ge(k.rs0.sem, k.rs_stage_mark[1][0])
        gpsimd.wait_ge(k.rs1.sem, k.rs_stage_mark[1][1])
        cc = gpsimd.collective_compute(
            "ReduceScatter", ALU.add, replica_groups=grp4,
            ins=[k.rs_in[1][:, :]], outs=[k.rs_out[1][:, :]])
        k.inc(cc, k.cc1)
    blk.gpsimd(gp)


def emit_barrier(k):
    blk = k.block
    marks = [(k.vC, k.vC.n), (k.aC, k.aC.n), (k.pC, k.pC.n),
             (k.rs0, k.rs0.n), (k.rs1, k.rs1.n),
             (k.sx0, k.sx0.n), (k.sx1, k.sx1.n),
             (k.sq0, k.sq0.n), (k.sq1, k.sq1.n),
             (k.bg0, k.bg0.n), (k.bg1, k.bg1.n),
             (k.wc0, k.wc0.n), (k.wc1, k.wc1.n),
             (k.we0, k.we0.n), (k.we1, k.we1.n)]

    for engname in ("sync", "vector", "scalar", "tensor"):
        def fn(eng, engname=engname):
            for ctr, val in marks:
                k.wait(eng, engname, ctr, val)
        getattr(blk, engname)(fn)


def emit_ln1(k, tile):
    blk = k.block
    ccw = k.cc0 if tile == 0 else k.cc1
    x1 = k.x1[tile]
    x1b = k.x1b[tile]

    k.d_marks = {}

    VB_LN = k.vC.n

    def sy(sync):
        k.wait(sync, "sync", ccw, 1)
        for j in range(KD):
            if j >= 2:
                k.wait(sync, "sync", k.vC, VB_LN + (j - 2) + 1)
            else:
                k.waitrel(sync, "sync", ("xrs", j % 2))
            k.inc(sync.dma_start(
                out=x1[:, j * 512:(j + 1) * 512],
                in_=k.rs_out[tile][j * 128:(j + 1) * 128, :]),
                k.xrp[j % 2], 16)
            m = k.inc(sync.dma_start(
                out=k.xrs[j % 2][:, :],
                in_=k.xft[j * 128:(j + 1) * 128,
                          tile * 512:(tile + 1) * 512]), k.xrp[j % 2], 16)
            k.d_marks[j] = m
    blk.sync(sy)

    def ve(vector):
        for j in range(KD):
            k.wait(vector, "vector", k.xrp[j % 2], k.d_marks[j])
            v = vector.tensor_tensor(x1[:, j * 512:(j + 1) * 512],
                                     x1[:, j * 512:(j + 1) * 512],
                                     k.xrs[j % 2][:, :], op=ALU.add)
            m = k.inc(v, k.vC)
            assert m == VB_LN + j + 1, "ln1 mark prediction broken"
            k.rel[("xrs", j % 2)] = (k.vC, m)
            k.d_marks[j] = m
    blk.vector(ve)

    def prep(j):
        def af(scalar, j=j):
            k.wait(scalar, "scalar", k.vC, k.d_marks[j])
            k.waitrel(scalar, "scalar", ("preb", j % 2))
            aa = scalar.copy(k.preb[j % 2][:, :], x1[:, j * 512:(j + 1) * 512])
            k.e_preb = k.inc(aa, k.aC)
        blk.scalar(af)

        def vf(vector, j=j):
            k.wait(vector, "vector", k.vC, k.d_marks[j])
            k.waitrel(vector, "vector", ("sqb", j % 2))
            v = vector.tensor_tensor(k.sqb[j % 2][:, :],
                                     x1[:, j * 512:(j + 1) * 512],
                                     x1[:, j * 512:(j + 1) * 512],
                                     op=ALU.mult)
            k.e_sqb = k.inc(v, k.vC)
        blk.vector(vf)

        def pf(tensor, j=j):
            k.wait(tensor, "tensor", k.aC, k.e_preb)
            k.wait(tensor, "tensor", k.vC, k.e_sqb)
            if j == 0:
                k.waitrel(tensor, "tensor", "pst")
            mm = tensor.matmul(k.pst[0:1, 0:512], k.c_onescol_b[:, 0:1],
                               k.preb[j % 2][:, :],
                               start=(j == 0), stop=(j == KD - 1))
            m1 = k.inc(mm, k.pC)
            k.rel[("preb", j % 2)] = (k.pC, m1)
            mm = tensor.matmul(k.pa[0:1, 0:512], k.c_onescol_b[:, 0:1],
                               k.sqb[j % 2][:, :],
                               start=(j == 0), stop=(j == KD - 1))
            m2 = k.inc(mm, k.pC)
            k.rel[("sqb", j % 2)] = (k.pC, m2)
            if j == KD - 1:
                k.m_stats = m2
        blk.tensor(pf)
    for j in range(KD):
        prep(j)

    def ve2(vector):
        k.wait(vector, "vector", k.pC, k.m_stats)
        v = vector.tensor_scalar(k.stat[0:1, 0:512], k.pst[0:1, 0:512],
                                 1.0 / 2048.0, None, op0=ALU.mult)
        k.inc(v, k.vC)
        v = vector.tensor_scalar(k.stat[0:1, 512:1024], k.pa[0:1, 0:512],
                                 1.0 / 2048.0, None, op0=ALU.mult)
        m = k.inc(v, k.vC)
        k.wait(vector, "vector", k.vC, m)
        v = vector.tensor_tensor(k.lnt[0:1, 0:512], k.stat[0:1, 0:512],
                                 k.stat[0:1, 0:512], op=ALU.mult)
        m = k.inc(v, k.vC)
        k.wait(vector, "vector", k.vC, m)
        v = vector.tensor_tensor(k.lnt[0:1, 0:512], k.stat[0:1, 512:1024],
                                 k.lnt[0:1, 0:512], op=ALU.subtract)
        k.e_var = k.inc(v, k.vC)
        k.rel["pst"] = (k.vC, k.e_var)
        k.rel["pa"] = (k.vC, k.e_var)
    blk.vector(ve2)

    def sc2(scalar):
        k.wait(scalar, "scalar", k.vC, k.e_var)
        aa = scalar.activation(k.lnt[0:1, 0:512], k.lnt[0:1, 0:512], AF.Ln,
                               bias=k.c_eps[0:1, 0:1])
        m = k.inc(aa, k.aC)
        k.wait(scalar, "scalar", k.aC, m)
        aa = scalar.activation(k.lnt[0:1, 0:512], k.lnt[0:1, 0:512], AF.Exp,
                               scale=k.c_nhalf[0:1, 0:1])
        k.e_rstd = k.inc(aa, k.aC)
    blk.scalar(sc2)

    def pe2(tensor):
        k.wait(tensor, "tensor", k.aC, k.e_rstd)
        k.waitrel(tensor, "tensor", "pst")
        mm = tensor.matmul(k.pst[:, 0:512], k.c_onesrow_f[0:1, :],
                           k.lnt[0:1, 0:512], start=True, stop=True)
        k.m_bc1 = k.inc(mm, k.pC)
    blk.tensor(pe2)

    def ve3(vector):
        k.wait(vector, "vector", k.pC, k.m_bc1)
        v = vector.tensor_copy(k.abc[:, :], k.pst[:, 0:512])
        m = k.inc(v, k.vC)
        k.rel["pst"] = (k.vC, m)
        k.e_abc = m
    blk.vector(ve3)

    def pe3(tensor):
        k.wait(tensor, "tensor", k.vC, k.e_abc)
        k.waitrel(tensor, "tensor", "pst")
        mm = tensor.matmul(k.pst[:, 0:512], k.c_onesrow_f[0:1, :],
                           k.stat[0:1, 0:512], start=True, stop=True)
        k.m_bc2 = k.inc(mm, k.pC)
    blk.tensor(pe3)

    def ve4(vector):
        k.wait(vector, "vector", k.pC, k.m_bc2)
        v = vector.tensor_copy(k.bbc[:, :], k.pst[:, 0:512])
        m = k.inc(v, k.vC)
        k.rel["pst"] = (k.vC, m)
        k.wait(vector, "vector", k.vC, m)
        k.x1_marks = {}
        for j in range(KD):
            v = vector.tensor_tensor(x1[:, j * 512:(j + 1) * 512],
                                     x1[:, j * 512:(j + 1) * 512],
                                     k.bbc[:, :], op=ALU.subtract)
            m = k.inc(v, k.vC)
            k.wait(vector, "vector", k.vC, m)
            v = vector.tensor_tensor(x1[:, j * 512:(j + 1) * 512],
                                     x1[:, j * 512:(j + 1) * 512],
                                     k.abc[:, :], op=ALU.mult)
            k.x1_marks[j] = k.inc(v, k.vC)
    blk.vector(ve4)

    def sc3(scalar):
        for j in range(KD):
            k.wait(scalar, "scalar", k.vC, k.x1_marks[j])
            aa = scalar.copy(x1b[:, j * 512:(j + 1) * 512],
                             x1[:, j * 512:(j + 1) * 512])
            k.inc(aa, k.aC)
    blk.scalar(sc3)
    k.x1b_done = k.aC.n
    k.rel[("x1done", tile)] = (k.aC, k.aC.n)


def emit_ffn_quarter(k, quarter):
    blk = k.block
    j0 = quarter * 16

    for jj in range(16):
        j = j0 + jj
        r, jl = j // 8, j % 8
        sbi = j % 3
        buf = k.wsl[sbi]
        buf3 = k.wsl[3 + sbi]

        def syw(sync, j=j, r=r, jl=jl, buf=buf, buf3=buf3, sbi=sbi):
            k.waitrel(sync, "sync", ("wsl", sbi))
            pass
            k.inc(sync.dma_start(
                out=buf[:, :].rearrange("p (kk c) -> p kk c", c=128),
                in_=k.w1b_out[r * Dm:(r + 1) * Dm,
                              jl * 128:(jl + 1) * 128].rearrange(
                    "(kk p) c -> p kk c", p=128)), k.wsp[sbi], 16)
            m = k.inc(sync.dma_start(
                out=buf3[:, :].rearrange("p (kk c) -> p kk c", c=128),
                in_=k.w3b_out[r * Dm:(r + 1) * Dm,
                              jl * 128:(jl + 1) * 128].rearrange(
                    "(kk p) c -> p kk c", p=128)), k.wsp[sbi], 16)
            k.wslab_mark = m
        blk.sync(syw)

        def pe(tensor, j=j, jj=jj, buf=buf, buf3=buf3, sbi=sbi,
               wmark=None):
            if j == j0 and quarter == 0:
                k.wait(tensor, "tensor", k.ccW1, 1)
                k.wait(tensor, "tensor", k.ccW3, 1)
                k.wait(tensor, "tensor", k.aC, k.x1b_done)
            k.wait(tensor, "tensor", k.wsp[sbi], k.wslab_mark)
            k.up_marks = {}
            for tile in range(2):
                gbank = k.pquad[:, (2 * tile) * 512:(2 * tile) * 512 + 512]
                lbank = k.pquad[:, (2 * tile + 1) * 512:
                                (2 * tile + 1) * 512 + 512]
                k.waitrel(tensor, "tensor", ("bank", 2 * tile))
                for kk in range(KD):
                    mm = tensor.matmul(
                        gbank, buf[:, kk * 128:(kk + 1) * 128],
                        k.x1b[tile][:, kk * 512:(kk + 1) * 512],
                        start=(kk == 0), stop=(kk == KD - 1))
                k.up_marks[(tile, "g")] = k.inc(mm, k.pC)
                k.waitrel(tensor, "tensor", ("bank", 2 * tile + 1))
                for kk in range(KD):
                    mm = tensor.matmul(
                        lbank, buf3[:, kk * 128:(kk + 1) * 128],
                        k.x1b[tile][:, kk * 512:(kk + 1) * 512],
                        start=(kk == 0), stop=(kk == KD - 1))
                k.up_marks[(tile, "l")] = k.inc(mm, k.pC)
            k.rel[("wsl", sbi)] = (k.pC, k.pC.n)
        blk.tensor(pe)

        def sc(scalar, jj=jj):
            k.gsl_marks = {}
            for tile in range(2):
                k.waitrel(scalar, "scalar", ("gsl", tile))
                k.wait(scalar, "scalar", k.pC, k.up_marks[(tile, "g")])
                aa = scalar.activation(
                    k.gsl[:, tile * 512:(tile + 1) * 512],
                    k.pquad[:, (2 * tile) * 512:(2 * tile) * 512 + 512],
                    AF.Silu)
                m = k.inc(aa, k.aC)
                k.rel[("bank", 2 * tile)] = (k.aC, m)
                k.gsl_marks[tile] = m
        blk.scalar(sc)

        def ve(vector, jj=jj):
            for tile in range(2):
                k.wait(vector, "vector", k.aC, k.gsl_marks[tile])
                k.wait(vector, "vector", k.pC, k.up_marks[(tile, "l")])
                v = vector.tensor_tensor(
                    k.hid[:, jj * 1024 + tile * 512:
                          jj * 1024 + (tile + 1) * 512],
                    k.gsl[:, tile * 512:(tile + 1) * 512],
                    k.pquad[:, (2 * tile + 1) * 512:
                            (2 * tile + 1) * 512 + 512],
                    op=ALU.mult)
                m = k.inc(v, k.vC)
                k.rel[("bank", 2 * tile + 1)] = (k.vC, m)
                k.rel[("gsl", tile)] = (k.vC, m)
        blk.vector(ve)
    k.hid_done = k.vC.n

    k.dac_wr = getattr(k, "dac_wr", {})
    k.dac_rd = getattr(k, "dac_rd", {})
    k.x2_marks = getattr(k, "x2_marks", {})
    k.dawm = getattr(k, "dawm", {})
    for st in range(8):
        k.w2s_marks = {}
        k.dn_jj_marks = {}
        k.dn_marks = {}

        if quarter > 0:
            def syr(sync, st=st):
                for g in range(4):
                    tile, dd = g // 2, g % 2
                    dj = 2 * st + dd
                    wr = k.dac_wr[(quarter - 1, tile, dj)]
                    k.wait(sync, "sync", wr[0], wr[1])
                    k.waitrel(sync, "sync", ("dar", g))
                    k.inc(sync.dma_start(
                        out=k.dar[g][:, :],
                        in_=k.dacc[dj * 128:(dj + 1) * 128,
                                   tile * 512:(tile + 1) * 512]), k.daR, 16)
                k.dac_rd_set = k.daR.n
            blk.sync(syr)

        for jj in range(16):
            j2 = j0 + jj

            def syw2(scalar, st=st, jj=jj, j2=j2):
                if st == 0 and quarter == 0 and jj == 0:
                    k.wait(scalar, "scalar", k.ccW2, 1)
                if jj >= 3:
                    k.wait(scalar, "scalar", k.pC, k.dn_jj_marks[jj - 3])
                else:
                    k.waitrel(scalar, "scalar", ("w2sl", jj % 3))
                m = k.inc(scalar.dma_start(
                    out=k.w2sl[jj % 3][:, 0:256],
                    in_=k.w2p_out[st * HID + j2 * 128:
                                  st * HID + (j2 + 1) * 128, :]),
                    k.w2p_[jj % 3], 16)
                k.w2s_marks[jj] = m
            blk.scalar(syw2)

            def ped(tensor, st=st, jj=jj):
                if st == 0 and jj == 0:
                    k.wait(tensor, "tensor", k.vC, k.hid_done)
                k.wait(tensor, "tensor", k.w2p_[jj % 3], k.w2s_marks[jj])
                for g in range(4):
                    tile, dd = g // 2, g % 2
                    if jj == 0:
                        k.waitrel(tensor, "tensor", ("bank", g))
                    bank = k.pquad[:, g * 512: g * 512 + 512]
                    mm = tensor.matmul(
                        bank, k.w2sl[jj % 3][:, dd * 128:(dd + 1) * 128],
                        k.hid[:, jj * 1024 + tile * 512:
                              jj * 1024 + (tile + 1) * 512],
                        start=(jj == 0), stop=(jj == 15))
                    if jj == 15:
                        k.dn_marks[g] = k.inc(mm, k.pC)
                if jj < 15:
                    k.dn_jj_marks[jj] = k.inc(mm, k.pC)
                else:
                    k.dn_jj_marks[jj] = k.dn_marks[3]
                    for b4 in range(3):
                        k.rel[("w2sl", b4)] = (k.pC, k.pC.n)
            blk.tensor(ped)

        def ved(vector, st=st):
            for g in range(4):
                tile, dd = g // 2, g % 2
                dj = 2 * st + dd
                k.wait(vector, "vector", k.pC, k.dn_marks[g])
                bank = k.pquad[:, g * 512: g * 512 + 512]
                if quarter == 0:
                    k.waitrel(vector, "vector", ("daw", g))
                    v = vector.tensor_copy(k.daw[g][:, :], bank)
                    m = k.inc(v, k.vC)
                    k.dawm[(tile, dj)] = m
                elif quarter < 3:
                    k.wait(vector, "vector", k.daR, k.dac_rd_set)
                    k.waitrel(vector, "vector", ("daw", g))
                    v = vector.tensor_tensor(k.daw[g][:, :], bank,
                                             k.dar[g][:, :], op=ALU.add)
                    m = k.inc(v, k.vC)
                    k.dawm[(tile, dj)] = m
                    k.rel[("dar", g)] = (k.vC, m)
                else:
                    k.wait(vector, "vector", k.daR, k.dac_rd_set)
                    x1 = k.x1[tile]
                    v = vector.tensor_tensor(k.dar[g][:, :], bank,
                                             k.dar[g][:, :], op=ALU.add)
                    m = k.inc(v, k.vC)
                    k.wait(vector, "vector", k.vC, m)
                    v = vector.tensor_tensor(
                        x1[:, dj * 512:(dj + 1) * 512],
                        x1[:, dj * 512:(dj + 1) * 512],
                        k.dar[g][:, :], op=ALU.add)
                    m = k.inc(v, k.vC)
                    k.x2_marks[(tile, dj)] = m
                    k.rel[("dar", g)] = (k.vC, m)
                k.rel[("bank", g)] = (k.vC, m)
        blk.vector(ved)

        if quarter < 3:
            def syd(sync, st=st):
                for g in range(4):
                    tile, dd = g // 2, g % 2
                    dj = 2 * st + dd
                    k.wait(sync, "sync", k.vC, k.dawm[(tile, dj)])
                    d = k.inc(sync.dma_start(
                        out=k.dacc[dj * 128:(dj + 1) * 128,
                                   tile * 512:(tile + 1) * 512],
                        in_=k.daw[g][:, :]), k.dwp[g], 16)
                    k.rel[("daw", g)] = (k.dwp[g], d)
                    k.dac_wr[(quarter, tile, dj)] = (k.dwp[g], d)
            blk.sync(syd)



def emit_ln2_out(k, tile):
    blk = k.block
    x1 = k.x1[tile]

    def prep(j):
        def af(scalar, j=j):
            k.wait(scalar, "scalar", k.vC, k.x2_marks[(tile, j)])
            k.waitrel(scalar, "scalar", ("preb", j % 2))
            aa = scalar.copy(k.preb[j % 2][:, :], x1[:, j * 512:(j + 1) * 512])
            k.e_preb = k.inc(aa, k.aC)
        blk.scalar(af)

        def vf(vector, j=j):
            k.wait(vector, "vector", k.vC, k.x2_marks[(tile, j)])
            k.waitrel(vector, "vector", ("sqb", j % 2))
            v = vector.tensor_tensor(k.sqb[j % 2][:, :],
                                     x1[:, j * 512:(j + 1) * 512],
                                     x1[:, j * 512:(j + 1) * 512],
                                     op=ALU.mult)
            k.e_sqb = k.inc(v, k.vC)
        blk.vector(vf)

        def pf(tensor, j=j):
            k.wait(tensor, "tensor", k.aC, k.e_preb)
            k.wait(tensor, "tensor", k.vC, k.e_sqb)
            if j == 0:
                k.waitrel(tensor, "tensor", "pst")
                k.waitrel(tensor, "tensor", ("bank", 0))
            mm = tensor.matmul(k.pst[0:1, 0:512], k.c_onescol_b[:, 0:1],
                               k.preb[j % 2][:, :],
                               start=(j == 0), stop=(j == KD - 1))
            m1 = k.inc(mm, k.pC)
            k.rel[("preb", j % 2)] = (k.pC, m1)
            mm = tensor.matmul(k.pquad[0:1, 0:512], k.c_onescol_b[:, 0:1],
                               k.sqb[j % 2][:, :],
                               start=(j == 0), stop=(j == KD - 1))
            m2 = k.inc(mm, k.pC)
            k.rel[("sqb", j % 2)] = (k.pC, m2)
            if j == KD - 1:
                k.m_stats = m2
        blk.tensor(pf)
    for j in range(KD):
        prep(j)

    def ve2(vector):
        k.wait(vector, "vector", k.pC, k.m_stats)
        v = vector.tensor_scalar(k.stat[0:1, 0:512], k.pst[0:1, 0:512],
                                 1.0 / 2048.0, None, op0=ALU.mult)
        k.inc(v, k.vC)
        v = vector.tensor_scalar(k.stat[0:1, 512:1024], k.pquad[0:1, 0:512],
                                 1.0 / 2048.0, None, op0=ALU.mult)
        m = k.inc(v, k.vC)
        k.wait(vector, "vector", k.vC, m)
        v = vector.tensor_tensor(k.lnt[0:1, 0:512], k.stat[0:1, 0:512],
                                 k.stat[0:1, 0:512], op=ALU.mult)
        m = k.inc(v, k.vC)
        k.wait(vector, "vector", k.vC, m)
        v = vector.tensor_tensor(k.lnt[0:1, 0:512], k.stat[0:1, 512:1024],
                                 k.lnt[0:1, 0:512], op=ALU.subtract)
        k.e_var = k.inc(v, k.vC)
        k.rel["pst"] = (k.vC, k.e_var)
        k.rel[("bank", 0)] = (k.vC, k.e_var)
    blk.vector(ve2)

    def sc2(scalar):
        k.wait(scalar, "scalar", k.vC, k.e_var)
        aa = scalar.activation(k.lnt[0:1, 0:512], k.lnt[0:1, 0:512], AF.Ln,
                               bias=k.c_eps[0:1, 0:1])
        m = k.inc(aa, k.aC)
        k.wait(scalar, "scalar", k.aC, m)
        aa = scalar.activation(k.lnt[0:1, 0:512], k.lnt[0:1, 0:512], AF.Exp,
                               scale=k.c_nhalf[0:1, 0:1])
        k.e_rstd = k.inc(aa, k.aC)
    blk.scalar(sc2)

    def pe2(tensor):
        k.wait(tensor, "tensor", k.aC, k.e_rstd)
        k.waitrel(tensor, "tensor", "pst")
        mm = tensor.matmul(k.pst[:, 0:512], k.c_onesrow_f[0:1, :],
                           k.lnt[0:1, 0:512], start=True, stop=True)
        k.m_bc1 = k.inc(mm, k.pC)
    blk.tensor(pe2)

    def ve3(vector):
        k.wait(vector, "vector", k.pC, k.m_bc1)
        v = vector.tensor_copy(k.abc[:, :], k.pst[:, 0:512])
        m = k.inc(v, k.vC)
        k.rel["pst"] = (k.vC, m)
        k.e_abc = m
    blk.vector(ve3)

    def pe3(tensor):
        k.wait(tensor, "tensor", k.vC, k.e_abc)
        k.waitrel(tensor, "tensor", "pst")
        mm = tensor.matmul(k.pst[:, 0:512], k.c_onesrow_f[0:1, :],
                           k.stat[0:1, 0:512], start=True, stop=True)
        k.m_bc2 = k.inc(mm, k.pC)
    blk.tensor(pe3)

    def ve4(vector):
        k.wait(vector, "vector", k.pC, k.m_bc2)
        v = vector.tensor_copy(k.bbc[:, :], k.pst[:, 0:512])
        m = k.inc(v, k.vC)
        k.rel["pst"] = (k.vC, m)
        k.wait(vector, "vector", k.vC, m)
        k.o_marks = {}
        for j in range(KD):
            v = vector.tensor_tensor(x1[:, j * 512:(j + 1) * 512],
                                     x1[:, j * 512:(j + 1) * 512],
                                     k.bbc[:, :], op=ALU.subtract)
            m = k.inc(v, k.vC)
            k.wait(vector, "vector", k.vC, m)
            v = vector.tensor_tensor(x1[:, j * 512:(j + 1) * 512],
                                     x1[:, j * 512:(j + 1) * 512],
                                     k.abc[:, :], op=ALU.mult)
            k.o_marks[j] = k.inc(v, k.vC)
    blk.vector(ve4)

    def syo(sync):
        for j in range(KD):
            k.wait(sync, "sync", k.vC, k.o_marks[j])
            k.inc(sync.dma_start(
                out=k.out.rearrange("(kk p) t -> p kk t", p=128)[
                    :, j, tile * 512:(tile + 1) * 512],
                in_=x1[:, j * 512:(j + 1) * 512]), k.dOut, 16)
    blk.sync(syo)


def kernel(L0_attn_hidden_states, L0_attn_Wq, L0_attn_Wk, L0_attn_Wv,
           L0_attn_Wb, L0_attn_o_norm_weight, L0_attn_Wo,
           L0_ffn_W1, L0_ffn_W2, L0_ffn_W3):
    x = np.ascontiguousarray(L0_attn_hidden_states, dtype=np.float32)
    consts = host_consts()
    nc = build_nc()

    w1t = np.ascontiguousarray(L0_ffn_W1.T)
    w3t = np.ascontiguousarray(L0_ffn_W3.T)
    w2t = np.ascontiguousarray(L0_ffn_W2.T)

    in_maps = []
    for c in range(8):
        b, hg = c // 4, c % 4
        xb_ = x[b]
        xt = np.ascontiguousarray(xb_.T)
        slots = np.concatenate([xb_[512 * hg:512 * (hg + 1)],
                                xb_[2048 + 512 * hg:2048 + 512 * (hg + 1)]])
        xft = np.ascontiguousarray(slots.T)
        m = {
            "xt": xt, "xft": xft,
            "wq": np.ascontiguousarray(L0_attn_Wq[:, 512 * hg:512 * (hg + 1)]),
            "wk": np.ascontiguousarray(L0_attn_Wk[:, 512 * hg:512 * (hg + 1)]),
            "wv": np.ascontiguousarray(L0_attn_Wv[:, 512 * hg:512 * (hg + 1)]),
            "wb4": np.ascontiguousarray(L0_attn_Wb[:, 4 * hg:4 * (hg + 1)]),
            "gam": np.ascontiguousarray(
                L0_attn_o_norm_weight.reshape(128, 1)),
            "woP": np.ascontiguousarray(L0_attn_Wo[512 * hg:512 * (hg + 1)]),
            "w1S": np.ascontiguousarray(w1t[:, 1024 * c:1024 * (c + 1)]),
            "w3S": np.ascontiguousarray(w3t[:, 1024 * c:1024 * (c + 1)]),
            "w2S": np.ascontiguousarray(w2t[:, 256 * c:256 * (c + 1)]),
        }
        m.update(consts)
        in_maps.append(m)

    res = run_bass_kernel_spmd(nc, in_maps, core_ids=list(range(8)))

    out = np.empty((B, L, Dm), np.float32)
    for c in range(8):
        b, hg = c // 4, c % 4
        o = np.asarray(res.results[c]["out"])
        o = o.reshape(KD, 128, 2, 512)
        t0 = o[:, :, 0].reshape(Dm, 512).T
        t1 = o[:, :, 1].reshape(Dm, 512).T
        out[b, 512 * hg:512 * (hg + 1)] = t0
        out[b, 2048 + 512 * hg:2048 + 512 * (hg + 1)] = t1
    return out
